# version 01
# baseline (speedup 1.0000x reference)
import numpy as np
import ml_dtypes
import concourse.bass as bass
import concourse.mybir as mybir
from concourse.bass_utils import run_bass_kernel_spmd
from contextlib import ExitStack

BF16 = mybir.dt.bfloat16
F32 = mybir.dt.float32
I8 = mybir.dt.int8
AF = mybir.ActivationFunctionType
ALU = mybir.AluOpType
AX = mybir.AxisListType

NCORES = 8
SEQ = 2048
NSEQ = 2
T = SEQ * NSEQ
D = 2048
DC = D // 128
H = 16
TD = 3 * D
INTER = 8192
JC = INTER // 128
NHALF = 2
THALF = T // NHALF
INV_SQRT_HD = 1.0 / float(np.float32(ml_dtypes.bfloat16(np.sqrt(128.0))))
EPS = 1e-5


def build_core_program(phases=(0, 2, 3, 4, 5), debug=False):
    nc = bass.Bass("TRN2", target_bir_lowering=False)

    x_d = nc.dram_tensor("x", [T, D], BF16, kind="ExternalInput")
    wqkv_d = nc.dram_tensor("wqkv", [D, TD], BF16, kind="ExternalInput")
    bqkv_d = nc.dram_tensor("bqkv", [1, TD], BF16, kind="ExternalInput")
    wout_d = nc.dram_tensor("wout", [D, D], BF16, kind="ExternalInput")
    bout_d = nc.dram_tensor("bout", [1, D], BF16, kind="ExternalInput")
    wg_d = nc.dram_tensor("wg", [D, INTER], I8, kind="ExternalInput")
    wu_d = nc.dram_tensor("wu", [D, INTER], I8, kind="ExternalInput")
    wd_d = nc.dram_tensor("wd", [INTER, D], I8, kind="ExternalInput")
    sg_d = nc.dram_tensor("sg", [JC, 128], F32, kind="ExternalInput")
    su_d = nc.dram_tensor("su", [JC, 128], F32, kind="ExternalInput")
    sd_d = nc.dram_tensor("sd", [DC, 128], F32, kind="ExternalInput")
    eye_d = nc.dram_tensor("eye", [128, 128], BF16, kind="ExternalInput")
    eyef_d = nc.dram_tensor("eyef", [128, 128], F32, kind="ExternalInput")
    mask_d = nc.dram_tensor("mask", [128, 2048], BF16, kind="ExternalInput")
    ones_d = nc.dram_tensor("ones", [1, 512], BF16, kind="ExternalInput")
    onesf_d = nc.dram_tensor("onesf", [128, 128], F32, kind="ExternalInput")

    out_d = nc.dram_tensor("out", [T, D], F32, kind="ExternalOutput")
    scratch_kind = "ExternalOutput" if debug else "Internal"
    qts_d = nc.dram_tensor("qts", [DC, 128, T], BF16, kind=scratch_kind)
    kts_d = nc.dram_tensor("kts", [DC, 128, T], BF16, kind=scratch_kind)
    vs_d = nc.dram_tensor("vs", [T, D], BF16, kind=scratch_kind)
    atts_d = nc.dram_tensor("atts", [DC, 128, T], BF16, kind=scratch_kind)
    x1s_d = nc.dram_tensor("x1s", [DC, 128, T], BF16, kind=scratch_kind)
    r2s_d = nc.dram_tensor("r2s", [T, D], F32, kind=scratch_kind)
    prm_d = nc.dram_tensor("prmdbg", [128, 64], F32, kind=scratch_kind) if debug else None
    s1b_d = nc.dram_tensor("s1bdbg", [8, 128, 512], F32, kind=scratch_kind) if debug else None
    s2b_d = nc.dram_tensor("s2bdbg", [8, 128, 512], F32, kind=scratch_kind) if debug else None

    ENGS = ("sync", "gpsimd", "tensor", "vector", "scalar")
    Q = {k: [] for k in ENGS}
    C = {"LD": 0, "LG": 0, "ST": 0, "TM": 0, "VM": 0, "AM": 0, "GM": 0}
    sems = {}

    def em(qname, fn):
        Q[qname].append(fn)

    def wait(qname, sem, val):
        if val <= 0:
            return
        em(qname, lambda e, s=sem, v=val: e.wait_ge(sems[s], v))

    def dma(qname, dst, src, sem):
        C[sem] += 16
        em(qname, lambda e, d=dst, s=src, sm=sem: e.dma_start(d, s).then_inc(sems[sm], 16))

    def dma_t(qname, dst, src, sem):
        C[sem] += 16
        em(qname, lambda e, d=dst, s=src, sm=sem: e.dma_start_transpose(d, s).then_inc(sems[sm], 16))

    def op(qname, f, sem=None):
        if sem is None:
            em(qname, lambda e, ff=f: ff(e))
        elif qname in ("scalar", "vector", "gpsimd"):
            C[sem] += 1
            em(qname, lambda e, ff=f: ff(e))
            em(qname, lambda e, sm=sem: e.drain().then_inc(sems[sm], 1))
        else:
            C[sem] += 1
            em(qname, lambda e, ff=f, sm=sem: ff(e).then_inc(sems[sm], 1))

    NBF = 77500
    NF32 = 10900

    with ExitStack() as ctx:
        e = ctx.enter_context
        BP = e(nc.sbuf_tensor("bp", [128, NBF], BF16))
        FP = e(nc.sbuf_tensor("fp", [128, NF32], F32))
        psA = e(nc.psum_tensor("psA", [128, 2048], F32))
        psB = e(nc.psum_tensor("psB", [128, 512], F32))
        psC = e(nc.psum_tensor("psC", [128, 512], F32))
        psT0 = e(nc.psum_tensor("psT0", [128, 1024], BF16))
        psT1 = e(nc.psum_tensor("psT1", [128, 1024], BF16))
        psT = [psT0, psT1]
        for s in ("LD", "LG", "ST", "TM", "VM", "AM", "GM"):
            sems[s] = e(nc.semaphore(s))
        block = e(nc.Block())

        pbf = 0
        def abf(n):
            nonlocal pbf
            o = pbf; pbf += n
            assert pbf <= NBF, f"bf16 pool overflow {pbf}"
            return o
        pf = 0
        def af32(n):
            nonlocal pf
            o = pf; pf += n
            assert pf <= NF32, f"f32 pool overflow {pf}"
            return o

        o_eye = abf(128)
        o_mask = abf(2048)
        o_ones = abf(512)
        o_bqkv = abf(TD)
        o_bout = abf(D)
        DYN0 = pbf

        o_eyef = af32(128)
        o_onesf = af32(128)
        o_sgT = af32(JC)
        o_suT = af32(JC)
        o_sdT = af32(DC)
        o_sstage = af32(128)
        o_cols = af32(32)
        o_sq = af32(2048)
        o_nm = af32(8)
        o_prmc = af32(64)
        FDYN0 = pf

        EYE = BP[:, o_eye:o_eye + 128]
        MASK = BP[:, o_mask:o_mask + 2048]
        ONES = BP[:, o_ones:o_ones + 512]
        BQKV = BP[:, o_bqkv:o_bqkv + TD]
        BOUT = BP[:, o_bout:o_bout + D]
        EYEF = FP[:, o_eyef:o_eyef + 128]
        ONESF = FP[:, o_onesf:o_onesf + 128]
        SGT = FP[:, o_sgT:o_sgT + JC]
        SUT = FP[:, o_suT:o_suT + JC]
        SDT = FP[:, o_sdT:o_sdT + DC]
        SQ = FP[:, o_sq:o_sq + 2048]

        if 0 in phases:
            dma("sync", EYE, eye_d[:, :], "LD")
            dma("sync", MASK, mask_d[:, :], "LD")
            dma("sync", ONES[0:1, :], ones_d[:, :], "LD")
            dma("sync", BQKV[0:1, :], bqkv_d[:, :], "LD")
            dma("sync", BOUT[0:1, :], bout_d[:, :], "LD")
            dma("sync", EYEF, eyef_d[:, :], "LD")
            dma("sync", ONESF, onesf_d[:, :], "LD")
            sst = FP[:, o_sstage:o_sstage + 128]
            for (src_d, rows, dstap) in ((sg_d, JC, SGT), (su_d, JC, SUT), (sd_d, DC, SDT)):
                dma("sync", sst[:rows, :], src_d[:, :], "LD")
                wait("tensor", "LD", C["LD"])
                op("tensor", lambda t, r=rows: t.transpose(psB[:, 0:r], sst[:r, :], EYEF[:r, :r]), "TM")
                wait("scalar", "TM", C["TM"])
                op("scalar", lambda s, r=rows, dd=dstap: s.copy(dd[:, 0:r], psB[:, 0:r]), "AM")
                wait("tensor", "AM", C["AM"])
                wait("sync", "AM", C["AM"])

        if 2 in phases:
            pbf2 = DYN0
            def abf2(n):
                nonlocal pbf2
                o = pbf2; pbf2 += n
                assert pbf2 <= NBF, f"ph2 bf16 overflow {pbf2}"
                return o
            o_xt = abf2(DC * THALF)
            o_wqk = abf2(2 * DC * 128)
            o_wv = abf2(DC * 512)
            o_qkst = abf2(2 * THALF)
            o_vst = abf2(2 * 512)
            XT = [BP[:, o_xt + dc * THALF: o_xt + (dc + 1) * THALF] for dc in range(DC)]
            WQK = [BP[:, o_wqk + i * DC * 128: o_wqk + (i + 1) * DC * 128] for i in range(2)]
            WV = BP[:, o_wv:o_wv + DC * 512]
            QKST = [BP[:, o_qkst + i * THALF: o_qkst + (i + 1) * THALF] for i in range(2)]
            VST = [BP[:, o_vst + i * 512: o_vst + (i + 1) * 512] for i in range(2)]

            bank_rel = [C["AM"]] * 4
            qk_store_rel = [0, 0]
            v_store_rel = [0, 0]
            wqk_rel = [0, 0]
            wv_rel = 0
            xt_rel = 0

            for hf in range(NHALF):
                wait("sync", "TM", xt_rel)
                for dc in range(DC):
                    dma_t("sync", XT[dc], x_d[hf * THALF:(hf + 1) * THALF, dc * 128:(dc + 1) * 128], "LD")
                ld_xt = C["LD"]

                for f in range(2 * DC):
                    wb = f % 2
                    wait("sync", "TM", wqk_rel[wb])
                    dma("sync", WQK[wb].rearrange("p (dc j) -> p dc j", dc=DC),
                        wqkv_d[:, f * 128:(f + 1) * 128].rearrange("(dc p) j -> p dc j", p=128), "LD")
                    ld_w = C["LD"]
                    qb = f % 2
                    wait("scalar", "ST", qk_store_rel[qb])
                    wait("tensor", "LD", max(ld_xt, ld_w))
                    for tb in range(THALF // 512):
                        b = (hf * 144 + f * 4 + tb) % 4
                        wait("tensor", "AM", bank_rel[b])
                        for k in range(DC):
                            op("tensor", lambda t, b=b, k=k, wb=wb, tb=tb:
                               t.matmul(psA[:, b * 512:(b + 1) * 512],
                                        WQK[wb][:, k * 128:(k + 1) * 128],
                                        XT[k][:, tb * 512:(tb + 1) * 512],
                                        start=(k == 0), stop=False))
                        op("tensor", lambda t, b=b, f=f:
                           t.matmul(psA[:, b * 512:(b + 1) * 512],
                                    BQKV[0:1, f * 128:(f + 1) * 128],
                                    ONES[0:1, 0:512], start=False, stop=True), "TM")
                        wait("scalar", "TM", C["TM"])
                        sc = INV_SQRT_HD if f < DC else 1.0
                        op("scalar", lambda s, b=b, qb=qb, tb=tb, sc=sc:
                           s.activation(QKST[qb][:, tb * 512:(tb + 1) * 512],
                                        psA[:, b * 512:(b + 1) * 512], AF.Copy, scale=sc), "AM")
                        bank_rel[b] = C["AM"]
                    wqk_rel[wb] = C["TM"]
                    wait("sync", "AM", C["AM"])
                    dst = (qts_d if f < DC else kts_d)[f % DC][:, hf * THALF:(hf + 1) * THALF]
                    dma("sync", dst, QKST[qb][:, :], "ST")
                    qk_store_rel[qb] = C["ST"]

                for vb in range(4):
                    wait("sync", "TM", wv_rel)
                    dma("sync", WV.rearrange("p (dc j) -> p dc j", dc=DC),
                        wqkv_d[:, 2 * D + vb * 512: 2 * D + (vb + 1) * 512]
                        .rearrange("(dc p) j -> p dc j", p=128), "LD")
                    ld_wv = C["LD"]
                    wait("tensor", "LD", ld_wv)
                    for tt in range(THALF // 128):
                        b = (vb * 16 + tt) % 4
                        vsb = tt % 2
                        wait("tensor", "AM", bank_rel[b])
                        wait("scalar", "ST", v_store_rel[vsb])
                        for k in range(DC):
                            op("tensor", lambda t, b=b, k=k, tt=tt:
                               t.matmul(psA[:, b * 512:(b + 1) * 512],
                                        XT[k][:, tt * 128:(tt + 1) * 128],
                                        WV[:, k * 512:(k + 1) * 512],
                                        start=(k == 0), stop=False))
                        op("tensor", lambda t, b=b, vb=vb:
                           t.matmul(psA[:, b * 512:(b + 1) * 512],
                                    ONES[0:1, 0:128],
                                    BQKV[0:1, 2 * D + vb * 512: 2 * D + vb * 512 + 512],
                                    start=False, stop=True), "TM")
                        wait("scalar", "TM", C["TM"])
                        op("scalar", lambda s, b=b, vsb=vsb:
                           s.copy(VST[vsb][:, :], psA[:, b * 512:(b + 1) * 512]), "AM")
                        bank_rel[b] = C["AM"]
                        wait("sync", "AM", C["AM"])
                        r0 = hf * THALF + tt * 128
                        dma("sync", vs_d[r0:r0 + 128, vb * 512:(vb + 1) * 512], VST[vsb][:, :], "ST")
                        v_store_rel[vsb] = C["ST"]
                    wv_rel = C["TM"]
                xt_rel = C["TM"]
            ph2_TM, ph2_AM, ph2_VM, ph2_ST = C["TM"], C["AM"], C["VM"], C["ST"]
        else:
            ph2_TM = ph2_AM = ph2_VM = ph2_ST = 0

        if 3 in phases:
            pbf3 = DYN0
            def abf3(n):
                nonlocal pbf3
                o = pbf3; pbf3 += n
                assert pbf3 <= NBF
                return o
            o_qh = abf3(2 * SEQ)
            o_kh = abf3(2 * SEQ)
            o_vh = abf3(2 * SEQ)
            o_p = abf3(2 * 2048)
            o_pt = abf3(16 * 512)
            o_ast = abf3(2 * 512)
            QH = [BP[:, o_qh + i * SEQ:o_qh + (i + 1) * SEQ] for i in range(2)]
            KH = [BP[:, o_kh + i * SEQ:o_kh + (i + 1) * SEQ] for i in range(2)]
            VH = [BP[:, o_vh + i * SEQ:o_vh + (i + 1) * SEQ] for i in range(2)]
            PB = [BP[:, o_p + i * 2048:o_p + (i + 1) * 2048] for i in range(2)]
            PT = BP[:, o_pt:o_pt + 16 * 512]
            PT3 = PT.rearrange("p (kt j) -> p kt j", kt=16)
            AST = [BP[:, o_ast + i * 512:o_ast + (i + 1) * 512] for i in range(2)]
            NM = [FP[:, o_nm + i:o_nm + i + 1] for i in range(2)]
            SSUM = [FP[:, o_nm + 2 + i:o_nm + 3 + i] for i in range(2)]
            SINV = [FP[:, o_nm + 4 + i:o_nm + 5 + i] for i in range(2)]

            wait("sync", "TM", ph2_TM)
            wait("sync", "AM", ph2_AM)

            qkv_rel = [0, 0]
            exp_done = C["AM"]
            pt_evac = C["AM"]
            av_done = [C["TM"], C["TM"]]
            av_evac = [C["AM"], C["AM"]]
            ast_rel = [0, 0]
            psT_rel = [C["AM"], C["AM"]]
            pb_rel = [C["TM"], C["TM"]]

            for s in range(NSEQ):
                for h in range(H):
                    hb = (s * H + h) % 2
                    wait("sync", "TM", qkv_rel[hb])
                    dma("sync", QH[hb], qts_d[h][:, s * SEQ:(s + 1) * SEQ], "LD")
                    dma("sync", KH[hb], kts_d[h][:, s * SEQ:(s + 1) * SEQ], "LD")
                    dma("sync", VH[hb].rearrange("p (kt j) -> p kt j", kt=16),
                        vs_d[s * SEQ:(s + 1) * SEQ, h * 128:(h + 1) * 128]
                        .rearrange("(kt p) j -> p kt j", p=128), "LD")
                    ld_h = C["LD"]
                    wait("tensor", "LD", ld_h)
                    for qb in range(4):
                        nkt = 4 * (qb + 1)
                        N = 512 * (qb + 1)
                        wait("gpsimd", "TM", max(av_done))
                        wait("gpsimd", "AM", pt_evac)
                        for qtl in range(3):
                            nz = 3 - qtl
                            qt = qb * 4 + qtl
                            op("gpsimd", lambda g, qt=qt, qtl=qtl, nz=nz:
                               g.memset(PT3[:, qt + 1:qt + 1 + nz, qtl * 128:(qtl + 1) * 128], 0.0), "GM")
                        gm_after = C["GM"]
                        wait("scalar", "TM", max(av_done))
                        for qtl in range(4):
                            qt = qb * 4 + qtl
                            pi = qt % 2
                            wait("tensor", "AM", exp_done)
                            for c in range(qb + 1):
                                last = (c == qb)
                                op("tensor", lambda t, c=c, qt=qt, hb=hb, last=last:
                                   t.matmul(psA[:, c * 512:(c + 1) * 512],
                                            QH[hb][:, qt * 128:(qt + 1) * 128],
                                            KH[hb][:, c * 512:(c + 1) * 512],
                                            start=True, stop=not last))
                            op("tensor", lambda t, qb=qb, qtl=qtl:
                               t.matmul(psA[:, qb * 512:(qb + 1) * 512],
                                        EYE, MASK[:, qtl * 512:(qtl + 1) * 512],
                                        start=False, stop=True), "TM")
                            sc_tm = C["TM"]
                            wait("vector", "TM", sc_tm)
                            op("vector", lambda v, pi=pi, N=N:
                               v.reduce_max(NM[pi][:, :], psA[:, :N], axis=AX.X, negate=True), "VM")
                            wait("scalar", "VM", C["VM"])
                            wait("scalar", "TM", pb_rel[pi])
                            op("scalar", lambda sC, pi=pi, N=N:
                               sC.activation(PB[pi][:, :N], psA[:, :N], AF.Exp,
                                             bias=NM[pi][:, :1], scale=1.0), "AM")
                            exp_done = C["AM"]
                            wait("vector", "AM", exp_done)
                            op("vector", lambda v, pi=pi, N=N:
                               v.reduce_sum(SSUM[pi][:, :], PB[pi][:, :N], axis=AX.X), "VM")
                            op("vector", lambda v, pi=pi: v.reciprocal(SINV[pi][:, :], SSUM[pi][:, :]), "VM")
                            op("vector", lambda v, pi=pi, N=N:
                               v.tensor_scalar(PB[pi][:, :N], PB[pi][:, :N], SINV[pi][:, :1],
                                               None, ALU.mult), "VM")
                            pscale_done = C["VM"]
                            wait("tensor", "VM", pscale_done)
                            wait("tensor", "GM", gm_after)
                            ntr = qt + 1
                            for kt in range(ntr):
                                bkk = (kt // 8) % 2
                                if kt % 8 == 0:
                                    wait("tensor", "AM", psT_rel[bkk])
                                lastt = (kt % 8 == 7 or kt == ntr - 1)
                                op("tensor", lambda t, kt=kt, pi=pi, bkk=bkk:
                                   t.transpose(psT[bkk][:, (kt % 8) * 128:(kt % 8 + 1) * 128],
                                               PB[pi][:, kt * 128:(kt + 1) * 128], EYE),
                                   "TM" if lastt else None)
                                if lastt:
                                    ntiles = kt % 8 + 1
                                    k0 = kt - ntiles + 1
                                    wait("scalar", "TM", C["TM"])
                                    op("scalar", lambda sC, k0=k0, ntiles=ntiles, qtl=qtl, bkk=bkk:
                                       sC.copy(PT3[:, k0:k0 + ntiles, qtl * 128:(qtl + 1) * 128],
                                               psT[bkk].rearrange("p (kt j) -> p kt j", kt=8)
                                               [:, 0:ntiles, :]), "AM")
                                    psT_rel[bkk] = C["AM"]
                            pb_rel[pi] = C["TM"]
                        pt_evac = C["AM"]
                        avb = qb % 2
                        ps_av = psB if avb == 0 else psC
                        wait("tensor", "AM", pt_evac)
                        wait("tensor", "AM", av_evac[avb])
                        for kt in range(nkt):
                            op("tensor", lambda t, kt=kt, hb=hb, ps_av=ps_av, nkt=nkt:
                               t.matmul(ps_av[:, :], VH[hb][:, kt * 128:(kt + 1) * 128],
                                        PT[:, kt * 512:(kt + 1) * 512],
                                        start=(kt == 0), stop=(kt == nkt - 1)),
                               "TM" if kt == nkt - 1 else None)
                        av_done[avb] = C["TM"]
                        wait("scalar", "TM", av_done[avb])
                        wait("scalar", "ST", ast_rel[avb])
                        op("scalar", lambda sC, avb=avb, ps_av=ps_av:
                           sC.copy(AST[avb][:, :], ps_av[:, :]), "AM")
                        av_evac[avb] = C["AM"]
                        wait("sync", "AM", av_evac[avb])
                        dma("sync", atts_d[h][:, s * SEQ + qb * 512: s * SEQ + (qb + 1) * 512],
                            AST[avb][:, :], "ST")
                        ast_rel[avb] = C["ST"]
                    qkv_rel[hb] = C["TM"]
            ph3_TM, ph3_AM, ph3_VM, ph3_ST = C["TM"], C["AM"], C["VM"], C["ST"]
        else:
            ph3_TM, ph3_AM, ph3_VM, ph3_ST = ph2_TM, ph2_AM, ph2_VM, ph2_ST

        if 4 in phases:
            pbf4 = DYN0
            def abf4(n):
                nonlocal pbf4
                o = pbf4; pbf4 += n
                assert pbf4 <= NBF
                return o
            o_attb = abf4(2 * DC * 512)
            o_wo = abf4(2 * DC * 512)
            o_xb = abf4(2 * 4 * D)
            o_r1 = abf4(4 * D)
            o_x1t = abf4(D)
            o_x1f = abf4(2 * D)
            ATTB = [BP[:, o_attb + i * DC * 512:o_attb + (i + 1) * DC * 512] for i in range(2)]
            WO = [BP[:, o_wo + i * DC * 512:o_wo + (i + 1) * DC * 512] for i in range(2)]
            XB = [BP[:, o_xb + i * 4 * D:o_xb + (i + 1) * 4 * D] for i in range(2)]
            R1 = [BP[:, o_r1 + tt * D:o_r1 + (tt + 1) * D] for tt in range(4)]
            X1T = BP[:, o_x1t:o_x1t + D]
            X1F = [BP[:, o_x1f + i * D:o_x1f + (i + 1) * D] for i in range(2)]
            COLS = FP[:, o_cols:o_cols + 32]

            wait("sync", "TM", ph3_TM)
            wait("sync", "AM", ph3_AM)
            wait("sync", "VM", ph3_VM)

            attb_rel = [0, 0]
            wo_rel = [0, 0]
            xb_rel = [0, 0]
            bank_rel4 = [{"AM": C["AM"], "VM": C["VM"]} for _ in range(4)]
            x1f_rel = [0, 0]
            psT_rel4 = [C["AM"], C["AM"]]
            x1t_rel = C["TM"]
            r1_rel = C["AM"]
            cols_rel = [C["VM"], C["VM"]]

            for tb in range(8):
                bb = tb % 2
                wait("sync", "TM", attb_rel[bb])
                dma("sync", ATTB[bb].rearrange("p (dc j) -> p dc j", dc=DC),
                    atts_d[:, :, tb * 512:(tb + 1) * 512].rearrange("dc p j -> p dc j"), "LD")
                wait("sync", "VM", xb_rel[bb])
                dma("sync", XB[bb].rearrange("p (tt j) -> p tt j", tt=4),
                    x_d[tb * 512:(tb + 1) * 512, :].rearrange("(tt p) j -> p tt j", p=128), "LD")
                ld_tb = C["LD"]
                for db in range(4):
                    wb = db % 2
                    wait("sync", "TM", wo_rel[wb])
                    dma("sync", WO[wb].rearrange("p (dc j) -> p dc j", dc=DC),
                        wout_d[:, db * 512:(db + 1) * 512].rearrange("(dc p) j -> p dc j", p=128), "LD")
                    ld_w = C["LD"]
                    wait("tensor", "LD", max(ld_tb, ld_w))
                    for tt in range(4):
                        b = (tb * 16 + db * 4 + tt) % 4
                        wait("tensor", "AM", bank_rel4[b]["AM"])
                        wait("tensor", "VM", bank_rel4[b]["VM"])
                        for k in range(DC):
                            op("tensor", lambda t, b=b, k=k, bb=bb, wb=wb, tt=tt:
                               t.matmul(psA[:, b * 512:(b + 1) * 512],
                                        ATTB[bb][:, k * 512 + tt * 128: k * 512 + (tt + 1) * 128],
                                        WO[wb][:, k * 512:(k + 1) * 512],
                                        start=(k == 0), stop=False))
                        op("tensor", lambda t, b=b, db=db:
                           t.matmul(psA[:, b * 512:(b + 1) * 512],
                                    ONES[0:1, 0:128], BOUT[0:1, db * 512:(db + 1) * 512],
                                    start=False, stop=True), "TM")
                        wait("vector", "TM", C["TM"])
                        wait("vector", "AM", r1_rel)
                        op("vector", lambda v, b=b, bb=bb, tt=tt, db=db:
                           v.tensor_add(R1[tt][:, db * 512:(db + 1) * 512],
                                        psA[:, b * 512:(b + 1) * 512],
                                        XB[bb][:, tt * D + db * 512: tt * D + (db + 1) * 512]), "VM")
                        bank_rel4[b] = {"AM": C["AM"], "VM": C["VM"]}
                    wo_rel[wb] = C["TM"]
                attb_rel[bb] = C["TM"]
                xb_rel[bb] = C["VM"]
                vm_r1 = C["VM"]
                for tt in range(4):
                    par = tt % 2
                    cs = par * 16
                    S1 = COLS[:, cs + 0:cs + 1]; S2 = COLS[:, cs + 1:cs + 2]
                    M = COLS[:, cs + 2:cs + 3]; MS = COLS[:, cs + 3:cs + 4]
                    M2 = COLS[:, cs + 4:cs + 5]; VAR = COLS[:, cs + 5:cs + 6]
                    SD = COLS[:, cs + 6:cs + 7]; RSTD = COLS[:, cs + 7:cs + 8]
                    NMR = COLS[:, cs + 8:cs + 9]
                    wait("vector", "VM", cols_rel[par])
                    op("vector", lambda v, tt=tt, S1=S1: v.reduce_sum(S1, R1[tt][:, :], axis=AX.X), "VM")
                    wait("scalar", "VM", C["VM"])
                    op("scalar", lambda sC, tt=tt:
                       sC.activation(SQ[:, :], R1[tt][:, :], AF.Square), "AM")
                    op("vector", lambda v, S1=S1, M=M: v.tensor_scalar(M, S1, 1.0 / D, None, ALU.mult), "VM")
                    wait("vector", "AM", C["AM"])
                    op("vector", lambda v, S2=S2: v.reduce_sum(S2, SQ[:, :], axis=AX.X), "VM")
                    op("vector", lambda v, S2=S2, MS=MS: v.tensor_scalar(MS, S2, 1.0 / D, None, ALU.mult), "VM")
                    op("vector", lambda v, M=M, M2=M2: v.tensor_mul(M2, M, M), "VM")
                    op("vector", lambda v, MS=MS, M2=M2, VAR=VAR: v.tensor_sub(VAR, MS, M2), "VM")
                    op("vector", lambda v, VAR=VAR: v.tensor_scalar(VAR, VAR, EPS, None, ALU.add), "VM")
                    wait("scalar", "VM", C["VM"])
                    op("scalar", lambda sC, VAR=VAR, SD=SD:
                       sC.activation(SD, VAR, AF.Sqrt), "AM")
                    wait("vector", "AM", C["AM"])
                    op("vector", lambda v, SD=SD, RSTD=RSTD: v.reciprocal(RSTD, SD), "VM")
                    op("vector", lambda v, M=M, RSTD=RSTD, NMR=NMR:
                       v.scalar_tensor_tensor(NMR, M, -1.0, RSTD, ALU.mult, ALU.mult), "VM")
                    wait("vector", "TM", x1t_rel)
                    op("vector", lambda v, tt=tt, RSTD=RSTD, NMR=NMR:
                       v.tensor_scalar(X1T[:, :], R1[tt][:, :], RSTD[:, :1], NMR[:, :1],
                                       ALU.mult, ALU.add), "VM")
                    cols_rel[par] = C["VM"]
                    vm_x1t = C["VM"]
                    xf = tt % 2
                    wait("tensor", "VM", vm_x1t)
                    wait("tensor", "ST", x1f_rel[xf])
                    for dc in range(DC):
                        bkk = (dc // 8) % 2
                        if dc % 8 == 0:
                            wait("tensor", "AM", psT_rel4[bkk])
                        op("tensor", lambda t, dc=dc, bkk=bkk:
                           t.transpose(psT[bkk][:, (dc % 8) * 128:(dc % 8 + 1) * 128],
                                       X1T[:, dc * 128:(dc + 1) * 128], EYE),
                           "TM" if dc % 8 == 7 else None)
                        if dc % 8 == 7:
                            wait("scalar", "TM", C["TM"])
                            op("scalar", lambda sC, dc=dc, bkk=bkk, xf=xf:
                               sC.copy(X1F[xf][:, (dc - 7) * 128:(dc + 1) * 128],
                                       psT[bkk][:, :]), "AM")
                            psT_rel4[bkk] = C["AM"]
                    x1t_rel = C["TM"]
                    wait("sync", "AM", C["AM"])
                    dma("sync", x1s_d[:, :, tb * 512 + tt * 128: tb * 512 + (tt + 1) * 128]
                        .rearrange("dc p j -> p dc j"),
                        X1F[xf].rearrange("p (dc j) -> p dc j", dc=DC), "ST")
                    x1f_rel[xf] = C["ST"]
                r1_rel = C["AM"]
            ph4_TM, ph4_AM, ph4_VM, ph4_ST = C["TM"], C["AM"], C["VM"], C["ST"]
        else:
            ph4_TM, ph4_AM, ph4_VM, ph4_ST = ph3_TM, ph3_AM, ph3_VM, ph3_ST

        if 5 in phases:
            pbf5 = DYN0
            def abf5(n):
                nonlocal pbf5
                o = pbf5; pbf5 += n
                assert pbf5 <= NBF, f"ph5 bf16 overflow {pbf5}"
                return o
            o_fuse = abf5(JC * 512)
            o_x1 = abf5(DC * 512)
            o_wg = abf5(2 * DC * 128)
            o_wu = abf5(2 * DC * 128)
            o_wd = abf5(2 * JC * 128)
            o_gt = abf5(512)
            o_ut = abf5(512)
            FUSE = BP[:, o_fuse:o_fuse + JC * 512]
            X1 = BP[:, o_x1:o_x1 + DC * 512]
            WG = [BP[:, o_wg + i * DC * 128:o_wg + (i + 1) * DC * 128] for i in range(2)]
            WU = [BP[:, o_wu + i * DC * 128:o_wu + (i + 1) * DC * 128] for i in range(2)]
            WD = [BP[:, o_wd + i * JC * 128:o_wd + (i + 1) * JC * 128] for i in range(2)]
            GT = BP[:, o_gt:o_gt + 512]
            UT = BP[:, o_ut:o_ut + 512]
            pf5 = FDYN0
            def af5(n):
                nonlocal pf5
                o = pf5; pf5 += n
                assert pf5 <= NF32, f"ph5 f32 overflow {pf5}"
                return o
            o_r2 = af5(2 * 512)
            o_dt = af5(2 * 512)
            o_sqf = af5(2 * 512)
            o_s1b = af5(512)
            o_s2b = af5(512)
            o_tmp = af5(512)
            o_tmp2 = af5(512)
            o_ostg = af5(2 * 512)
            R2 = [FP[:, o_r2 + i * 512:o_r2 + (i + 1) * 512] for i in range(2)]
            DT = [FP[:, o_dt + i * 512:o_dt + (i + 1) * 512] for i in range(2)]
            SQF = [FP[:, o_sqf + i * 512:o_sqf + (i + 1) * 512] for i in range(2)]
            S1B = FP[:, o_s1b:o_s1b + 512]
            S2B = FP[:, o_s2b:o_s2b + 512]
            TMP = FP[:, o_tmp:o_tmp + 512]
            TMP2 = FP[:, o_tmp2:o_tmp2 + 512]
            OSTG = [FP[:, o_ostg + i * 512:o_ostg + (i + 1) * 512] for i in range(2)]
            PRMC = FP[:, o_prmc:o_prmc + 64]

            wait("sync", "TM", ph4_TM)
            wait("sync", "AM", ph4_AM)
            wait("sync", "VM", ph4_VM)
            wait("gpsimd", "TM", ph4_TM)
            wait("gpsimd", "AM", ph4_AM)
            wait("gpsimd", "VM", ph4_VM)

            wg_rel = [0, 0]
            wd_rel = [0, 0]
            gbank_rel = [C["AM"]] * 4
            fuse_rel = 0
            x1_rel_tm = 0
            x1_rel_vm = 0
            x1_rel_am = 0
            dn_evac = [C["AM"], C["AM"]]
            r2_rel_tm = [C["TM"], C["TM"]]
            r2_rel_am = [C["AM"], C["AM"]]
            dt_rel = [C["VM"], C["VM"]]
            sqf_rel = [C["TM"], C["TM"]]
            trb_rel = [C["AM"], C["AM"]]
            s12_rel = [C["VM"], C["VM"]]
            ostg_rel = [0, 0]
            prm_rel = 0

            for sl in range(8):
                wait("sync", "ST", ph4_ST)
                wait("sync", "TM", x1_rel_tm)
                wait("sync", "VM", x1_rel_vm)
                dma("sync", X1.rearrange("p (dc j) -> p dc j", dc=DC),
                    x1s_d[:, :, sl * 512:(sl + 1) * 512].rearrange("dc p j -> p dc j"), "LD")
                ld_x1 = C["LD"]
                for j in range(JC):
                    wb = j % 2
                    wait("gpsimd", "TM", wg_rel[wb])
                    dma("gpsimd", WG[wb].rearrange("p (dc i) -> p dc i", dc=DC),
                        wg_d[:, j * 128:(j + 1) * 128].rearrange("(dc p) i -> p dc i", p=128), "LG")
                    dma("gpsimd", WU[wb].rearrange("p (dc i) -> p dc i", dc=DC),
                        wu_d[:, j * 128:(j + 1) * 128].rearrange("(dc p) i -> p dc i", p=128), "LG")
                    ld_w = C["LG"]
                    bg = (2 * j) % 4
                    bu = (2 * j + 1) % 4
                    wait("tensor", "LD", ld_x1)
                    wait("tensor", "LG", ld_w)
                    wait("tensor", "AM", max(gbank_rel[bg], gbank_rel[bu]))
                    for k in range(DC):
                        op("tensor", lambda t, bg=bg, k=k, wb=wb:
                           t.matmul(psA[:, bg * 512:(bg + 1) * 512],
                                    WG[wb][:, k * 128:(k + 1) * 128],
                                    X1[:, k * 512:(k + 1) * 512],
                                    start=(k == 0), stop=(k == DC - 1)),
                           "TM" if k == DC - 1 else None)
                    for k in range(DC):
                        op("tensor", lambda t, bu=bu, k=k, wb=wb:
                           t.matmul(psA[:, bu * 512:(bu + 1) * 512],
                                    WU[wb][:, k * 128:(k + 1) * 128],
                                    X1[:, k * 512:(k + 1) * 512],
                                    start=(k == 0), stop=(k == DC - 1)),
                           "TM" if k == DC - 1 else None)
                    wg_rel[wb] = C["TM"]
                    wait("scalar", "TM", C["TM"])
                    wait("scalar", "VM", C["VM"])
                    op("scalar", lambda sC, bg=bg, j=j:
                       sC.activation(GT[:, :], psA[:, bg * 512:(bg + 1) * 512], AF.Silu,
                                     scale=SGT[:, j:j + 1]), "AM")
                    gbank_rel[bg] = C["AM"]
                    op("scalar", lambda sC, bu=bu, j=j:
                       sC.activation(UT[:, :], psA[:, bu * 512:(bu + 1) * 512], AF.Copy,
                                     scale=SUT[:, j:j + 1]), "AM")
                    gbank_rel[bu] = C["AM"]
                    wait("vector", "AM", C["AM"])
                    wait("vector", "TM", fuse_rel)
                    op("vector", lambda v, j=j:
                       v.tensor_mul(FUSE[:, j * 512:(j + 1) * 512], GT[:, :], UT[:, :]), "VM")
                vm_fuse = C["VM"]
                for dout in range(DC):
                    wb = dout % 2
                    wait("gpsimd", "TM", wd_rel[wb])
                    dma("gpsimd", WD[wb].rearrange("p (jc i) -> p jc i", jc=JC),
                        wd_d[:, dout * 128:(dout + 1) * 128].rearrange("(jc p) i -> p jc i", p=128), "LG")
                    ld_w = C["LG"]
                    dnb = dout % 2
                    ps_dn = psB if dnb == 0 else psC
                    wait("tensor", "LG", ld_w)
                    wait("tensor", "VM", vm_fuse)
                    wait("tensor", "AM", dn_evac[dnb])
                    for jc in range(JC):
                        op("tensor", lambda t, jc=jc, wb=wb, ps_dn=ps_dn:
                           t.matmul(ps_dn[:, :], WD[wb][:, jc * 128:(jc + 1) * 128],
                                    FUSE[:, jc * 512:(jc + 1) * 512],
                                    start=(jc == 0), stop=(jc == JC - 1)),
                           "TM" if jc == JC - 1 else None)
                    wd_rel[wb] = C["TM"]
                    dn_done = C["TM"]
                    wait("scalar", "TM", dn_done)
                    wait("scalar", "VM", dt_rel[dnb])
                    op("scalar", lambda sC, dnb=dnb, dout=dout, ps_dn=ps_dn:
                       sC.activation(DT[dnb][:, :], ps_dn[:, :], AF.Copy,
                                     scale=SDT[:, dout:dout + 1]), "AM")
                    dn_evac[dnb] = C["AM"]
                    wait("vector", "AM", dn_evac[dnb])
                    wait("vector", "TM", r2_rel_tm[dnb])
                    wait("vector", "AM", r2_rel_am[dnb])
                    op("vector", lambda v, dnb=dnb, dout=dout:
                       v.tensor_add(R2[dnb][:, :], DT[dnb][:, :], X1[:, dout * 512:(dout + 1) * 512]), "VM")
                    dt_rel[dnb] = C["VM"]
                    vm_r2 = C["VM"]
                    wait("scalar", "VM", vm_r2)
                    wait("scalar", "TM", sqf_rel[dnb])
                    op("scalar", lambda sC, dnb=dnb:
                       sC.activation(SQF[dnb][:, :], R2[dnb][:, :], AF.Square), "AM")
                    am_sq = C["AM"]
                    wait("tensor", "VM", vm_r2)
                    wait("tensor", "VM", s12_rel[0])
                    op("tensor", lambda t, dnb=dnb:
                       t.matmul(psA[:, 2 * 512:3 * 512], ONESF, R2[dnb][:, :],
                                start=True, stop=True), "TM")
                    tm_s1 = C["TM"]
                    wait("tensor", "AM", am_sq)
                    wait("tensor", "VM", s12_rel[1])
                    op("tensor", lambda t, dnb=dnb:
                       t.matmul(psA[:, 3 * 512:4 * 512], ONESF, SQF[dnb][:, :],
                                start=True, stop=True), "TM")
                    tm_s2 = C["TM"]
                    sqf_rel[dnb] = tm_s2
                    wait("vector", "TM", tm_s1)
                    if dout == 0:
                        op("vector", lambda v: v.tensor_copy(S1B[:, :], psA[:, 2 * 512:3 * 512]), "VM")
                    else:
                        op("vector", lambda v: v.tensor_add(S1B[:, :], S1B[:, :], psA[:, 2 * 512:3 * 512]), "VM")
                    s12_rel[0] = C["VM"]
                    wait("vector", "TM", tm_s2)
                    if dout == 0:
                        op("vector", lambda v: v.tensor_copy(S2B[:, :], psA[:, 3 * 512:4 * 512]), "VM")
                    else:
                        op("vector", lambda v: v.tensor_add(S2B[:, :], S2B[:, :], psA[:, 3 * 512:4 * 512]), "VM")
                    s12_rel[1] = C["VM"]
                    bt = dout % 2
                    wait("tensor", "VM", vm_r2)
                    wait("tensor", "AM", trb_rel[bt])
                    for tt in range(4):
                        op("tensor", lambda t, tt=tt, dnb=dnb, bt=bt:
                           t.transpose(psA[:, bt * 512 + tt * 128: bt * 512 + (tt + 1) * 128],
                                       R2[dnb][:, tt * 128:(tt + 1) * 128], EYEF),
                           "TM" if tt == 3 else None)
                    r2_rel_tm[dnb] = C["TM"]
                    r2_rel_am[dnb] = am_sq
                    ob = dout % 2
                    wait("scalar", "TM", C["TM"])
                    wait("scalar", "ST", ostg_rel[ob])
                    op("scalar", lambda sC, bt=bt, ob=ob:
                       sC.copy(OSTG[ob][:, :], psA[:, bt * 512:(bt + 1) * 512]), "AM")
                    trb_rel[bt] = C["AM"]
                    wait("sync", "AM", C["AM"])
                    dma("sync",
                        r2s_d[sl * 512:(sl + 1) * 512, dout * 128:(dout + 1) * 128]
                        .rearrange("(tt p) j -> p tt j", p=128),
                        OSTG[ob].rearrange("p (tt j) -> p tt j", tt=4), "ST")
                    ostg_rel[ob] = C["ST"]
                if debug:
                    wait("sync", "VM", C["VM"])
                    dma("sync", s1b_d[sl], S1B[:, :], "ST")
                    dma("sync", s2b_d[sl], S2B[:, :], "ST")
                    wait("vector", "ST", C["ST"])
                wait("vector", "AM", prm_rel)
                op("vector", lambda v: v.tensor_scalar(S1B[:, :], S1B[:, :], 1.0 / D, None, ALU.mult), "VM")
                op("vector", lambda v: v.tensor_scalar(S2B[:, :], S2B[:, :], 1.0 / D, None, ALU.mult), "VM")
                op("vector", lambda v: v.tensor_mul(TMP[:, :], S1B[:, :], S1B[:, :]), "VM")
                op("vector", lambda v: v.tensor_sub(S2B[:, :], S2B[:, :], TMP[:, :]), "VM")
                op("vector", lambda v: v.tensor_scalar(S2B[:, :], S2B[:, :], EPS, None, ALU.add), "VM")
                wait("scalar", "VM", C["VM"])
                op("scalar", lambda sC: sC.activation(TMP[:, :], S2B[:, :], AF.Sqrt), "AM")
                wait("vector", "AM", C["AM"])
                op("vector", lambda v: v.reciprocal(TMP2[:, :], TMP[:, :]), "VM")
                op("vector", lambda v:
                   v.scalar_tensor_tensor(S1B[:, :], S1B[:, :], -1.0, TMP2[:, :],
                                          ALU.mult, ALU.mult), "VM")
                vm_prm = C["VM"]
                wait("tensor", "VM", vm_prm)
                wait("tensor", "AM", trb_rel[0])
                for tt in range(4):
                    gt_ = sl * 4 + tt
                    op("tensor", lambda t, tt=tt:
                       t.transpose(psA[:, tt * 2:tt * 2 + 1], TMP2[0:1, tt * 128:(tt + 1) * 128],
                                   EYEF[:1, :1]), None)
                    op("tensor", lambda t, tt=tt:
                       t.transpose(psA[:, tt * 2 + 1:tt * 2 + 2], S1B[0:1, tt * 128:(tt + 1) * 128],
                                   EYEF[:1, :1]),
                       "TM" if tt == 3 else None)
                tm_prm = C["TM"]
                wait("scalar", "TM", tm_prm)
                for tt in range(4):
                    gt_ = sl * 4 + tt
                    op("scalar", lambda sC, tt=tt, gt_=gt_:
                       sC.copy(PRMC[:, gt_:gt_ + 1], psA[:, tt * 2:tt * 2 + 1]), None)
                    op("scalar", lambda sC, tt=tt, gt_=gt_:
                       sC.copy(PRMC[:, 32 + gt_:32 + gt_ + 1], psA[:, tt * 2 + 1:tt * 2 + 2]),
                       "AM" if tt == 3 else None)
                prm_rel = C["AM"]
                trb_rel[0] = C["AM"]
                gbank_rel = [C["AM"]] * 4
                fuse_rel = C["TM"]
                x1_rel_tm = C["TM"]
                x1_rel_vm = C["VM"]
            ph5_TM, ph5_AM, ph5_VM, ph5_ST = C["TM"], C["AM"], C["VM"], C["ST"]

            pf6 = FDYN0
            def af6(n):
                nonlocal pf6
                o = pf6; pf6 += n
                assert pf6 <= NF32
                return o
            o_rb = af6(2 * 2048)
            o_ob = af6(2 * 2048)
            RB = [FP[:, o_rb + i * 2048:o_rb + (i + 1) * 2048] for i in range(2)]
            OB = [FP[:, o_ob + i * 2048:o_ob + (i + 1) * 2048] for i in range(2)]
            wait("sync", "TM", ph5_TM)
            wait("sync", "AM", ph5_AM)
            wait("sync", "VM", ph5_VM)
            rb_rel = [0, 0]
            ob_rel = [0, 0]
            for rb_i in range(T // 128):
                bi = rb_i % 2
                wait("sync", "VM", rb_rel[bi])
                dma("sync", RB[bi][:, :], r2s_d[rb_i * 128:(rb_i + 1) * 128, :], "LD")
                ld_rb = C["LD"]
                wait("vector", "LD", ld_rb)
                wait("vector", "ST", ob_rel[bi])
                op("vector", lambda v, bi=bi, rb_i=rb_i:
                   v.tensor_scalar(OB[bi][:, :], RB[bi][:, :], PRMC[:, rb_i:rb_i + 1],
                                   PRMC[:, 32 + rb_i:32 + rb_i + 1],
                                   ALU.mult, ALU.add), "VM")
                rb_rel[bi] = C["VM"]
                wait("sync", "VM", C["VM"])
                dma("sync", out_d[rb_i * 128:(rb_i + 1) * 128, :], OB[bi][:, :], "ST")
                ob_rel[bi] = C["ST"]

        if debug and 5 in phases:
            wait("sync", "AM", C["AM"])
            dma("sync", prm_d[:, :], FP[:, o_prmc:o_prmc + 64], "ST")
        wait("sync", "ST", C["ST"])
        wait("sync", "LG", C["LG"])

        @block.sync
        def _(eng):
            for f in Q["sync"]:
                f(eng)

        @block.gpsimd
        def _(eng):
            for f in Q["gpsimd"]:
                f(eng)

        @block.tensor
        def _(eng):
            for f in Q["tensor"]:
                f(eng)

        @block.vector
        def _(eng):
            for f in Q["vector"]:
                f(eng)

        @block.scalar
        def _(eng):
            for f in Q["scalar"]:
                f(eng)

    return nc


def _causal_mask():
    m = np.zeros((128, 2048), dtype=np.float32)
    p = np.arange(128)[:, None]
    for r in range(4):
        j = np.arange(512)[None, :]
        m[:, r * 512:(r + 1) * 512] = np.where(j > r * 128 + p, -1e30, 0.0)
    return m.astype(ml_dtypes.bfloat16)


def make_core_inputs(core, inputs):
    X = np.asarray(inputs["L0_attn_x"])
    x_c = np.ascontiguousarray(X[2 * core:2 * core + 2].reshape(T, D))
    return {
        "x": x_c,
        "wqkv": np.ascontiguousarray(inputs["L0_attn_W_qkv"]),
        "bqkv": np.ascontiguousarray(np.asarray(inputs["L0_attn_b_qkv"]).reshape(1, TD)),
        "wout": np.ascontiguousarray(inputs["L0_attn_W_out"]),
        "bout": np.ascontiguousarray(np.asarray(inputs["L0_attn_b_out"]).reshape(1, D)),
        "wg": np.ascontiguousarray(inputs["L0_ffn_w_gate"]),
        "wu": np.ascontiguousarray(inputs["L0_ffn_w_up"]),
        "wd": np.ascontiguousarray(inputs["L0_ffn_w_down"]),
        "sg": np.ascontiguousarray(np.asarray(inputs["L0_ffn_s_gate"], dtype=np.float32).reshape(JC, 128)),
        "su": np.ascontiguousarray(np.asarray(inputs["L0_ffn_s_up"], dtype=np.float32).reshape(JC, 128)),
        "sd": np.ascontiguousarray(np.asarray(inputs["L0_ffn_s_down"], dtype=np.float32).reshape(DC, 128)),
        "eye": np.eye(128, dtype=ml_dtypes.bfloat16),
        "eyef": np.eye(128, dtype=np.float32),
        "onesf": np.ones((128, 128), dtype=np.float32),
        "mask": _causal_mask(),
        "ones": np.ones((1, 512), dtype=ml_dtypes.bfloat16),
    }


def kernel(**inputs):
    nc = build_core_program()
    in_maps = [make_core_inputs(c, inputs) for c in range(NCORES)]
    res = run_bass_kernel_spmd(nc, in_maps, core_ids=list(range(NCORES)))
    out = np.concatenate([r["out"].reshape(NSEQ, SEQ, D) for r in res.results], axis=0)
    return out.astype(np.float32)


# version 02
# speedup vs baseline: 1.0014x; 1.0014x over previous
import numpy as np
import ml_dtypes
import concourse.bass as bass
import concourse.mybir as mybir
from concourse.bass_utils import run_bass_kernel_spmd
from contextlib import ExitStack

BF16 = mybir.dt.bfloat16
F32 = mybir.dt.float32
I8 = mybir.dt.int8
AF = mybir.ActivationFunctionType
ALU = mybir.AluOpType
AX = mybir.AxisListType

NCORES = 8
SEQ = 2048
NSEQ = 2
T = SEQ * NSEQ
D = 2048
DC = D // 128
H = 16
TD = 3 * D
INTER = 8192
JC = INTER // 128
NHALF = 2
THALF = T // NHALF
INV_SQRT_HD = 1.0 / float(np.float32(ml_dtypes.bfloat16(np.sqrt(128.0))))
EPS = 1e-5


def build_core_program(phases=(0, 2, 3, 4, 5), debug=False):
    nc = bass.Bass("TRN2", target_bir_lowering=False)

    x_d = nc.dram_tensor("x", [T, D], BF16, kind="ExternalInput")
    wqkv_d = nc.dram_tensor("wqkv", [D, TD], BF16, kind="ExternalInput")
    bqkv_d = nc.dram_tensor("bqkv", [1, TD], BF16, kind="ExternalInput")
    wout_d = nc.dram_tensor("wout", [D, D], BF16, kind="ExternalInput")
    bout_d = nc.dram_tensor("bout", [1, D], BF16, kind="ExternalInput")
    wg_d = nc.dram_tensor("wg", [D, INTER], I8, kind="ExternalInput")
    wu_d = nc.dram_tensor("wu", [D, INTER], I8, kind="ExternalInput")
    wd_d = nc.dram_tensor("wd", [INTER, D], I8, kind="ExternalInput")
    sg_d = nc.dram_tensor("sg", [JC, 128], F32, kind="ExternalInput")
    su_d = nc.dram_tensor("su", [JC, 128], F32, kind="ExternalInput")
    sd_d = nc.dram_tensor("sd", [DC, 128], F32, kind="ExternalInput")
    eye_d = nc.dram_tensor("eye", [128, 128], BF16, kind="ExternalInput")
    eyef_d = nc.dram_tensor("eyef", [128, 128], F32, kind="ExternalInput")
    mask_d = nc.dram_tensor("mask", [128, 2048], BF16, kind="ExternalInput")
    ones_d = nc.dram_tensor("ones", [128, 512], BF16, kind="ExternalInput")
    onesf_d = nc.dram_tensor("onesf", [128, 128], F32, kind="ExternalInput")

    out_d = nc.dram_tensor("out", [T, D], F32, kind="ExternalOutput")
    scratch_kind = "ExternalOutput" if debug else "Internal"
    qts_d = nc.dram_tensor("qts", [DC, 128, T], BF16, kind=scratch_kind)
    kts_d = nc.dram_tensor("kts", [DC, 128, T], BF16, kind=scratch_kind)
    vs_d = nc.dram_tensor("vs", [T, D], BF16, kind=scratch_kind)
    atts_d = nc.dram_tensor("atts", [DC, 128, T], BF16, kind=scratch_kind)
    x1s_d = nc.dram_tensor("x1s", [DC, 128, T], BF16, kind=scratch_kind)
    r2s_d = nc.dram_tensor("r2s", [T, D], F32, kind=scratch_kind)
    prm_d = nc.dram_tensor("prmdbg", [128, 64], F32, kind=scratch_kind) if debug else None
    s1b_d = nc.dram_tensor("s1bdbg", [8, 128, 512], F32, kind=scratch_kind) if debug else None
    s2b_d = nc.dram_tensor("s2bdbg", [8, 128, 512], F32, kind=scratch_kind) if debug else None

    ENGS = ("sync", "gpsimd", "tensor", "vector", "scalar")
    Q = {k: [] for k in ENGS}
    C = {"LD": 0, "LG": 0, "ST": 0, "TM": 0, "VM": 0, "AM": 0, "GM": 0}
    sems = {}

    def em(qname, fn):
        Q[qname].append(fn)

    def wait(qname, sem, val):
        if val <= 0:
            return
        em(qname, lambda e, s=sem, v=val: e.wait_ge(sems[s], v))

    def dma(qname, dst, src, sem):
        C[sem] += 16
        em(qname, lambda e, d=dst, s=src, sm=sem: e.dma_start(d, s).then_inc(sems[sm], 16))

    def dma_t(qname, dst, src, sem):
        C[sem] += 16
        em(qname, lambda e, d=dst, s=src, sm=sem: e.dma_start_transpose(d, s).then_inc(sems[sm], 16))

    def op(qname, f, sem=None):
        if sem is None:
            em(qname, lambda e, ff=f: ff(e))
        elif qname in ("scalar", "vector", "gpsimd"):
            C[sem] += 1
            em(qname, lambda e, ff=f: ff(e))
            em(qname, lambda e, sm=sem: e.drain().then_inc(sems[sm], 1))
        else:
            C[sem] += 1
            em(qname, lambda e, ff=f, sm=sem: ff(e).then_inc(sems[sm], 1))

    NBF = 77500
    NF32 = 12500

    with ExitStack() as ctx:
        e = ctx.enter_context
        BP = e(nc.sbuf_tensor("bp", [128, NBF], BF16))
        FP = e(nc.sbuf_tensor("fp", [128, NF32], F32))
        psA = e(nc.psum_tensor("psA", [128, 2048], F32))
        psB = e(nc.psum_tensor("psB", [128, 512], F32))
        psC = e(nc.psum_tensor("psC", [128, 512], F32))
        psT0 = e(nc.psum_tensor("psT0", [128, 1024], BF16))
        psT1 = e(nc.psum_tensor("psT1", [128, 1024], BF16))
        psT = [psT0, psT1]
        for s in ("LD", "LG", "ST", "TM", "VM", "AM", "GM"):
            sems[s] = e(nc.semaphore(s))
        block = e(nc.Block())

        pbf = 0
        def abf(n):
            nonlocal pbf
            o = pbf; pbf += n
            assert pbf <= NBF, f"bf16 pool overflow {pbf}"
            return o
        pf = 0
        def af32(n):
            nonlocal pf
            o = pf; pf += n
            assert pf <= NF32, f"f32 pool overflow {pf}"
            return o

        o_eye = abf(128)
        o_mask = abf(2048)
        o_ones = abf(512)
        o_bqkv = abf(TD)
        o_bout = abf(D)
        DYN0 = pbf

        o_eyef = af32(128)
        o_onesf = af32(128)
        o_sgT = af32(JC)
        o_suT = af32(JC)
        o_sdT = af32(DC)
        o_sstage = af32(128)
        o_cols = af32(32)
        o_sq = af32(2048)
        o_nm = af32(8)
        o_prmc = af32(64)
        o_srow = af32(1024)
        o_rsb = af32(512)
        FDYN0 = pf

        EYE = BP[:, o_eye:o_eye + 128]
        MASK = BP[:, o_mask:o_mask + 2048]
        ONES = BP[:, o_ones:o_ones + 512]
        BQKV = BP[:, o_bqkv:o_bqkv + TD]
        BOUT = BP[:, o_bout:o_bout + D]
        EYEF = FP[:, o_eyef:o_eyef + 128]
        ONESF = FP[:, o_onesf:o_onesf + 128]
        SGT = FP[:, o_sgT:o_sgT + JC]
        SUT = FP[:, o_suT:o_suT + JC]
        SDT = FP[:, o_sdT:o_sdT + DC]
        SQ = FP[:, o_sq:o_sq + 2048]

        if 0 in phases:
            dma("sync", EYE, eye_d[:, :], "LD")
            dma("sync", MASK, mask_d[:, :], "LD")
            dma("sync", ONES[:, :], ones_d[:, :], "LD")
            dma("sync", BQKV[0:1, :], bqkv_d[:, :], "LD")
            dma("sync", BOUT[0:1, :], bout_d[:, :], "LD")
            dma("sync", EYEF, eyef_d[:, :], "LD")
            dma("sync", ONESF, onesf_d[:, :], "LD")
            sst = FP[:, o_sstage:o_sstage + 128]
            for (src_d, rows, dstap) in ((sg_d, JC, SGT), (su_d, JC, SUT), (sd_d, DC, SDT)):
                dma("sync", sst[:rows, :], src_d[:, :], "LD")
                wait("tensor", "LD", C["LD"])
                op("tensor", lambda t, r=rows: t.transpose(psB[:, 0:r], sst[:r, :], EYEF[:r, :r]), "TM")
                wait("scalar", "TM", C["TM"])
                op("scalar", lambda s, r=rows, dd=dstap: s.copy(dd[:, 0:r], psB[:, 0:r]), "AM")
                wait("tensor", "AM", C["AM"])
                wait("sync", "AM", C["AM"])

        if 2 in phases:
            pbf2 = DYN0
            def abf2(n):
                nonlocal pbf2
                o = pbf2; pbf2 += n
                assert pbf2 <= NBF, f"ph2 bf16 overflow {pbf2}"
                return o
            o_xt = abf2(DC * THALF)
            o_wqk = abf2(2 * DC * 128)
            o_wv = abf2(DC * 512)
            o_qkst = abf2(2 * THALF)
            o_vst = abf2(2 * 512)
            XT = [BP[:, o_xt + dc * THALF: o_xt + (dc + 1) * THALF] for dc in range(DC)]
            WQK = [BP[:, o_wqk + i * DC * 128: o_wqk + (i + 1) * DC * 128] for i in range(2)]
            WV = BP[:, o_wv:o_wv + DC * 512]
            QKST = [BP[:, o_qkst + i * THALF: o_qkst + (i + 1) * THALF] for i in range(2)]
            VST = [BP[:, o_vst + i * 512: o_vst + (i + 1) * 512] for i in range(2)]

            bank_rel = [C["AM"]] * 4
            qk_store_rel = [0, 0]
            v_store_rel = [0, 0]
            wqk_rel = [0, 0]
            wv_rel = 0
            xt_rel = 0

            for hf in range(NHALF):
                wait("sync", "TM", xt_rel)
                for dc in range(DC):
                    dma_t("sync", XT[dc], x_d[hf * THALF:(hf + 1) * THALF, dc * 128:(dc + 1) * 128], "LD")
                ld_xt = C["LD"]

                for f in range(2 * DC):
                    wb = f % 2
                    wait("sync", "TM", wqk_rel[wb])
                    dma("sync", WQK[wb].rearrange("p (dc j) -> p dc j", dc=DC),
                        wqkv_d[:, f * 128:(f + 1) * 128].rearrange("(dc p) j -> p dc j", p=128), "LD")
                    ld_w = C["LD"]
                    qb = f % 2
                    wait("scalar", "ST", qk_store_rel[qb])
                    wait("tensor", "LD", max(ld_xt, ld_w))
                    for tb in range(THALF // 512):
                        b = (hf * 144 + f * 4 + tb) % 4
                        wait("tensor", "AM", bank_rel[b])
                        for k in range(DC):
                            op("tensor", lambda t, b=b, k=k, wb=wb, tb=tb:
                               t.matmul(psA[:, b * 512:(b + 1) * 512],
                                        WQK[wb][:, k * 128:(k + 1) * 128],
                                        XT[k][:, tb * 512:(tb + 1) * 512],
                                        start=(k == 0), stop=False))
                        op("tensor", lambda t, b=b, f=f:
                           t.matmul(psA[:, b * 512:(b + 1) * 512],
                                    BQKV[0:1, f * 128:(f + 1) * 128],
                                    ONES[0:1, 0:512], start=False, stop=True), "TM")
                        wait("scalar", "TM", C["TM"])
                        sc = INV_SQRT_HD if f < DC else 1.0
                        op("scalar", lambda s, b=b, qb=qb, tb=tb, sc=sc:
                           s.activation(QKST[qb][:, tb * 512:(tb + 1) * 512],
                                        psA[:, b * 512:(b + 1) * 512], AF.Copy, scale=sc), "AM")
                        bank_rel[b] = C["AM"]
                    wqk_rel[wb] = C["TM"]
                    wait("sync", "AM", C["AM"])
                    dst = (qts_d if f < DC else kts_d)[f % DC][:, hf * THALF:(hf + 1) * THALF]
                    dma("sync", dst, QKST[qb][:, :], "ST")
                    qk_store_rel[qb] = C["ST"]

                for vb in range(4):
                    wait("sync", "TM", wv_rel)
                    dma("sync", WV.rearrange("p (dc j) -> p dc j", dc=DC),
                        wqkv_d[:, 2 * D + vb * 512: 2 * D + (vb + 1) * 512]
                        .rearrange("(dc p) j -> p dc j", p=128), "LD")
                    ld_wv = C["LD"]
                    wait("tensor", "LD", ld_wv)
                    for tt in range(THALF // 128):
                        b = (vb * 16 + tt) % 4
                        vsb = tt % 2
                        wait("tensor", "AM", bank_rel[b])
                        wait("scalar", "ST", v_store_rel[vsb])
                        for k in range(DC):
                            op("tensor", lambda t, b=b, k=k, tt=tt:
                               t.matmul(psA[:, b * 512:(b + 1) * 512],
                                        XT[k][:, tt * 128:(tt + 1) * 128],
                                        WV[:, k * 512:(k + 1) * 512],
                                        start=(k == 0), stop=False))
                        op("tensor", lambda t, b=b, vb=vb:
                           t.matmul(psA[:, b * 512:(b + 1) * 512],
                                    ONES[0:1, 0:128],
                                    BQKV[0:1, 2 * D + vb * 512: 2 * D + vb * 512 + 512],
                                    start=False, stop=True), "TM")
                        wait("scalar", "TM", C["TM"])
                        op("scalar", lambda s, b=b, vsb=vsb:
                           s.copy(VST[vsb][:, :], psA[:, b * 512:(b + 1) * 512]), "AM")
                        bank_rel[b] = C["AM"]
                        wait("sync", "AM", C["AM"])
                        r0 = hf * THALF + tt * 128
                        dma("sync", vs_d[r0:r0 + 128, vb * 512:(vb + 1) * 512], VST[vsb][:, :], "ST")
                        v_store_rel[vsb] = C["ST"]
                    wv_rel = C["TM"]
                xt_rel = C["TM"]
            ph2_TM, ph2_AM, ph2_VM, ph2_ST = C["TM"], C["AM"], C["VM"], C["ST"]
        else:
            ph2_TM = ph2_AM = ph2_VM = ph2_ST = 0

        if 3 in phases:
            pbf3 = DYN0
            def abf3(n):
                nonlocal pbf3
                o = pbf3; pbf3 += n
                assert pbf3 <= NBF
                return o
            o_qh = abf3(2 * SEQ)
            o_kh = abf3(2 * SEQ)
            o_vh = abf3(2 * SEQ)
            o_p = abf3(2 * 2048)
            o_pt = abf3(16 * 512)
            o_ast = abf3(2 * 512)
            QH = [BP[:, o_qh + i * SEQ:o_qh + (i + 1) * SEQ] for i in range(2)]
            KH = [BP[:, o_kh + i * SEQ:o_kh + (i + 1) * SEQ] for i in range(2)]
            VH = [BP[:, o_vh + i * SEQ:o_vh + (i + 1) * SEQ] for i in range(2)]
            PB = [BP[:, o_p + i * 2048:o_p + (i + 1) * 2048] for i in range(2)]
            PT = BP[:, o_pt:o_pt + 16 * 512]
            PT3 = PT.rearrange("p (kt j) -> p kt j", kt=16)
            AST = [BP[:, o_ast + i * 512:o_ast + (i + 1) * 512] for i in range(2)]
            NM = [FP[:, o_nm + i:o_nm + i + 1] for i in range(2)]
            SR1 = FP[0:1, o_srow:o_srow + 512]
            SR2 = FP[0:1, o_srow + 512:o_srow + 1024]
            RSB = FP[:, o_rsb:o_rsb + 512]

            wait("sync", "TM", ph2_TM)
            wait("sync", "AM", ph2_AM)

            qkv_rel = [0, 0]
            exp_done = C["AM"]
            pt_am = C["AM"]
            pt_vm = C["VM"]
            av_done = C["TM"]
            o_evac = C["VM"]
            ast_rel = [0, 0]
            psT_rel = [{"AM": C["AM"], "VM": C["VM"]} for _ in range(2)]
            pb_rel = [C["TM"], C["TM"]]
            r_evac = C["VM"]
            srow_rel = C["TM"]

            for s in range(NSEQ):
                for h in range(H):
                    hb = (s * H + h) % 2
                    wait("sync", "TM", qkv_rel[hb])
                    dma("sync", QH[hb], qts_d[h][:, s * SEQ:(s + 1) * SEQ], "LD")
                    dma("sync", KH[hb], kts_d[h][:, s * SEQ:(s + 1) * SEQ], "LD")
                    dma("sync", VH[hb].rearrange("p (kt j) -> p kt j", kt=16),
                        vs_d[s * SEQ:(s + 1) * SEQ, h * 128:(h + 1) * 128]
                        .rearrange("(kt p) j -> p kt j", p=128), "LD")
                    ld_h = C["LD"]
                    wait("tensor", "LD", ld_h)
                    for qb in range(4):
                        nkt = 4 * (qb + 1)
                        N = 512 * (qb + 1)
                        wait("gpsimd", "TM", av_done)
                        wait("gpsimd", "AM", pt_am)
                        wait("gpsimd", "VM", pt_vm)
                        for qtl in range(3):
                            nz = 3 - qtl
                            qt = qb * 4 + qtl
                            op("gpsimd", lambda g, qt=qt, qtl=qtl, nz=nz:
                               g.memset(PT3[:, qt + 1:qt + 1 + nz, qtl * 128:(qtl + 1) * 128], 0.0), "GM")
                        gm_after = C["GM"]
                        wait("scalar", "TM", av_done)
                        wait("vector", "TM", av_done)
                        for qtl in range(4):
                            qt = qb * 4 + qtl
                            pi = qt % 2
                            wait("tensor", "AM", exp_done)
                            for c in range(qb + 1):
                                last = (c == qb)
                                op("tensor", lambda t, c=c, qt=qt, hb=hb, last=last:
                                   t.matmul(psA[:, c * 512:(c + 1) * 512],
                                            QH[hb][:, qt * 128:(qt + 1) * 128],
                                            KH[hb][:, c * 512:(c + 1) * 512],
                                            start=True, stop=not last))
                            op("tensor", lambda t, qb=qb, qtl=qtl:
                               t.matmul(psA[:, qb * 512:(qb + 1) * 512],
                                        EYE, MASK[:, qtl * 512:(qtl + 1) * 512],
                                        start=False, stop=True), "TM")
                            sc_tm = C["TM"]
                            wait("vector", "TM", sc_tm)
                            op("vector", lambda v, pi=pi, N=N:
                               v.reduce_max(NM[pi][:, :], psA[:, :N], axis=AX.X, negate=True), "VM")
                            wait("scalar", "VM", C["VM"])
                            wait("scalar", "TM", pb_rel[pi])
                            op("scalar", lambda sC, pi=pi, N=N:
                               sC.activation(PB[pi][:, :N], psA[:, :N], AF.Exp,
                                             bias=NM[pi][:, :1], scale=1.0), "AM")
                            exp_done = C["AM"]
                            wait("tensor", "AM", exp_done)
                            wait("tensor", "GM", gm_after)
                            ntr = qt + 1
                            for kt in range(ntr):
                                bkk = (kt // 8) % 2
                                if kt % 8 == 0:
                                    wait("tensor", "AM", psT_rel[bkk]["AM"])
                                    wait("tensor", "VM", psT_rel[bkk]["VM"])
                                lastt = (kt % 8 == 7 or kt == ntr - 1)
                                op("tensor", lambda t, kt=kt, pi=pi, bkk=bkk:
                                   t.transpose(psT[bkk][:, (kt % 8) * 128:(kt % 8 + 1) * 128],
                                               PB[pi][:, kt * 128:(kt + 1) * 128], EYE),
                                   "TM" if lastt else None)
                                if lastt:
                                    ntiles = kt % 8 + 1
                                    k0 = kt - ntiles + 1
                                    eng3 = "scalar" if bkk == 0 else "vector"
                                    wait(eng3, "TM", C["TM"])
                                    if eng3 == "scalar":
                                        op(eng3, lambda sC, k0=k0, ntiles=ntiles, qtl=qtl, bkk=bkk:
                                           sC.copy(PT3[:, k0:k0 + ntiles, qtl * 128:(qtl + 1) * 128],
                                                   psT[bkk].rearrange("p (kt j) -> p kt j", kt=8)
                                                   [:, 0:ntiles, :]), "AM")
                                        psT_rel[bkk] = {"AM": C["AM"], "VM": 0}
                                        pt_am = C["AM"]
                                    else:
                                        op(eng3, lambda v, k0=k0, ntiles=ntiles, qtl=qtl, bkk=bkk:
                                           v.tensor_copy(PT3[:, k0:k0 + ntiles, qtl * 128:(qtl + 1) * 128],
                                                         psT[bkk].rearrange("p (kt j) -> p kt j", kt=8)
                                                         [:, 0:ntiles, :]), "VM")
                                        psT_rel[bkk] = {"AM": 0, "VM": C["VM"]}
                                        pt_vm = C["VM"]
                            pb_rel[pi] = C["TM"]
                        wait("tensor", "AM", pt_am)
                        wait("tensor", "VM", pt_vm)
                        wait("tensor", "GM", gm_after)
                        wait("tensor", "VM", r_evac)
                        for kt in range(nkt):
                            op("tensor", lambda t, kt=kt, nkt=nkt:
                               t.matmul(psC[0:1, :], ONES[:, 0:1],
                                        PT[:, kt * 512:(kt + 1) * 512],
                                        start=(kt == 0), stop=(kt == nkt - 1)),
                               "TM" if kt == nkt - 1 else None)
                        sums_tm = C["TM"]
                        wait("scalar", "TM", sums_tm)
                        op("scalar", lambda sC: sC.copy(SR1[:, :], psC[0:1, :]), "AM")
                        op("scalar", lambda sC: sC.activation(SR2[:, :], SR1[:, :], AF.Ln), "AM")
                        op("scalar", lambda sC: sC.activation(SR1[:, :], SR2[:, :], AF.Exp, scale=-1.0), "AM")
                        srow_am = C["AM"]
                        wait("tensor", "AM", srow_am)
                        op("tensor", lambda t:
                           t.matmul(psC[:, :], ONESF[0:1, 0:128], SR1[:, :],
                                    start=True, stop=True), "TM")
                        bcast_tm = C["TM"]
                        wait("vector", "TM", bcast_tm)
                        op("vector", lambda v: v.tensor_copy(RSB[:, :], psC[:, :]), "VM")
                        r_evac = C["VM"]
                        wait("tensor", "VM", o_evac)
                        for kt in range(nkt):
                            op("tensor", lambda t, kt=kt, hb=hb, nkt=nkt:
                               t.matmul(psB[:, :], VH[hb][:, kt * 128:(kt + 1) * 128],
                                        PT[:, kt * 512:(kt + 1) * 512],
                                        start=(kt == 0), stop=(kt == nkt - 1)),
                               "TM" if kt == nkt - 1 else None)
                        av_done = C["TM"]
                        ab = qb % 2
                        wait("vector", "TM", av_done)
                        wait("vector", "VM", r_evac)
                        wait("vector", "ST", ast_rel[ab])
                        op("vector", lambda v, ab=ab:
                           v.tensor_mul(AST[ab][:, :], psB[:, :], RSB[:, :]), "VM")
                        o_evac = C["VM"]
                        wait("sync", "VM", o_evac)
                        dma("sync", atts_d[h][:, s * SEQ + qb * 512: s * SEQ + (qb + 1) * 512],
                            AST[ab][:, :], "ST")
                        ast_rel[ab] = C["ST"]
                    qkv_rel[hb] = C["TM"]
            ph3_TM, ph3_AM, ph3_VM, ph3_ST = C["TM"], C["AM"], C["VM"], C["ST"]
        else:
            ph3_TM, ph3_AM, ph3_VM, ph3_ST = ph2_TM, ph2_AM, ph2_VM, ph2_ST


        if 4 in phases:
            pbf4 = DYN0
            def abf4(n):
                nonlocal pbf4
                o = pbf4; pbf4 += n
                assert pbf4 <= NBF
                return o
            o_attb = abf4(2 * DC * 512)
            o_wo = abf4(2 * DC * 512)
            o_xb = abf4(2 * 4 * D)
            o_r1 = abf4(4 * D)
            o_x1t = abf4(D)
            o_x1f = abf4(2 * D)
            ATTB = [BP[:, o_attb + i * DC * 512:o_attb + (i + 1) * DC * 512] for i in range(2)]
            WO = [BP[:, o_wo + i * DC * 512:o_wo + (i + 1) * DC * 512] for i in range(2)]
            XB = [BP[:, o_xb + i * 4 * D:o_xb + (i + 1) * 4 * D] for i in range(2)]
            R1 = [BP[:, o_r1 + tt * D:o_r1 + (tt + 1) * D] for tt in range(4)]
            X1T = BP[:, o_x1t:o_x1t + D]
            X1F = [BP[:, o_x1f + i * D:o_x1f + (i + 1) * D] for i in range(2)]
            COLS = FP[:, o_cols:o_cols + 32]

            wait("sync", "TM", ph3_TM)
            wait("sync", "AM", ph3_AM)
            wait("sync", "VM", ph3_VM)

            attb_rel = [0, 0]
            wo_rel = [0, 0]
            xb_rel = [0, 0]
            bank_rel4 = [{"AM": C["AM"], "VM": C["VM"]} for _ in range(4)]
            x1f_rel = [0, 0]
            psT_rel4 = [C["AM"], C["AM"]]
            x1t_rel = C["TM"]
            r1_rel = C["AM"]
            cols_rel = [C["VM"], C["VM"]]

            for tb in range(8):
                bb = tb % 2
                wait("sync", "TM", attb_rel[bb])
                dma("sync", ATTB[bb].rearrange("p (dc j) -> p dc j", dc=DC),
                    atts_d[:, :, tb * 512:(tb + 1) * 512].rearrange("dc p j -> p dc j"), "LD")
                wait("sync", "VM", xb_rel[bb])
                dma("sync", XB[bb].rearrange("p (tt j) -> p tt j", tt=4),
                    x_d[tb * 512:(tb + 1) * 512, :].rearrange("(tt p) j -> p tt j", p=128), "LD")
                ld_tb = C["LD"]
                for db in range(4):
                    wb = db % 2
                    wait("sync", "TM", wo_rel[wb])
                    dma("sync", WO[wb].rearrange("p (dc j) -> p dc j", dc=DC),
                        wout_d[:, db * 512:(db + 1) * 512].rearrange("(dc p) j -> p dc j", p=128), "LD")
                    ld_w = C["LD"]
                    wait("tensor", "LD", max(ld_tb, ld_w))
                    for tt in range(4):
                        b = (tb * 16 + db * 4 + tt) % 4
                        wait("tensor", "AM", bank_rel4[b]["AM"])
                        wait("tensor", "VM", bank_rel4[b]["VM"])
                        for k in range(DC):
                            op("tensor", lambda t, b=b, k=k, bb=bb, wb=wb, tt=tt:
                               t.matmul(psA[:, b * 512:(b + 1) * 512],
                                        ATTB[bb][:, k * 512 + tt * 128: k * 512 + (tt + 1) * 128],
                                        WO[wb][:, k * 512:(k + 1) * 512],
                                        start=(k == 0), stop=False))
                        op("tensor", lambda t, b=b, db=db:
                           t.matmul(psA[:, b * 512:(b + 1) * 512],
                                    ONES[0:1, 0:128], BOUT[0:1, db * 512:(db + 1) * 512],
                                    start=False, stop=True), "TM")
                        wait("vector", "TM", C["TM"])
                        wait("vector", "AM", r1_rel)
                        op("vector", lambda v, b=b, bb=bb, tt=tt, db=db:
                           v.tensor_add(R1[tt][:, db * 512:(db + 1) * 512],
                                        psA[:, b * 512:(b + 1) * 512],
                                        XB[bb][:, tt * D + db * 512: tt * D + (db + 1) * 512]), "VM")
                        bank_rel4[b] = {"AM": C["AM"], "VM": C["VM"]}
                    wo_rel[wb] = C["TM"]
                attb_rel[bb] = C["TM"]
                xb_rel[bb] = C["VM"]
                vm_r1 = C["VM"]
                for tt in range(4):
                    par = tt % 2
                    cs = par * 16
                    S1 = COLS[:, cs + 0:cs + 1]; S2 = COLS[:, cs + 1:cs + 2]
                    M = COLS[:, cs + 2:cs + 3]; MS = COLS[:, cs + 3:cs + 4]
                    M2 = COLS[:, cs + 4:cs + 5]; VAR = COLS[:, cs + 5:cs + 6]
                    SD = COLS[:, cs + 6:cs + 7]; RSTD = COLS[:, cs + 7:cs + 8]
                    NMR = COLS[:, cs + 8:cs + 9]
                    wait("vector", "VM", cols_rel[par])
                    op("vector", lambda v, tt=tt, S1=S1: v.reduce_sum(S1, R1[tt][:, :], axis=AX.X), "VM")
                    wait("scalar", "VM", C["VM"])
                    op("scalar", lambda sC, tt=tt:
                       sC.activation(SQ[:, :], R1[tt][:, :], AF.Square), "AM")
                    op("vector", lambda v, S1=S1, M=M: v.tensor_scalar(M, S1, 1.0 / D, None, ALU.mult), "VM")
                    wait("vector", "AM", C["AM"])
                    op("vector", lambda v, S2=S2: v.reduce_sum(S2, SQ[:, :], axis=AX.X), "VM")
                    op("vector", lambda v, S2=S2, MS=MS: v.tensor_scalar(MS, S2, 1.0 / D, None, ALU.mult), "VM")
                    op("vector", lambda v, M=M, M2=M2: v.tensor_mul(M2, M, M), "VM")
                    op("vector", lambda v, MS=MS, M2=M2, VAR=VAR: v.tensor_sub(VAR, MS, M2), "VM")
                    op("vector", lambda v, VAR=VAR: v.tensor_scalar(VAR, VAR, EPS, None, ALU.add), "VM")
                    wait("scalar", "VM", C["VM"])
                    op("scalar", lambda sC, VAR=VAR, SD=SD:
                       sC.activation(SD, VAR, AF.Sqrt), "AM")
                    wait("vector", "AM", C["AM"])
                    op("vector", lambda v, SD=SD, RSTD=RSTD: v.reciprocal(RSTD, SD), "VM")
                    op("vector", lambda v, M=M, RSTD=RSTD, NMR=NMR:
                       v.scalar_tensor_tensor(NMR, M, -1.0, RSTD, ALU.mult, ALU.mult), "VM")
                    wait("vector", "TM", x1t_rel)
                    op("vector", lambda v, tt=tt, RSTD=RSTD, NMR=NMR:
                       v.tensor_scalar(X1T[:, :], R1[tt][:, :], RSTD[:, :1], NMR[:, :1],
                                       ALU.mult, ALU.add), "VM")
                    cols_rel[par] = C["VM"]
                    vm_x1t = C["VM"]
                    xf = tt % 2
                    wait("tensor", "VM", vm_x1t)
                    wait("tensor", "ST", x1f_rel[xf])
                    for dc in range(DC):
                        bkk = (dc // 8) % 2
                        if dc % 8 == 0:
                            wait("tensor", "AM", psT_rel4[bkk])
                        op("tensor", lambda t, dc=dc, bkk=bkk:
                           t.transpose(psT[bkk][:, (dc % 8) * 128:(dc % 8 + 1) * 128],
                                       X1T[:, dc * 128:(dc + 1) * 128], EYE),
                           "TM" if dc % 8 == 7 else None)
                        if dc % 8 == 7:
                            wait("scalar", "TM", C["TM"])
                            op("scalar", lambda sC, dc=dc, bkk=bkk, xf=xf:
                               sC.copy(X1F[xf][:, (dc - 7) * 128:(dc + 1) * 128],
                                       psT[bkk][:, :]), "AM")
                            psT_rel4[bkk] = C["AM"]
                    x1t_rel = C["TM"]
                    wait("sync", "AM", C["AM"])
                    dma("sync", x1s_d[:, :, tb * 512 + tt * 128: tb * 512 + (tt + 1) * 128]
                        .rearrange("dc p j -> p dc j"),
                        X1F[xf].rearrange("p (dc j) -> p dc j", dc=DC), "ST")
                    x1f_rel[xf] = C["ST"]
                r1_rel = C["AM"]
            ph4_TM, ph4_AM, ph4_VM, ph4_ST = C["TM"], C["AM"], C["VM"], C["ST"]
        else:
            ph4_TM, ph4_AM, ph4_VM, ph4_ST = ph3_TM, ph3_AM, ph3_VM, ph3_ST

        if 5 in phases:
            pbf5 = DYN0
            def abf5(n):
                nonlocal pbf5
                o = pbf5; pbf5 += n
                assert pbf5 <= NBF, f"ph5 bf16 overflow {pbf5}"
                return o
            o_fuse = abf5(JC * 512)
            o_x1 = abf5(DC * 512)
            o_wg = abf5(2 * DC * 128)
            o_wu = abf5(2 * DC * 128)
            o_wd = abf5(2 * JC * 128)
            o_gt = abf5(512)
            o_ut = abf5(512)
            FUSE = BP[:, o_fuse:o_fuse + JC * 512]
            X1 = BP[:, o_x1:o_x1 + DC * 512]
            WG = [BP[:, o_wg + i * DC * 128:o_wg + (i + 1) * DC * 128] for i in range(2)]
            WU = [BP[:, o_wu + i * DC * 128:o_wu + (i + 1) * DC * 128] for i in range(2)]
            WD = [BP[:, o_wd + i * JC * 128:o_wd + (i + 1) * JC * 128] for i in range(2)]
            GT = BP[:, o_gt:o_gt + 512]
            UT = BP[:, o_ut:o_ut + 512]
            pf5 = FDYN0
            def af5(n):
                nonlocal pf5
                o = pf5; pf5 += n
                assert pf5 <= NF32, f"ph5 f32 overflow {pf5}"
                return o
            o_r2 = af5(2 * 512)
            o_dt = af5(2 * 512)
            o_sqf = af5(2 * 512)
            o_s1b = af5(512)
            o_s2b = af5(512)
            o_tmp = af5(512)
            o_tmp2 = af5(512)
            o_ostg = af5(2 * 512)
            R2 = [FP[:, o_r2 + i * 512:o_r2 + (i + 1) * 512] for i in range(2)]
            DT = [FP[:, o_dt + i * 512:o_dt + (i + 1) * 512] for i in range(2)]
            SQF = [FP[:, o_sqf + i * 512:o_sqf + (i + 1) * 512] for i in range(2)]
            S1B = FP[:, o_s1b:o_s1b + 512]
            S2B = FP[:, o_s2b:o_s2b + 512]
            TMP = FP[:, o_tmp:o_tmp + 512]
            TMP2 = FP[:, o_tmp2:o_tmp2 + 512]
            OSTG = [FP[:, o_ostg + i * 512:o_ostg + (i + 1) * 512] for i in range(2)]
            PRMC = FP[:, o_prmc:o_prmc + 64]

            wait("sync", "TM", ph4_TM)
            wait("sync", "AM", ph4_AM)
            wait("sync", "VM", ph4_VM)
            wait("gpsimd", "TM", ph4_TM)
            wait("gpsimd", "AM", ph4_AM)
            wait("gpsimd", "VM", ph4_VM)

            wg_rel = [0, 0]
            wd_rel = [0, 0]
            gbank_rel = [C["AM"]] * 4
            fuse_rel = 0
            x1_rel_tm = 0
            x1_rel_vm = 0
            x1_rel_am = 0
            dn_evac = [C["AM"], C["AM"]]
            r2_rel_tm = [C["TM"], C["TM"]]
            r2_rel_am = [C["AM"], C["AM"]]
            dt_rel = [C["VM"], C["VM"]]
            sqf_rel = [C["TM"], C["TM"]]
            trb_rel = [C["AM"], C["AM"]]
            s12_rel = [C["VM"], C["VM"]]
            ostg_rel = [0, 0]
            prm_rel = 0

            for sl in range(8):
                wait("sync", "ST", ph4_ST)
                wait("sync", "TM", x1_rel_tm)
                wait("sync", "VM", x1_rel_vm)
                dma("sync", X1.rearrange("p (dc j) -> p dc j", dc=DC),
                    x1s_d[:, :, sl * 512:(sl + 1) * 512].rearrange("dc p j -> p dc j"), "LD")
                ld_x1 = C["LD"]
                for j in range(JC):
                    wb = j % 2
                    wait("gpsimd", "TM", wg_rel[wb])
                    dma("gpsimd", WG[wb].rearrange("p (dc i) -> p dc i", dc=DC),
                        wg_d[:, j * 128:(j + 1) * 128].rearrange("(dc p) i -> p dc i", p=128), "LG")
                    dma("gpsimd", WU[wb].rearrange("p (dc i) -> p dc i", dc=DC),
                        wu_d[:, j * 128:(j + 1) * 128].rearrange("(dc p) i -> p dc i", p=128), "LG")
                    ld_w = C["LG"]
                    bg = (2 * j) % 4
                    bu = (2 * j + 1) % 4
                    wait("tensor", "LD", ld_x1)
                    wait("tensor", "LG", ld_w)
                    wait("tensor", "AM", max(gbank_rel[bg], gbank_rel[bu]))
                    for k in range(DC):
                        op("tensor", lambda t, bg=bg, k=k, wb=wb:
                           t.matmul(psA[:, bg * 512:(bg + 1) * 512],
                                    WG[wb][:, k * 128:(k + 1) * 128],
                                    X1[:, k * 512:(k + 1) * 512],
                                    start=(k == 0), stop=(k == DC - 1)),
                           "TM" if k == DC - 1 else None)
                    for k in range(DC):
                        op("tensor", lambda t, bu=bu, k=k, wb=wb:
                           t.matmul(psA[:, bu * 512:(bu + 1) * 512],
                                    WU[wb][:, k * 128:(k + 1) * 128],
                                    X1[:, k * 512:(k + 1) * 512],
                                    start=(k == 0), stop=(k == DC - 1)),
                           "TM" if k == DC - 1 else None)
                    wg_rel[wb] = C["TM"]
                    wait("scalar", "TM", C["TM"])
                    wait("scalar", "VM", C["VM"])
                    op("scalar", lambda sC, bg=bg, j=j:
                       sC.activation(GT[:, :], psA[:, bg * 512:(bg + 1) * 512], AF.Silu,
                                     scale=SGT[:, j:j + 1]), "AM")
                    gbank_rel[bg] = C["AM"]
                    op("scalar", lambda sC, bu=bu, j=j:
                       sC.activation(UT[:, :], psA[:, bu * 512:(bu + 1) * 512], AF.Copy,
                                     scale=SUT[:, j:j + 1]), "AM")
                    gbank_rel[bu] = C["AM"]
                    wait("vector", "AM", C["AM"])
                    wait("vector", "TM", fuse_rel)
                    op("vector", lambda v, j=j:
                       v.tensor_mul(FUSE[:, j * 512:(j + 1) * 512], GT[:, :], UT[:, :]), "VM")
                vm_fuse = C["VM"]
                for dout in range(DC):
                    wb = dout % 2
                    wait("gpsimd", "TM", wd_rel[wb])
                    dma("gpsimd", WD[wb].rearrange("p (jc i) -> p jc i", jc=JC),
                        wd_d[:, dout * 128:(dout + 1) * 128].rearrange("(jc p) i -> p jc i", p=128), "LG")
                    ld_w = C["LG"]
                    dnb = dout % 2
                    ps_dn = psB if dnb == 0 else psC
                    wait("tensor", "LG", ld_w)
                    wait("tensor", "VM", vm_fuse)
                    wait("tensor", "AM", dn_evac[dnb])
                    for jc in range(JC):
                        op("tensor", lambda t, jc=jc, wb=wb, ps_dn=ps_dn:
                           t.matmul(ps_dn[:, :], WD[wb][:, jc * 128:(jc + 1) * 128],
                                    FUSE[:, jc * 512:(jc + 1) * 512],
                                    start=(jc == 0), stop=(jc == JC - 1)),
                           "TM" if jc == JC - 1 else None)
                    wd_rel[wb] = C["TM"]
                    dn_done = C["TM"]
                    wait("scalar", "TM", dn_done)
                    wait("scalar", "VM", dt_rel[dnb])
                    op("scalar", lambda sC, dnb=dnb, dout=dout, ps_dn=ps_dn:
                       sC.activation(DT[dnb][:, :], ps_dn[:, :], AF.Copy,
                                     scale=SDT[:, dout:dout + 1]), "AM")
                    dn_evac[dnb] = C["AM"]
                    wait("vector", "AM", dn_evac[dnb])
                    wait("vector", "TM", r2_rel_tm[dnb])
                    wait("vector", "AM", r2_rel_am[dnb])
                    op("vector", lambda v, dnb=dnb, dout=dout:
                       v.tensor_add(R2[dnb][:, :], DT[dnb][:, :], X1[:, dout * 512:(dout + 1) * 512]), "VM")
                    dt_rel[dnb] = C["VM"]
                    vm_r2 = C["VM"]
                    wait("scalar", "VM", vm_r2)
                    wait("scalar", "TM", sqf_rel[dnb])
                    op("scalar", lambda sC, dnb=dnb:
                       sC.activation(SQF[dnb][:, :], R2[dnb][:, :], AF.Square), "AM")
                    am_sq = C["AM"]
                    wait("tensor", "VM", vm_r2)
                    wait("tensor", "VM", s12_rel[0])
                    op("tensor", lambda t, dnb=dnb:
                       t.matmul(psA[:, 2 * 512:3 * 512], ONESF, R2[dnb][:, :],
                                start=True, stop=True), "TM")
                    tm_s1 = C["TM"]
                    wait("tensor", "AM", am_sq)
                    wait("tensor", "VM", s12_rel[1])
                    op("tensor", lambda t, dnb=dnb:
                       t.matmul(psA[:, 3 * 512:4 * 512], ONESF, SQF[dnb][:, :],
                                start=True, stop=True), "TM")
                    tm_s2 = C["TM"]
                    sqf_rel[dnb] = tm_s2
                    wait("vector", "TM", tm_s1)
                    if dout == 0:
                        op("vector", lambda v: v.tensor_copy(S1B[:, :], psA[:, 2 * 512:3 * 512]), "VM")
                    else:
                        op("vector", lambda v: v.tensor_add(S1B[:, :], S1B[:, :], psA[:, 2 * 512:3 * 512]), "VM")
                    s12_rel[0] = C["VM"]
                    wait("vector", "TM", tm_s2)
                    if dout == 0:
                        op("vector", lambda v: v.tensor_copy(S2B[:, :], psA[:, 3 * 512:4 * 512]), "VM")
                    else:
                        op("vector", lambda v: v.tensor_add(S2B[:, :], S2B[:, :], psA[:, 3 * 512:4 * 512]), "VM")
                    s12_rel[1] = C["VM"]
                    bt = dout % 2
                    wait("tensor", "VM", vm_r2)
                    wait("tensor", "AM", trb_rel[bt])
                    for tt in range(4):
                        op("tensor", lambda t, tt=tt, dnb=dnb, bt=bt:
                           t.transpose(psA[:, bt * 512 + tt * 128: bt * 512 + (tt + 1) * 128],
                                       R2[dnb][:, tt * 128:(tt + 1) * 128], EYEF),
                           "TM" if tt == 3 else None)
                    r2_rel_tm[dnb] = C["TM"]
                    r2_rel_am[dnb] = am_sq
                    ob = dout % 2
                    wait("scalar", "TM", C["TM"])
                    wait("scalar", "ST", ostg_rel[ob])
                    op("scalar", lambda sC, bt=bt, ob=ob:
                       sC.copy(OSTG[ob][:, :], psA[:, bt * 512:(bt + 1) * 512]), "AM")
                    trb_rel[bt] = C["AM"]
                    wait("sync", "AM", C["AM"])
                    dma("sync",
                        r2s_d[sl * 512:(sl + 1) * 512, dout * 128:(dout + 1) * 128]
                        .rearrange("(tt p) j -> p tt j", p=128),
                        OSTG[ob].rearrange("p (tt j) -> p tt j", tt=4), "ST")
                    ostg_rel[ob] = C["ST"]
                if debug:
                    wait("sync", "VM", C["VM"])
                    dma("sync", s1b_d[sl], S1B[:, :], "ST")
                    dma("sync", s2b_d[sl], S2B[:, :], "ST")
                    wait("vector", "ST", C["ST"])
                wait("vector", "AM", prm_rel)
                op("vector", lambda v: v.tensor_scalar(S1B[:, :], S1B[:, :], 1.0 / D, None, ALU.mult), "VM")
                op("vector", lambda v: v.tensor_scalar(S2B[:, :], S2B[:, :], 1.0 / D, None, ALU.mult), "VM")
                op("vector", lambda v: v.tensor_mul(TMP[:, :], S1B[:, :], S1B[:, :]), "VM")
                op("vector", lambda v: v.tensor_sub(S2B[:, :], S2B[:, :], TMP[:, :]), "VM")
                op("vector", lambda v: v.tensor_scalar(S2B[:, :], S2B[:, :], EPS, None, ALU.add), "VM")
                wait("scalar", "VM", C["VM"])
                op("scalar", lambda sC: sC.activation(TMP[:, :], S2B[:, :], AF.Sqrt), "AM")
                wait("vector", "AM", C["AM"])
                op("vector", lambda v: v.reciprocal(TMP2[:, :], TMP[:, :]), "VM")
                op("vector", lambda v:
                   v.scalar_tensor_tensor(S1B[:, :], S1B[:, :], -1.0, TMP2[:, :],
                                          ALU.mult, ALU.mult), "VM")
                vm_prm = C["VM"]
                wait("tensor", "VM", vm_prm)
                wait("tensor", "AM", trb_rel[0])
                for tt in range(4):
                    gt_ = sl * 4 + tt
                    op("tensor", lambda t, tt=tt:
                       t.transpose(psA[:, tt * 2:tt * 2 + 1], TMP2[0:1, tt * 128:(tt + 1) * 128],
                                   EYEF[:1, :1]), None)
                    op("tensor", lambda t, tt=tt:
                       t.transpose(psA[:, tt * 2 + 1:tt * 2 + 2], S1B[0:1, tt * 128:(tt + 1) * 128],
                                   EYEF[:1, :1]),
                       "TM" if tt == 3 else None)
                tm_prm = C["TM"]
                wait("scalar", "TM", tm_prm)
                for tt in range(4):
                    gt_ = sl * 4 + tt
                    op("scalar", lambda sC, tt=tt, gt_=gt_:
                       sC.copy(PRMC[:, gt_:gt_ + 1], psA[:, tt * 2:tt * 2 + 1]), None)
                    op("scalar", lambda sC, tt=tt, gt_=gt_:
                       sC.copy(PRMC[:, 32 + gt_:32 + gt_ + 1], psA[:, tt * 2 + 1:tt * 2 + 2]),
                       "AM" if tt == 3 else None)
                prm_rel = C["AM"]
                trb_rel[0] = C["AM"]
                gbank_rel = [C["AM"]] * 4
                fuse_rel = C["TM"]
                x1_rel_tm = C["TM"]
                x1_rel_vm = C["VM"]
            ph5_TM, ph5_AM, ph5_VM, ph5_ST = C["TM"], C["AM"], C["VM"], C["ST"]

            pf6 = FDYN0
            def af6(n):
                nonlocal pf6
                o = pf6; pf6 += n
                assert pf6 <= NF32
                return o
            o_rb = af6(2 * 2048)
            o_ob = af6(2 * 2048)
            RB = [FP[:, o_rb + i * 2048:o_rb + (i + 1) * 2048] for i in range(2)]
            OB = [FP[:, o_ob + i * 2048:o_ob + (i + 1) * 2048] for i in range(2)]
            wait("sync", "TM", ph5_TM)
            wait("sync", "AM", ph5_AM)
            wait("sync", "VM", ph5_VM)
            rb_rel = [0, 0]
            ob_rel = [0, 0]
            for rb_i in range(T // 128):
                bi = rb_i % 2
                wait("sync", "VM", rb_rel[bi])
                dma("sync", RB[bi][:, :], r2s_d[rb_i * 128:(rb_i + 1) * 128, :], "LD")
                ld_rb = C["LD"]
                wait("vector", "LD", ld_rb)
                wait("vector", "ST", ob_rel[bi])
                op("vector", lambda v, bi=bi, rb_i=rb_i:
                   v.tensor_scalar(OB[bi][:, :], RB[bi][:, :], PRMC[:, rb_i:rb_i + 1],
                                   PRMC[:, 32 + rb_i:32 + rb_i + 1],
                                   ALU.mult, ALU.add), "VM")
                rb_rel[bi] = C["VM"]
                wait("sync", "VM", C["VM"])
                dma("sync", out_d[rb_i * 128:(rb_i + 1) * 128, :], OB[bi][:, :], "ST")
                ob_rel[bi] = C["ST"]

        if debug and 5 in phases:
            wait("sync", "AM", C["AM"])
            dma("sync", prm_d[:, :], FP[:, o_prmc:o_prmc + 64], "ST")
        wait("sync", "ST", C["ST"])
        wait("sync", "LG", C["LG"])

        @block.sync
        def _(eng):
            for f in Q["sync"]:
                f(eng)

        @block.gpsimd
        def _(eng):
            for f in Q["gpsimd"]:
                f(eng)

        @block.tensor
        def _(eng):
            for f in Q["tensor"]:
                f(eng)

        @block.vector
        def _(eng):
            for f in Q["vector"]:
                f(eng)

        @block.scalar
        def _(eng):
            for f in Q["scalar"]:
                f(eng)

    return nc


def _causal_mask():
    m = np.zeros((128, 2048), dtype=np.float32)
    p = np.arange(128)[:, None]
    for r in range(4):
        j = np.arange(512)[None, :]
        m[:, r * 512:(r + 1) * 512] = np.where(j > r * 128 + p, -1e30, 0.0)
    return m.astype(ml_dtypes.bfloat16)


def make_core_inputs(core, inputs):
    X = np.asarray(inputs["L0_attn_x"])
    x_c = np.ascontiguousarray(X[2 * core:2 * core + 2].reshape(T, D))
    return {
        "x": x_c,
        "wqkv": np.ascontiguousarray(inputs["L0_attn_W_qkv"]),
        "bqkv": np.ascontiguousarray(np.asarray(inputs["L0_attn_b_qkv"]).reshape(1, TD)),
        "wout": np.ascontiguousarray(inputs["L0_attn_W_out"]),
        "bout": np.ascontiguousarray(np.asarray(inputs["L0_attn_b_out"]).reshape(1, D)),
        "wg": np.ascontiguousarray(inputs["L0_ffn_w_gate"]),
        "wu": np.ascontiguousarray(inputs["L0_ffn_w_up"]),
        "wd": np.ascontiguousarray(inputs["L0_ffn_w_down"]),
        "sg": np.ascontiguousarray(np.asarray(inputs["L0_ffn_s_gate"], dtype=np.float32).reshape(JC, 128)),
        "su": np.ascontiguousarray(np.asarray(inputs["L0_ffn_s_up"], dtype=np.float32).reshape(JC, 128)),
        "sd": np.ascontiguousarray(np.asarray(inputs["L0_ffn_s_down"], dtype=np.float32).reshape(DC, 128)),
        "eye": np.eye(128, dtype=ml_dtypes.bfloat16),
        "eyef": np.eye(128, dtype=np.float32),
        "onesf": np.ones((128, 128), dtype=np.float32),
        "mask": _causal_mask(),
        "ones": np.ones((128, 512), dtype=ml_dtypes.bfloat16),
    }


def kernel(**inputs):
    nc = build_core_program()
    in_maps = [make_core_inputs(c, inputs) for c in range(NCORES)]
    res = run_bass_kernel_spmd(nc, in_maps, core_ids=list(range(NCORES)))
    out = np.concatenate([r["out"].reshape(NSEQ, SEQ, D) for r in res.results], axis=0)
    return out.astype(np.float32)


# version 04
# speedup vs baseline: 1.0185x; 1.0171x over previous
import numpy as np
import ml_dtypes
import concourse.bass as bass
import concourse.mybir as mybir
from concourse.bass_utils import run_bass_kernel_spmd
from contextlib import ExitStack

BF16 = mybir.dt.bfloat16
F32 = mybir.dt.float32
I8 = mybir.dt.int8
AF = mybir.ActivationFunctionType
ALU = mybir.AluOpType
AX = mybir.AxisListType

NCORES = 8
SEQ = 2048
NSEQ = 2
T = SEQ * NSEQ
D = 2048
DC = D // 128
H = 16
TD = 3 * D
INTER = 8192
JC = INTER // 128
NHALF = 2
THALF = T // NHALF
INV_SQRT_HD = 1.0 / float(np.float32(ml_dtypes.bfloat16(np.sqrt(128.0))))
EPS = 1e-5


def build_core_program(phases=(0, 2, 3, 4, 5), debug=False):
    nc = bass.Bass("TRN2", target_bir_lowering=False)

    x_d = nc.dram_tensor("x", [T, D], BF16, kind="ExternalInput")
    wqkv_d = nc.dram_tensor("wqkv", [D, TD], BF16, kind="ExternalInput")
    bqkv_d = nc.dram_tensor("bqkv", [1, TD], BF16, kind="ExternalInput")
    wout_d = nc.dram_tensor("wout", [D, D], BF16, kind="ExternalInput")
    bout_d = nc.dram_tensor("bout", [1, D], BF16, kind="ExternalInput")
    wg_d = nc.dram_tensor("wg", [D, INTER], I8, kind="ExternalInput")
    wu_d = nc.dram_tensor("wu", [D, INTER], I8, kind="ExternalInput")
    wd_d = nc.dram_tensor("wd", [INTER, D], I8, kind="ExternalInput")
    sg_d = nc.dram_tensor("sg", [JC, 128], F32, kind="ExternalInput")
    su_d = nc.dram_tensor("su", [JC, 128], F32, kind="ExternalInput")
    sd_d = nc.dram_tensor("sd", [DC, 128], F32, kind="ExternalInput")
    eye_d = nc.dram_tensor("eye", [128, 128], BF16, kind="ExternalInput")
    eyef_d = nc.dram_tensor("eyef", [128, 128], F32, kind="ExternalInput")
    mask_d = nc.dram_tensor("mask", [128, 2048], BF16, kind="ExternalInput")
    ones_d = nc.dram_tensor("ones", [128, 512], BF16, kind="ExternalInput")
    onesf_d = nc.dram_tensor("onesf", [128, 128], F32, kind="ExternalInput")

    out_d = nc.dram_tensor("out", [T, D], F32, kind="ExternalOutput")
    scratch_kind = "ExternalOutput" if debug else "Internal"
    qts_d = nc.dram_tensor("qts", [DC, 128, T], BF16, kind=scratch_kind)
    kts_d = nc.dram_tensor("kts", [DC, 128, T], BF16, kind=scratch_kind)
    vs_d = nc.dram_tensor("vs", [T, D], BF16, kind=scratch_kind)
    atts_d = nc.dram_tensor("atts", [DC, 128, T], BF16, kind=scratch_kind)
    x1s_d = nc.dram_tensor("x1s", [DC, 128, T], BF16, kind=scratch_kind)
    r2s_d = nc.dram_tensor("r2s", [T, D], F32, kind=scratch_kind)
    prm_d = nc.dram_tensor("prmdbg", [128, 64], F32, kind=scratch_kind) if debug else None
    s1b_d = nc.dram_tensor("s1bdbg", [8, 128, 512], F32, kind=scratch_kind) if debug else None
    s2b_d = nc.dram_tensor("s2bdbg", [8, 128, 512], F32, kind=scratch_kind) if debug else None

    ENGS = ("sync", "gpsimd", "tensor", "vector", "scalar")
    Q = {k: [] for k in ENGS}
    C = {"LD": 0, "LG": 0, "ST": 0, "TM": 0, "VM": 0, "AM": 0, "GM": 0}
    sems = {}

    def em(qname, fn):
        Q[qname].append(fn)

    def wait(qname, sem, val):
        if val <= 0:
            return
        em(qname, lambda e, s=sem, v=val: e.wait_ge(sems[s], v))

    def dma(qname, dst, src, sem):
        C[sem] += 16
        em(qname, lambda e, d=dst, s=src, sm=sem: e.dma_start(d, s).then_inc(sems[sm], 16))

    def dma_t(qname, dst, src, sem):
        C[sem] += 16
        em(qname, lambda e, d=dst, s=src, sm=sem: e.dma_start_transpose(d, s).then_inc(sems[sm], 16))

    def op(qname, f, sem=None):
        if sem is None:
            em(qname, lambda e, ff=f: ff(e))
        elif qname in ("scalar", "vector", "gpsimd"):
            C[sem] += 1
            em(qname, lambda e, ff=f: ff(e))
            em(qname, lambda e, sm=sem: e.drain().then_inc(sems[sm], 1))
        else:
            C[sem] += 1
            em(qname, lambda e, ff=f, sm=sem: ff(e).then_inc(sems[sm], 1))

    NBF = 78500
    NF32 = 12500

    with ExitStack() as ctx:
        e = ctx.enter_context
        BP = e(nc.sbuf_tensor("bp", [128, NBF], BF16))
        FP = e(nc.sbuf_tensor("fp", [128, NF32], F32))
        psA = e(nc.psum_tensor("psA", [128, 2048], F32))
        psB = e(nc.psum_tensor("psB", [128, 512], F32))
        psC = e(nc.psum_tensor("psC", [128, 512], F32))
        psT0 = e(nc.psum_tensor("psT0", [128, 1024], BF16))
        psT1 = e(nc.psum_tensor("psT1", [128, 1024], BF16))
        psT = [psT0, psT1]
        for s in ("LD", "LG", "ST", "TM", "VM", "AM", "GM"):
            sems[s] = e(nc.semaphore(s))
        block = e(nc.Block())

        pbf = 0
        def abf(n):
            nonlocal pbf
            o = pbf; pbf += n
            assert pbf <= NBF, f"bf16 pool overflow {pbf}"
            return o
        pf = 0
        def af32(n):
            nonlocal pf
            o = pf; pf += n
            assert pf <= NF32, f"f32 pool overflow {pf}"
            return o

        o_eye = abf(128)
        o_mask = abf(2048)
        o_ones = abf(512)
        o_bqkv = abf(TD)
        o_bout = abf(D)
        DYN0 = pbf

        o_eyef = af32(128)
        o_onesf = af32(128)
        o_sgT = af32(JC)
        o_suT = af32(JC)
        o_sdT = af32(DC)
        o_sstage = af32(128)
        o_cols = af32(32)
        o_sq = af32(2048)
        o_nm = af32(8)
        o_nm4 = af32(16)
        o_prmc = af32(64)
        o_srow = af32(1024)
        o_rsb = af32(512)
        FDYN0 = pf

        EYE = BP[:, o_eye:o_eye + 128]
        MASK = BP[:, o_mask:o_mask + 2048]
        ONES = BP[:, o_ones:o_ones + 512]
        BQKV = BP[:, o_bqkv:o_bqkv + TD]
        BOUT = BP[:, o_bout:o_bout + D]
        EYEF = FP[:, o_eyef:o_eyef + 128]
        ONESF = FP[:, o_onesf:o_onesf + 128]
        SGT = FP[:, o_sgT:o_sgT + JC]
        SUT = FP[:, o_suT:o_suT + JC]
        SDT = FP[:, o_sdT:o_sdT + DC]
        SQ = FP[:, o_sq:o_sq + 2048]

        if 0 in phases:
            dma("sync", EYE, eye_d[:, :], "LD")
            dma("sync", MASK, mask_d[:, :], "LD")
            dma("sync", ONES[:, :], ones_d[:, :], "LD")
            dma("sync", BQKV[0:1, :], bqkv_d[:, :], "LD")
            dma("sync", BOUT[0:1, :], bout_d[:, :], "LD")
            dma("sync", EYEF, eyef_d[:, :], "LD")
            dma("sync", ONESF, onesf_d[:, :], "LD")
            sst = FP[:, o_sstage:o_sstage + 128]
            for (src_d, rows, dstap) in ((sg_d, JC, SGT), (su_d, JC, SUT), (sd_d, DC, SDT)):
                dma("sync", sst[:rows, :], src_d[:, :], "LD")
                wait("tensor", "LD", C["LD"])
                op("tensor", lambda t, r=rows: t.transpose(psB[:, 0:r], sst[:r, :], EYEF[:r, :r]), "TM")
                wait("scalar", "TM", C["TM"])
                op("scalar", lambda s, r=rows, dd=dstap: s.copy(dd[:, 0:r], psB[:, 0:r]), "AM")
                wait("tensor", "AM", C["AM"])
                wait("sync", "AM", C["AM"])

        if 2 in phases:
            pbf2 = DYN0
            def abf2(n):
                nonlocal pbf2
                o = pbf2; pbf2 += n
                assert pbf2 <= NBF, f"ph2 bf16 overflow {pbf2}"
                return o
            o_xt = abf2(DC * THALF)
            o_wqk = abf2(2 * DC * 128)
            o_wv = abf2(2 * DC * 512)
            o_qkst = abf2(2 * THALF)
            o_vst = abf2(2 * 512)
            XT = [BP[:, o_xt + dc * THALF: o_xt + (dc + 1) * THALF] for dc in range(DC)]
            WQK = [BP[:, o_wqk + i * DC * 128: o_wqk + (i + 1) * DC * 128] for i in range(2)]
            WVb = [BP[:, o_wv + i * DC * 512:o_wv + (i + 1) * DC * 512] for i in range(2)]
            QKST = [BP[:, o_qkst + i * THALF: o_qkst + (i + 1) * THALF] for i in range(2)]
            VST = [BP[:, o_vst + i * 512: o_vst + (i + 1) * 512] for i in range(2)]

            bank_rel = [{"AM": C["AM"], "VM": C["VM"]} for _ in range(4)]
            ech = 0
            qk_store_rel = [0, 0]
            v_store_rel = [0, 0]
            wqk_rel = [0, 0]
            wv_rel = [0, 0]
            xt_rel = 0

            for hf in range(NHALF):
                wait("sync", "TM", xt_rel)
                for dc in range(DC):
                    dma_t("sync", XT[dc], x_d[hf * THALF:(hf + 1) * THALF, dc * 128:(dc + 1) * 128], "LD")
                ld_xt = C["LD"]

                def load_wqk(f2):
                    wb2 = f2 % 2
                    wait("sync", "TM", wqk_rel[wb2])
                    dma("sync", WQK[wb2].rearrange("p (dc j) -> p dc j", dc=DC),
                        wqkv_d[:, f2 * 128:(f2 + 1) * 128].rearrange("(dc p) j -> p dc j", p=128), "LD")
                    return C["LD"]
                ld_wqk = [0] * (2 * DC)
                ld_wqk[0] = load_wqk(0)
                for f in range(2 * DC):
                    wb = f % 2
                    if f + 1 < 2 * DC:
                        ld_wqk[f + 1] = load_wqk(f + 1)
                    ld_w = ld_wqk[f]
                    qb = f % 2
                    wait("scalar", "ST", qk_store_rel[qb])
                    wait("vector", "ST", qk_store_rel[qb])
                    wait("tensor", "LD", max(ld_xt, ld_w))
                    for tb in range(THALF // 512):
                        b = (hf * 144 + f * 4 + tb) % 4
                        wait("tensor", "AM", bank_rel[b]["AM"])
                        wait("tensor", "VM", bank_rel[b]["VM"])
                        for k in range(DC):
                            op("tensor", lambda t, b=b, k=k, wb=wb, tb=tb:
                               t.matmul(psA[:, b * 512:(b + 1) * 512],
                                        WQK[wb][:, k * 128:(k + 1) * 128],
                                        XT[k][:, tb * 512:(tb + 1) * 512],
                                        start=(k == 0), stop=False))
                        op("tensor", lambda t, b=b, f=f:
                           t.matmul(psA[:, b * 512:(b + 1) * 512],
                                    BQKV[0:1, f * 128:(f + 1) * 128],
                                    ONES[0:1, 0:512], start=False, stop=True), "TM")
                        sc = INV_SQRT_HD if f < DC else 1.0
                        ech += 1
                        if ech % 2 == 0:
                            wait("scalar", "TM", C["TM"])
                            op("scalar", lambda s, b=b, qb=qb, tb=tb, sc=sc:
                               s.activation(QKST[qb][:, tb * 512:(tb + 1) * 512],
                                            psA[:, b * 512:(b + 1) * 512], AF.Copy, scale=sc), "AM")
                            bank_rel[b] = {"AM": C["AM"], "VM": 0}
                        else:
                            wait("vector", "TM", C["TM"])
                            op("vector", lambda v, b=b, qb=qb, tb=tb, sc=sc:
                               v.tensor_scalar(QKST[qb][:, tb * 512:(tb + 1) * 512],
                                               psA[:, b * 512:(b + 1) * 512], sc, None, ALU.mult), "VM")
                            bank_rel[b] = {"AM": 0, "VM": C["VM"]}
                    wqk_rel[wb] = C["TM"]
                    wait("sync", "AM", C["AM"])
                    wait("sync", "VM", C["VM"])
                    dst = (qts_d if f < DC else kts_d)[f % DC][:, hf * THALF:(hf + 1) * THALF]
                    dma("sync", dst, QKST[qb][:, :], "ST")
                    qk_store_rel[qb] = C["ST"]

                def load_wv(vb2):
                    wait("sync", "TM", wv_rel[vb2 % 2])
                    dma("sync", WVb[vb2 % 2].rearrange("p (dc j) -> p dc j", dc=DC),
                        wqkv_d[:, 2 * D + vb2 * 512: 2 * D + (vb2 + 1) * 512]
                        .rearrange("(dc p) j -> p dc j", p=128), "LD")
                    return C["LD"]
                ld_wvs = [0] * 4
                ld_wvs[0] = load_wv(0)
                for vb in range(4):
                    if vb + 1 < 4:
                        ld_wvs[vb + 1] = load_wv(vb + 1)
                    wait("tensor", "LD", ld_wvs[vb])
                    for tt in range(THALF // 128):
                        b = (vb * 16 + tt) % 4
                        vsb = tt % 2
                        wait("tensor", "AM", bank_rel[b]["AM"])
                        wait("tensor", "VM", bank_rel[b]["VM"])
                        for k in range(DC):
                            op("tensor", lambda t, b=b, k=k, tt=tt, WV=WVb[vb % 2]:
                               t.matmul(psA[:, b * 512:(b + 1) * 512],
                                        XT[k][:, tt * 128:(tt + 1) * 128],
                                        WV[:, k * 512:(k + 1) * 512],
                                        start=(k == 0), stop=False))
                        op("tensor", lambda t, b=b, vb=vb:
                           t.matmul(psA[:, b * 512:(b + 1) * 512],
                                    ONES[0:1, 0:128],
                                    BQKV[0:1, 2 * D + vb * 512: 2 * D + vb * 512 + 512],
                                    start=False, stop=True), "TM")
                        ech += 1
                        if ech % 2 == 0:
                            wait("scalar", "TM", C["TM"])
                            wait("scalar", "ST", v_store_rel[vsb])
                            op("scalar", lambda s, b=b, vsb=vsb:
                               s.copy(VST[vsb][:, :], psA[:, b * 512:(b + 1) * 512]), "AM")
                            bank_rel[b] = {"AM": C["AM"], "VM": 0}
                            wait("sync", "AM", C["AM"])
                        else:
                            wait("vector", "TM", C["TM"])
                            wait("vector", "ST", v_store_rel[vsb])
                            op("vector", lambda v, b=b, vsb=vsb:
                               v.tensor_copy(VST[vsb][:, :], psA[:, b * 512:(b + 1) * 512]), "VM")
                            bank_rel[b] = {"AM": 0, "VM": C["VM"]}
                            wait("sync", "VM", C["VM"])
                        r0 = hf * THALF + tt * 128
                        dma("sync", vs_d[r0:r0 + 128, vb * 512:(vb + 1) * 512], VST[vsb][:, :], "ST")
                        v_store_rel[vsb] = C["ST"]
                    wv_rel[vb % 2] = C["TM"]
                xt_rel = C["TM"]
            ph2_TM, ph2_AM, ph2_VM, ph2_ST = C["TM"], C["AM"], C["VM"], C["ST"]
        else:
            ph2_TM = ph2_AM = ph2_VM = ph2_ST = 0

        if 3 in phases:
            pbf3 = DYN0
            def abf3(n):
                nonlocal pbf3
                o = pbf3; pbf3 += n
                assert pbf3 <= NBF
                return o
            o_qh = abf3(2 * SEQ)
            o_kh = abf3(2 * SEQ)
            o_vh = abf3(2 * SEQ)
            o_p = abf3(4 * 2048)
            o_pt = abf3(16 * 512)
            o_ast = abf3(2 * 512)
            QH = [BP[:, o_qh + i * SEQ:o_qh + (i + 1) * SEQ] for i in range(2)]
            KH = [BP[:, o_kh + i * SEQ:o_kh + (i + 1) * SEQ] for i in range(2)]
            VH = [BP[:, o_vh + i * SEQ:o_vh + (i + 1) * SEQ] for i in range(2)]
            PB = [BP[:, o_p + i * 2048:o_p + (i + 1) * 2048] for i in range(4)]
            PT = BP[:, o_pt:o_pt + 16 * 512]
            PT3 = PT.rearrange("p (kt j) -> p kt j", kt=16)
            AST = [BP[:, o_ast + i * 512:o_ast + (i + 1) * 512] for i in range(2)]
            NM = [FP[:, o_nm + i:o_nm + i + 1] for i in range(4)]
            NM4 = FP[:, o_nm4:o_nm4 + 16]
            SR1 = FP[0:1, o_srow:o_srow + 512]
            SR2 = FP[0:1, o_srow + 512:o_srow + 1024]
            RSB = FP[:, o_rsb:o_rsb + 512]

            wait("sync", "TM", ph2_TM)
            wait("sync", "AM", ph2_AM)

            qkv_rel = [0, 0]
            bank_rel = [{"AM": C["AM"], "VM": C["VM"]} for _ in range(4)]
            exp_done = [C["AM"]] * 4
            pt_am = C["AM"]
            pt_vm = C["VM"]
            av_done = C["TM"]
            o_evac = C["VM"]
            ast_rel = [0, 0]
            psT_rel = [{"AM": C["AM"], "VM": C["VM"]} for _ in range(2)]
            pb_rel = [C["TM"]] * 4
            r_evac = C["VM"]
            gchunk = 0

            pairs = [(s, h) for s in range(NSEQ) for h in range(H)]
            def emit_loads(idx):
                s, h = pairs[idx]
                hb = idx % 2
                wait("sync", "TM", qkv_rel[hb])
                dma("sync", QH[hb], qts_d[h][:, s * SEQ:(s + 1) * SEQ], "LD")
                dma("sync", KH[hb], kts_d[h][:, s * SEQ:(s + 1) * SEQ], "LD")
                dma("sync", VH[hb].rearrange("p (kt j) -> p kt j", kt=16),
                    vs_d[s * SEQ:(s + 1) * SEQ, h * 128:(h + 1) * 128]
                    .rearrange("(kt p) j -> p kt j", p=128), "LD")
                return C["LD"]

            ld_done = [0] * len(pairs)
            ld_done[0] = emit_loads(0)
            pend_tr = []
            for idx, (s, h) in enumerate(pairs):
                hb = idx % 2
                if idx + 1 < len(pairs):
                    ld_done[idx + 1] = emit_loads(idx + 1)
                wait("tensor", "LD", ld_done[idx])
                for qb in range(4):
                    nkt = 4 * (qb + 1)
                    N = 512 * (qb + 1)
                    wait("gpsimd", "TM", av_done)
                    wait("gpsimd", "AM", pt_am)
                    wait("gpsimd", "VM", pt_vm)
                    for qtl in range(3):
                        nz = 3 - qtl
                        qt = qb * 4 + qtl
                        op("gpsimd", lambda g, qt=qt, qtl=qtl, nz=nz:
                           g.memset(PT3[:, qt + 1:qt + 1 + nz, qtl * 128:(qtl + 1) * 128], 0.0), "GM")
                    gm_after = C["GM"]
                    wait("scalar", "TM", av_done)
                    wait("vector", "TM", av_done)
                    def emit_transposes(qt2, pi2, qtl2, exp_am):
                        nonlocal pt_am, pt_vm
                        wait("tensor", "AM", exp_am)
                        wait("tensor", "GM", gm_after)
                        ntr = qt2 + 1
                        for kt in range(ntr):
                            bkk = (kt // 8) % 2
                            if kt % 8 == 0:
                                wait("tensor", "AM", psT_rel[bkk]["AM"])
                                wait("tensor", "VM", psT_rel[bkk]["VM"])
                            lastt = (kt % 8 == 7 or kt == ntr - 1)
                            op("tensor", lambda t, kt=kt, pi2=pi2, bkk=bkk:
                               t.transpose(psT[bkk][:, (kt % 8) * 128:(kt % 8 + 1) * 128],
                                           PB[pi2][:, kt * 128:(kt + 1) * 128], EYE),
                               "TM" if lastt else None)
                            if lastt:
                                ntiles = kt % 8 + 1
                                k0 = kt - ntiles + 1
                                eng3 = "scalar" if bkk == 0 else "vector"
                                wait(eng3, "TM", C["TM"])
                                if eng3 == "scalar":
                                    op(eng3, lambda sC, k0=k0, ntiles=ntiles, qtl2=qtl2, bkk=bkk:
                                       sC.copy(PT3[:, k0:k0 + ntiles, qtl2 * 128:(qtl2 + 1) * 128],
                                               psT[bkk].rearrange("p (kt j) -> p kt j", kt=8)
                                               [:, 0:ntiles, :]), "AM")
                                    psT_rel[bkk] = {"AM": C["AM"], "VM": 0}
                                    pt_am = C["AM"]
                                else:
                                    op(eng3, lambda v, k0=k0, ntiles=ntiles, qtl2=qtl2, bkk=bkk:
                                       v.tensor_copy(PT3[:, k0:k0 + ntiles, qtl2 * 128:(qtl2 + 1) * 128],
                                                     psT[bkk].rearrange("p (kt j) -> p kt j", kt=8)
                                                     [:, 0:ntiles, :]), "VM")
                                    psT_rel[bkk] = {"AM": 0, "VM": C["VM"]}
                                    pt_vm = C["VM"]
                        pb_rel[pi2] = C["TM"]

                    for qtl in range(4):
                        qt = qb * 4 + qtl
                        pi = qt % 4
                        tile_banks = []
                        for c in range(qb + 1):
                            b = gchunk % 4
                            gchunk += 1
                            tile_banks.append((c, b))
                            diag = (c == qb)
                            wait("tensor", "AM", bank_rel[b]["AM"])
                            op("tensor", lambda t, c=c, qt=qt, hb=hb, b=b, diag=diag:
                               t.matmul(psA[:, b * 512:(b + 1) * 512],
                                        QH[hb][:, qt * 128:(qt + 1) * 128],
                                        KH[hb][:, c * 512:(c + 1) * 512],
                                        start=True, stop=not diag),
                               None if diag else "TM")
                            if diag:
                                op("tensor", lambda t, b=b, qtl=qtl:
                                   t.matmul(psA[:, b * 512:(b + 1) * 512],
                                            EYE, MASK[:, qtl * 512:(qtl + 1) * 512],
                                            start=False, stop=True), "TM")
                            mm_tm = C["TM"]
                            wait("vector", "TM", mm_tm)
                            if c == 0:
                                wait("vector", "AM", exp_done[pi])
                            op("vector", lambda v, pi=pi, c=c, b=b:
                               v.reduce_max(NM4[:, pi * 4 + c:pi * 4 + c + 1],
                                            psA[:, b * 512:(b + 1) * 512],
                                            axis=AX.X, negate=True),
                               "VM" if diag else None)
                        op("vector", lambda v, pi=pi, qb=qb:
                           v.tensor_reduce(NM[pi][:, :], NM4[:, pi * 4:pi * 4 + qb + 1],
                                           axis=AX.X, op=ALU.min), "VM")
                        wait("scalar", "VM", C["VM"])
                        wait("scalar", "TM", pb_rel[pi])
                        for c, b in tile_banks:
                            lastc = (c == qb)
                            op("scalar", lambda sC, pi=pi, c=c, b=b:
                               sC.activation(PB[pi][:, c * 512:(c + 1) * 512],
                                             psA[:, b * 512:(b + 1) * 512], AF.Exp,
                                             bias=NM[pi][:, :1], scale=1.0),
                               "AM" if lastc else None)
                        exp_done[pi] = C["AM"]
                        for c, b in tile_banks:
                            bank_rel[b] = {"AM": exp_done[pi], "VM": 0}
                        pend_tr.append((qt, pi, qtl, exp_done[pi]))
                        if len(pend_tr) > 2:
                            emit_transposes(*pend_tr.pop(0))
                    while pend_tr:
                        emit_transposes(*pend_tr.pop(0))
                    wait("tensor", "AM", pt_am)
                    wait("tensor", "VM", pt_vm)
                    wait("tensor", "GM", gm_after)
                    wait("tensor", "VM", r_evac)
                    for kt in range(nkt):
                        op("tensor", lambda t, kt=kt, nkt=nkt:
                           t.matmul(psC[0:1, :], ONES[:, 0:1],
                                    PT[:, kt * 512:(kt + 1) * 512],
                                    start=(kt == 0), stop=(kt == nkt - 1)),
                           "TM" if kt == nkt - 1 else None)
                    sums_tm = C["TM"]
                    wait("scalar", "TM", sums_tm)
                    op("scalar", lambda sC: sC.copy(SR1[:, :], psC[0:1, :]), "AM")
                    op("scalar", lambda sC: sC.activation(SR2[:, :], SR1[:, :], AF.Ln), "AM")
                    op("scalar", lambda sC: sC.activation(SR1[:, :], SR2[:, :], AF.Exp, scale=-1.0), "AM")
                    srow_am = C["AM"]
                    wait("tensor", "AM", srow_am)
                    op("tensor", lambda t:
                       t.matmul(psC[:, :], ONESF[0:1, 0:128], SR1[:, :],
                                start=True, stop=True), "TM")
                    bcast_tm = C["TM"]
                    wait("vector", "TM", bcast_tm)
                    op("vector", lambda v: v.tensor_copy(RSB[:, :], psC[:, :]), "VM")
                    r_evac = C["VM"]
                    wait("tensor", "VM", o_evac)
                    for kt in range(nkt):
                        op("tensor", lambda t, kt=kt, hb=hb, nkt=nkt:
                           t.matmul(psB[:, :], VH[hb][:, kt * 128:(kt + 1) * 128],
                                    PT[:, kt * 512:(kt + 1) * 512],
                                    start=(kt == 0), stop=(kt == nkt - 1)),
                           "TM" if kt == nkt - 1 else None)
                    av_done = C["TM"]
                    ab = qb % 2
                    wait("vector", "TM", av_done)
                    wait("vector", "VM", r_evac)
                    wait("vector", "ST", ast_rel[ab])
                    op("vector", lambda v, ab=ab:
                       v.tensor_mul(AST[ab][:, :], psB[:, :], RSB[:, :]), "VM")
                    o_evac = C["VM"]
                    wait("sync", "VM", o_evac)
                    dma("sync", atts_d[h][:, s * SEQ + qb * 512: s * SEQ + (qb + 1) * 512],
                        AST[ab][:, :], "ST")
                    ast_rel[ab] = C["ST"]
                qkv_rel[hb] = C["TM"]
            ph3_TM, ph3_AM, ph3_VM, ph3_ST = C["TM"], C["AM"], C["VM"], C["ST"]
        else:
            ph3_TM, ph3_AM, ph3_VM, ph3_ST = ph2_TM, ph2_AM, ph2_VM, ph2_ST


        if 4 in phases:
            pbf4 = DYN0
            def abf4(n):
                nonlocal pbf4
                o = pbf4; pbf4 += n
                assert pbf4 <= NBF
                return o
            o_attb = abf4(2 * DC * 512)
            o_wo = abf4(2 * DC * 512)
            o_xb = abf4(2 * 4 * D)
            o_r1 = abf4(4 * D)
            o_x1t = abf4(D)
            o_x1f = abf4(2 * D)
            ATTB = [BP[:, o_attb + i * DC * 512:o_attb + (i + 1) * DC * 512] for i in range(2)]
            WO = [BP[:, o_wo + i * DC * 512:o_wo + (i + 1) * DC * 512] for i in range(2)]
            XB = [BP[:, o_xb + i * 4 * D:o_xb + (i + 1) * 4 * D] for i in range(2)]
            R1 = [BP[:, o_r1 + tt * D:o_r1 + (tt + 1) * D] for tt in range(4)]
            X1T = BP[:, o_x1t:o_x1t + D]
            X1F = [BP[:, o_x1f + i * D:o_x1f + (i + 1) * D] for i in range(2)]
            COLS = FP[:, o_cols:o_cols + 32]

            wait("sync", "TM", ph3_TM)
            wait("sync", "AM", ph3_AM)
            wait("sync", "VM", ph3_VM)

            attb_rel = [0, 0]
            wo_rel = [0, 0]
            xb_rel = [0, 0]
            bank_rel4 = [{"AM": C["AM"], "VM": C["VM"]} for _ in range(4)]
            x1f_rel = [0, 0]
            psT_rel4 = [C["AM"], C["AM"]]
            x1t_rel = C["TM"]
            r1_rel = C["AM"]
            cols_rel = [C["VM"], C["VM"]]

            for tb in range(8):
                bb = tb % 2
                wait("sync", "TM", attb_rel[bb])
                dma("sync", ATTB[bb].rearrange("p (dc j) -> p dc j", dc=DC),
                    atts_d[:, :, tb * 512:(tb + 1) * 512].rearrange("dc p j -> p dc j"), "LD")
                wait("sync", "VM", xb_rel[bb])
                dma("sync", XB[bb].rearrange("p (tt j) -> p tt j", tt=4),
                    x_d[tb * 512:(tb + 1) * 512, :].rearrange("(tt p) j -> p tt j", p=128), "LD")
                ld_tb = C["LD"]
                def load_wo(db2):
                    wb2 = db2 % 2
                    wait("sync", "TM", wo_rel[wb2])
                    dma("sync", WO[wb2].rearrange("p (dc j) -> p dc j", dc=DC),
                        wout_d[:, db2 * 512:(db2 + 1) * 512].rearrange("(dc p) j -> p dc j", p=128), "LD")
                    return C["LD"]
                if tb == 0:
                    ld_wo = [0] * 4
                    ld_wo[0] = load_wo(0)
                for db in range(4):
                    wb = db % 2
                    if db + 1 < 4:
                        ld_wo[db + 1] = load_wo(db + 1)
                    elif tb < 7:
                        ld_wo[0] = load_wo(0)
                    ld_w = ld_wo[db]
                    wait("tensor", "LD", max(ld_tb, ld_w))
                    for tt in range(4):
                        b = (tb * 16 + db * 4 + tt) % 4
                        wait("tensor", "AM", bank_rel4[b]["AM"])
                        wait("tensor", "VM", bank_rel4[b]["VM"])
                        for k in range(DC):
                            op("tensor", lambda t, b=b, k=k, bb=bb, wb=wb, tt=tt:
                               t.matmul(psA[:, b * 512:(b + 1) * 512],
                                        ATTB[bb][:, k * 512 + tt * 128: k * 512 + (tt + 1) * 128],
                                        WO[wb][:, k * 512:(k + 1) * 512],
                                        start=(k == 0), stop=False))
                        op("tensor", lambda t, b=b, db=db:
                           t.matmul(psA[:, b * 512:(b + 1) * 512],
                                    ONES[0:1, 0:128], BOUT[0:1, db * 512:(db + 1) * 512],
                                    start=False, stop=True), "TM")
                        wait("vector", "TM", C["TM"])
                        wait("vector", "AM", r1_rel)
                        op("vector", lambda v, b=b, bb=bb, tt=tt, db=db:
                           v.tensor_add(R1[tt][:, db * 512:(db + 1) * 512],
                                        psA[:, b * 512:(b + 1) * 512],
                                        XB[bb][:, tt * D + db * 512: tt * D + (db + 1) * 512]), "VM")
                        bank_rel4[b] = {"AM": C["AM"], "VM": C["VM"]}
                    wo_rel[wb] = C["TM"]
                attb_rel[bb] = C["TM"]
                xb_rel[bb] = C["VM"]
                vm_r1 = C["VM"]
                for tt in range(4):
                    par = tt % 2
                    cs = par * 16
                    S1 = COLS[:, cs + 0:cs + 1]; S2 = COLS[:, cs + 1:cs + 2]
                    M = COLS[:, cs + 2:cs + 3]; MS = COLS[:, cs + 3:cs + 4]
                    M2 = COLS[:, cs + 4:cs + 5]; VAR = COLS[:, cs + 5:cs + 6]
                    SD = COLS[:, cs + 6:cs + 7]; RSTD = COLS[:, cs + 7:cs + 8]
                    NMR = COLS[:, cs + 8:cs + 9]
                    wait("vector", "VM", cols_rel[par])
                    op("vector", lambda v, tt=tt, S1=S1: v.reduce_sum(S1, R1[tt][:, :], axis=AX.X), "VM")
                    wait("scalar", "VM", C["VM"])
                    op("scalar", lambda sC, tt=tt:
                       sC.activation(SQ[:, :], R1[tt][:, :], AF.Square), "AM")
                    op("vector", lambda v, S1=S1, M=M: v.tensor_scalar(M, S1, 1.0 / D, None, ALU.mult), "VM")
                    wait("vector", "AM", C["AM"])
                    op("vector", lambda v, S2=S2: v.reduce_sum(S2, SQ[:, :], axis=AX.X), "VM")
                    op("vector", lambda v, S2=S2, MS=MS: v.tensor_scalar(MS, S2, 1.0 / D, None, ALU.mult), "VM")
                    op("vector", lambda v, M=M, M2=M2: v.tensor_mul(M2, M, M), "VM")
                    op("vector", lambda v, MS=MS, M2=M2, VAR=VAR: v.tensor_sub(VAR, MS, M2), "VM")
                    op("vector", lambda v, VAR=VAR: v.tensor_scalar(VAR, VAR, EPS, None, ALU.add), "VM")
                    wait("scalar", "VM", C["VM"])
                    op("scalar", lambda sC, VAR=VAR, SD=SD:
                       sC.activation(SD, VAR, AF.Sqrt), "AM")
                    wait("vector", "AM", C["AM"])
                    op("vector", lambda v, SD=SD, RSTD=RSTD: v.reciprocal(RSTD, SD), "VM")
                    op("vector", lambda v, M=M, RSTD=RSTD, NMR=NMR:
                       v.scalar_tensor_tensor(NMR, M, -1.0, RSTD, ALU.mult, ALU.mult), "VM")
                    wait("vector", "TM", x1t_rel)
                    op("vector", lambda v, tt=tt, RSTD=RSTD, NMR=NMR:
                       v.tensor_scalar(X1T[:, :], R1[tt][:, :], RSTD[:, :1], NMR[:, :1],
                                       ALU.mult, ALU.add), "VM")
                    cols_rel[par] = C["VM"]
                    vm_x1t = C["VM"]
                    xf = tt % 2
                    wait("tensor", "VM", vm_x1t)
                    wait("tensor", "ST", x1f_rel[xf])
                    for dc in range(DC):
                        bkk = (dc // 8) % 2
                        if dc % 8 == 0:
                            wait("tensor", "AM", psT_rel4[bkk])
                        op("tensor", lambda t, dc=dc, bkk=bkk:
                           t.transpose(psT[bkk][:, (dc % 8) * 128:(dc % 8 + 1) * 128],
                                       X1T[:, dc * 128:(dc + 1) * 128], EYE),
                           "TM" if dc % 8 == 7 else None)
                        if dc % 8 == 7:
                            wait("scalar", "TM", C["TM"])
                            op("scalar", lambda sC, dc=dc, bkk=bkk, xf=xf:
                               sC.copy(X1F[xf][:, (dc - 7) * 128:(dc + 1) * 128],
                                       psT[bkk][:, :]), "AM")
                            psT_rel4[bkk] = C["AM"]
                    x1t_rel = C["TM"]
                    wait("sync", "AM", C["AM"])
                    dma("sync", x1s_d[:, :, tb * 512 + tt * 128: tb * 512 + (tt + 1) * 128]
                        .rearrange("dc p j -> p dc j"),
                        X1F[xf].rearrange("p (dc j) -> p dc j", dc=DC), "ST")
                    x1f_rel[xf] = C["ST"]
                r1_rel = C["AM"]
            ph4_TM, ph4_AM, ph4_VM, ph4_ST = C["TM"], C["AM"], C["VM"], C["ST"]
        else:
            ph4_TM, ph4_AM, ph4_VM, ph4_ST = ph3_TM, ph3_AM, ph3_VM, ph3_ST

        if 5 in phases:
            pbf5 = DYN0
            def abf5(n):
                nonlocal pbf5
                o = pbf5; pbf5 += n
                assert pbf5 <= NBF, f"ph5 bf16 overflow {pbf5}"
                return o
            o_fuse = abf5(JC * 512)
            o_x1 = abf5(DC * 512)
            o_wg = abf5(2 * DC * 128)
            o_wu = abf5(2 * DC * 128)
            o_wd = abf5(2 * JC * 128)
            o_gt = abf5(2 * 512)
            o_ut = abf5(2 * 512)
            FUSE = BP[:, o_fuse:o_fuse + JC * 512]
            X1 = BP[:, o_x1:o_x1 + DC * 512]
            WG = [BP[:, o_wg + i * DC * 128:o_wg + (i + 1) * DC * 128] for i in range(2)]
            WU = [BP[:, o_wu + i * DC * 128:o_wu + (i + 1) * DC * 128] for i in range(2)]
            WD = [BP[:, o_wd + i * JC * 128:o_wd + (i + 1) * JC * 128] for i in range(2)]
            GT = [BP[:, o_gt + i * 512:o_gt + (i + 1) * 512] for i in range(2)]
            UT = [BP[:, o_ut + i * 512:o_ut + (i + 1) * 512] for i in range(2)]
            pf5 = FDYN0
            def af5(n):
                nonlocal pf5
                o = pf5; pf5 += n
                assert pf5 <= NF32, f"ph5 f32 overflow {pf5}"
                return o
            o_r2 = af5(2 * 512)
            o_dt = af5(2 * 512)
            o_sqf = af5(2 * 512)
            o_s1b = af5(512)
            o_s2b = af5(512)
            o_tmp = af5(512)
            o_tmp2 = af5(512)
            o_ostg = af5(2 * 512)
            R2 = [FP[:, o_r2 + i * 512:o_r2 + (i + 1) * 512] for i in range(2)]
            DT = [FP[:, o_dt + i * 512:o_dt + (i + 1) * 512] for i in range(2)]
            SQF = [FP[:, o_sqf + i * 512:o_sqf + (i + 1) * 512] for i in range(2)]
            S1B = FP[:, o_s1b:o_s1b + 512]
            S2B = FP[:, o_s2b:o_s2b + 512]
            TMP = FP[:, o_tmp:o_tmp + 512]
            TMP2 = FP[:, o_tmp2:o_tmp2 + 512]
            OSTG = [FP[:, o_ostg + i * 512:o_ostg + (i + 1) * 512] for i in range(2)]
            PRMC = FP[:, o_prmc:o_prmc + 64]

            wait("sync", "TM", ph4_TM)
            wait("sync", "AM", ph4_AM)
            wait("sync", "VM", ph4_VM)
            wait("gpsimd", "TM", ph4_TM)
            wait("gpsimd", "AM", ph4_AM)
            wait("gpsimd", "VM", ph4_VM)

            wg_rel = [0, 0]
            wd_rel = [0, 0]
            gtu_rel = [0, 0]
            gbank_rel = [C["AM"]] * 4
            fuse_rel = 0
            x1_rel_tm = 0
            x1_rel_vm = 0
            x1_rel_am = 0
            dn_evac = [C["AM"], C["AM"]]
            r2_rel_tm = [C["TM"], C["TM"]]
            r2_rel_am = [C["AM"], C["AM"]]
            dt_rel = [C["VM"], C["VM"]]
            sqf_rel = [C["TM"], C["TM"]]
            trb_rel = [C["AM"], C["AM"]]
            s12_rel = [C["VM"], C["VM"]]
            ostg_rel = [0, 0]
            prm_rel = 0

            for sl in range(8):
                wait("sync", "ST", ph4_ST)
                wait("sync", "TM", x1_rel_tm)
                wait("sync", "VM", x1_rel_vm)
                dma("sync", X1.rearrange("p (dc j) -> p dc j", dc=DC),
                    x1s_d[:, :, sl * 512:(sl + 1) * 512].rearrange("dc p j -> p dc j"), "LD")
                ld_x1 = C["LD"]
                for j in range(JC):
                    wb = j % 2
                    wait("gpsimd", "TM", wg_rel[wb])
                    dma("gpsimd", WG[wb].rearrange("p (dc i) -> p dc i", dc=DC),
                        wg_d[:, j * 128:(j + 1) * 128].rearrange("(dc p) i -> p dc i", p=128), "LG")
                    dma("gpsimd", WU[wb].rearrange("p (dc i) -> p dc i", dc=DC),
                        wu_d[:, j * 128:(j + 1) * 128].rearrange("(dc p) i -> p dc i", p=128), "LG")
                    ld_w = C["LG"]
                    bg = (2 * j) % 4
                    bu = (2 * j + 1) % 4
                    wait("tensor", "LD", ld_x1)
                    wait("tensor", "LG", ld_w)
                    wait("tensor", "AM", max(gbank_rel[bg], gbank_rel[bu]))
                    for k in range(DC):
                        op("tensor", lambda t, bg=bg, k=k, wb=wb:
                           t.matmul(psA[:, bg * 512:(bg + 1) * 512],
                                    WG[wb][:, k * 128:(k + 1) * 128],
                                    X1[:, k * 512:(k + 1) * 512],
                                    start=(k == 0), stop=(k == DC - 1)),
                           "TM" if k == DC - 1 else None)
                    for k in range(DC):
                        op("tensor", lambda t, bu=bu, k=k, wb=wb:
                           t.matmul(psA[:, bu * 512:(bu + 1) * 512],
                                    WU[wb][:, k * 128:(k + 1) * 128],
                                    X1[:, k * 512:(k + 1) * 512],
                                    start=(k == 0), stop=(k == DC - 1)),
                           "TM" if k == DC - 1 else None)
                    wg_rel[wb] = C["TM"]
                    gp = j % 2
                    wait("scalar", "TM", C["TM"])
                    wait("scalar", "VM", gtu_rel[gp])
                    op("scalar", lambda sC, bg=bg, j=j, gp=gp:
                       sC.activation(GT[gp][:, :], psA[:, bg * 512:(bg + 1) * 512], AF.Silu,
                                     scale=SGT[:, j:j + 1]), None)
                    op("scalar", lambda sC, bu=bu, j=j, gp=gp:
                       sC.activation(UT[gp][:, :], psA[:, bu * 512:(bu + 1) * 512], AF.Copy,
                                     scale=SUT[:, j:j + 1]), "AM")
                    gbank_rel[bg] = C["AM"]
                    gbank_rel[bu] = C["AM"]
                    wait("vector", "AM", C["AM"])
                    wait("vector", "TM", fuse_rel)
                    op("vector", lambda v, j=j, gp=gp:
                       v.tensor_mul(FUSE[:, j * 512:(j + 1) * 512], GT[gp][:, :], UT[gp][:, :]), "VM")
                    gtu_rel[gp] = C["VM"]
                vm_fuse = C["VM"]
                def emit_tail(dout, dnb, vm_r2, am_sq):
                    nonlocal s12_rel, sqf_rel, r2_rel_tm, r2_rel_am, trb_rel, ostg_rel
                    wait("tensor", "VM", vm_r2)
                    wait("tensor", "VM", s12_rel[0])
                    op("tensor", lambda t, dnb=dnb:
                       t.matmul(psA[:, 2 * 512:3 * 512], ONESF, R2[dnb][:, :],
                                start=True, stop=True), "TM")
                    tm_s1 = C["TM"]
                    wait("tensor", "AM", am_sq)
                    wait("tensor", "VM", s12_rel[1])
                    op("tensor", lambda t, dnb=dnb:
                       t.matmul(psA[:, 3 * 512:4 * 512], ONESF, SQF[dnb][:, :],
                                start=True, stop=True), "TM")
                    tm_s2 = C["TM"]
                    sqf_rel[dnb] = tm_s2
                    wait("vector", "TM", tm_s1)
                    if dout == 0:
                        op("vector", lambda v: v.tensor_copy(S1B[:, :], psA[:, 2 * 512:3 * 512]), "VM")
                    else:
                        op("vector", lambda v: v.tensor_add(S1B[:, :], S1B[:, :], psA[:, 2 * 512:3 * 512]), "VM")
                    s12_rel[0] = C["VM"]
                    wait("vector", "TM", tm_s2)
                    if dout == 0:
                        op("vector", lambda v: v.tensor_copy(S2B[:, :], psA[:, 3 * 512:4 * 512]), "VM")
                    else:
                        op("vector", lambda v: v.tensor_add(S2B[:, :], S2B[:, :], psA[:, 3 * 512:4 * 512]), "VM")
                    s12_rel[1] = C["VM"]
                    bt = dout % 2
                    wait("tensor", "AM", trb_rel[bt])
                    for tt in range(4):
                        op("tensor", lambda t, tt=tt, dnb=dnb, bt=bt:
                           t.transpose(psA[:, bt * 512 + tt * 128: bt * 512 + (tt + 1) * 128],
                                       R2[dnb][:, tt * 128:(tt + 1) * 128], EYEF),
                           "TM" if tt == 3 else None)
                    r2_rel_tm[dnb] = C["TM"]
                    r2_rel_am[dnb] = am_sq
                    ob = dout % 2
                    wait("scalar", "TM", C["TM"])
                    wait("scalar", "ST", ostg_rel[ob])
                    op("scalar", lambda sC, bt=bt, ob=ob:
                       sC.copy(OSTG[ob][:, :], psA[:, bt * 512:(bt + 1) * 512]), "AM")
                    trb_rel[bt] = C["AM"]
                    wait("sync", "AM", C["AM"])
                    dma("sync",
                        r2s_d[sl * 512:(sl + 1) * 512, dout * 128:(dout + 1) * 128]
                        .rearrange("(tt p) j -> p tt j", p=128),
                        OSTG[ob].rearrange("p (tt j) -> p tt j", tt=4), "ST")
                    ostg_rel[ob] = C["ST"]

                pend_tail = None
                for dout in range(DC):
                    wb = dout % 2
                    wait("gpsimd", "TM", wd_rel[wb])
                    dma("gpsimd", WD[wb].rearrange("p (jc i) -> p jc i", jc=JC),
                        wd_d[:, dout * 128:(dout + 1) * 128].rearrange("(jc p) i -> p jc i", p=128), "LG")
                    ld_w = C["LG"]
                    dnb = dout % 2
                    ps_dn = psB if dnb == 0 else psC
                    wait("tensor", "LG", ld_w)
                    wait("tensor", "VM", vm_fuse)
                    wait("tensor", "AM", dn_evac[dnb])
                    for jc in range(JC):
                        op("tensor", lambda t, jc=jc, wb=wb, ps_dn=ps_dn:
                           t.matmul(ps_dn[:, :], WD[wb][:, jc * 128:(jc + 1) * 128],
                                    FUSE[:, jc * 512:(jc + 1) * 512],
                                    start=(jc == 0), stop=(jc == JC - 1)),
                           "TM" if jc == JC - 1 else None)
                    wd_rel[wb] = C["TM"]
                    dn_done = C["TM"]
                    wait("scalar", "TM", dn_done)
                    wait("scalar", "VM", dt_rel[dnb])
                    op("scalar", lambda sC, dnb=dnb, dout=dout, ps_dn=ps_dn:
                       sC.activation(DT[dnb][:, :], ps_dn[:, :], AF.Copy,
                                     scale=SDT[:, dout:dout + 1]), "AM")
                    dn_evac[dnb] = C["AM"]
                    wait("vector", "AM", dn_evac[dnb])
                    wait("vector", "TM", r2_rel_tm[dnb])
                    wait("vector", "AM", r2_rel_am[dnb])
                    op("vector", lambda v, dnb=dnb, dout=dout:
                       v.tensor_add(R2[dnb][:, :], DT[dnb][:, :], X1[:, dout * 512:(dout + 1) * 512]), "VM")
                    dt_rel[dnb] = C["VM"]
                    vm_r2 = C["VM"]
                    wait("scalar", "VM", vm_r2)
                    wait("scalar", "TM", sqf_rel[dnb])
                    op("scalar", lambda sC, dnb=dnb:
                       sC.activation(SQF[dnb][:, :], R2[dnb][:, :], AF.Square), "AM")
                    am_sq = C["AM"]
                    if pend_tail is not None:
                        emit_tail(*pend_tail)
                    pend_tail = (dout, dnb, vm_r2, am_sq)
                if pend_tail is not None:
                    emit_tail(*pend_tail)
                    pend_tail = None
                if debug:
                    wait("sync", "VM", C["VM"])
                    dma("sync", s1b_d[sl], S1B[:, :], "ST")
                    dma("sync", s2b_d[sl], S2B[:, :], "ST")
                    wait("vector", "ST", C["ST"])
                wait("vector", "AM", prm_rel)
                op("vector", lambda v: v.tensor_scalar(S1B[:, :], S1B[:, :], 1.0 / D, None, ALU.mult), "VM")
                op("vector", lambda v: v.tensor_scalar(S2B[:, :], S2B[:, :], 1.0 / D, None, ALU.mult), "VM")
                op("vector", lambda v: v.tensor_mul(TMP[:, :], S1B[:, :], S1B[:, :]), "VM")
                op("vector", lambda v: v.tensor_sub(S2B[:, :], S2B[:, :], TMP[:, :]), "VM")
                op("vector", lambda v: v.tensor_scalar(S2B[:, :], S2B[:, :], EPS, None, ALU.add), "VM")
                wait("scalar", "VM", C["VM"])
                op("scalar", lambda sC: sC.activation(TMP[:, :], S2B[:, :], AF.Sqrt), "AM")
                wait("vector", "AM", C["AM"])
                op("vector", lambda v: v.reciprocal(TMP2[:, :], TMP[:, :]), "VM")
                op("vector", lambda v:
                   v.scalar_tensor_tensor(S1B[:, :], S1B[:, :], -1.0, TMP2[:, :],
                                          ALU.mult, ALU.mult), "VM")
                vm_prm = C["VM"]
                wait("tensor", "VM", vm_prm)
                wait("tensor", "AM", trb_rel[0])
                for tt in range(4):
                    gt_ = sl * 4 + tt
                    op("tensor", lambda t, tt=tt:
                       t.transpose(psA[:, tt * 2:tt * 2 + 1], TMP2[0:1, tt * 128:(tt + 1) * 128],
                                   EYEF[:1, :1]), None)
                    op("tensor", lambda t, tt=tt:
                       t.transpose(psA[:, tt * 2 + 1:tt * 2 + 2], S1B[0:1, tt * 128:(tt + 1) * 128],
                                   EYEF[:1, :1]),
                       "TM" if tt == 3 else None)
                tm_prm = C["TM"]
                wait("scalar", "TM", tm_prm)
                for tt in range(4):
                    gt_ = sl * 4 + tt
                    op("scalar", lambda sC, tt=tt, gt_=gt_:
                       sC.copy(PRMC[:, gt_:gt_ + 1], psA[:, tt * 2:tt * 2 + 1]), None)
                    op("scalar", lambda sC, tt=tt, gt_=gt_:
                       sC.copy(PRMC[:, 32 + gt_:32 + gt_ + 1], psA[:, tt * 2 + 1:tt * 2 + 2]),
                       "AM" if tt == 3 else None)
                prm_rel = C["AM"]
                trb_rel[0] = C["AM"]
                gbank_rel = [C["AM"]] * 4
                fuse_rel = C["TM"]
                x1_rel_tm = C["TM"]
                x1_rel_vm = C["VM"]
            ph5_TM, ph5_AM, ph5_VM, ph5_ST = C["TM"], C["AM"], C["VM"], C["ST"]

            pf6 = FDYN0
            def af6(n):
                nonlocal pf6
                o = pf6; pf6 += n
                assert pf6 <= NF32
                return o
            o_rb = af6(2 * 2048)
            o_ob = af6(2 * 2048)
            RB = [FP[:, o_rb + i * 2048:o_rb + (i + 1) * 2048] for i in range(2)]
            OB = [FP[:, o_ob + i * 2048:o_ob + (i + 1) * 2048] for i in range(2)]
            wait("sync", "TM", ph5_TM)
            wait("sync", "AM", ph5_AM)
            wait("sync", "VM", ph5_VM)
            rb_rel = [0, 0]
            ob_rel = [0, 0]
            for rb_i in range(T // 128):
                bi = rb_i % 2
                wait("sync", "VM", rb_rel[bi])
                dma("sync", RB[bi][:, :], r2s_d[rb_i * 128:(rb_i + 1) * 128, :], "LD")
                ld_rb = C["LD"]
                wait("vector", "LD", ld_rb)
                wait("vector", "ST", ob_rel[bi])
                op("vector", lambda v, bi=bi, rb_i=rb_i:
                   v.tensor_scalar(OB[bi][:, :], RB[bi][:, :], PRMC[:, rb_i:rb_i + 1],
                                   PRMC[:, 32 + rb_i:32 + rb_i + 1],
                                   ALU.mult, ALU.add), "VM")
                rb_rel[bi] = C["VM"]
                wait("sync", "VM", C["VM"])
                dma("sync", out_d[rb_i * 128:(rb_i + 1) * 128, :], OB[bi][:, :], "ST")
                ob_rel[bi] = C["ST"]

        if debug and 5 in phases:
            wait("sync", "AM", C["AM"])
            dma("sync", prm_d[:, :], FP[:, o_prmc:o_prmc + 64], "ST")
        wait("sync", "ST", C["ST"])
        wait("sync", "LG", C["LG"])

        @block.sync
        def _(eng):
            for f in Q["sync"]:
                f(eng)

        @block.gpsimd
        def _(eng):
            for f in Q["gpsimd"]:
                f(eng)

        @block.tensor
        def _(eng):
            for f in Q["tensor"]:
                f(eng)

        @block.vector
        def _(eng):
            for f in Q["vector"]:
                f(eng)

        @block.scalar
        def _(eng):
            for f in Q["scalar"]:
                f(eng)

    return nc


def _causal_mask():
    m = np.zeros((128, 2048), dtype=np.float32)
    p = np.arange(128)[:, None]
    for r in range(4):
        j = np.arange(512)[None, :]
        m[:, r * 512:(r + 1) * 512] = np.where(j > r * 128 + p, -1e30, 0.0)
    return m.astype(ml_dtypes.bfloat16)


def make_core_inputs(core, inputs):
    X = np.asarray(inputs["L0_attn_x"])
    x_c = np.ascontiguousarray(X[2 * core:2 * core + 2].reshape(T, D))
    return {
        "x": x_c,
        "wqkv": np.ascontiguousarray(inputs["L0_attn_W_qkv"]),
        "bqkv": np.ascontiguousarray(np.asarray(inputs["L0_attn_b_qkv"]).reshape(1, TD)),
        "wout": np.ascontiguousarray(inputs["L0_attn_W_out"]),
        "bout": np.ascontiguousarray(np.asarray(inputs["L0_attn_b_out"]).reshape(1, D)),
        "wg": np.ascontiguousarray(inputs["L0_ffn_w_gate"]),
        "wu": np.ascontiguousarray(inputs["L0_ffn_w_up"]),
        "wd": np.ascontiguousarray(inputs["L0_ffn_w_down"]),
        "sg": np.ascontiguousarray(np.asarray(inputs["L0_ffn_s_gate"], dtype=np.float32).reshape(JC, 128)),
        "su": np.ascontiguousarray(np.asarray(inputs["L0_ffn_s_up"], dtype=np.float32).reshape(JC, 128)),
        "sd": np.ascontiguousarray(np.asarray(inputs["L0_ffn_s_down"], dtype=np.float32).reshape(DC, 128)),
        "eye": np.eye(128, dtype=ml_dtypes.bfloat16),
        "eyef": np.eye(128, dtype=np.float32),
        "onesf": np.ones((128, 128), dtype=np.float32),
        "mask": _causal_mask(),
        "ones": np.ones((128, 512), dtype=ml_dtypes.bfloat16),
    }


def kernel(**inputs):
    nc = build_core_program()
    in_maps = [make_core_inputs(c, inputs) for c in range(NCORES)]
    res = run_bass_kernel_spmd(nc, in_maps, core_ids=list(range(NCORES)))
    out = np.concatenate([r["out"].reshape(NSEQ, SEQ, D) for r in res.results], axis=0)
    return out.astype(np.float32)


# version 06
# speedup vs baseline: 1.1235x; 1.1031x over previous
import numpy as np
import ml_dtypes
import concourse.bass as bass
import concourse.mybir as mybir
from concourse.bass_utils import run_bass_kernel_spmd
from contextlib import ExitStack

BF16 = mybir.dt.bfloat16
F32 = mybir.dt.float32
I8 = mybir.dt.int8
AF = mybir.ActivationFunctionType
ALU = mybir.AluOpType
AX = mybir.AxisListType

NCORES = 8
SEQ = 2048
NSEQ = 2
T = SEQ * NSEQ
D = 2048
DC = D // 128
H = 16
TD = 3 * D
INTER = 8192
JC = INTER // 128
NHALF = 2
THALF = T // NHALF
INV_SQRT_HD = 1.0 / float(np.float32(ml_dtypes.bfloat16(np.sqrt(128.0))))
EPS = 1e-5


def build_core_program(phases=(0, 2, 3, 4, 5), debug=False):
    nc = bass.Bass("TRN2", target_bir_lowering=False)

    x_d = nc.dram_tensor("x", [T, D], BF16, kind="ExternalInput")
    wqkv_d = nc.dram_tensor("wqkv", [D, TD], BF16, kind="ExternalInput")
    bqkv_d = nc.dram_tensor("bqkv", [1, TD], BF16, kind="ExternalInput")
    wout_d = nc.dram_tensor("wout", [D, D], BF16, kind="ExternalInput")
    bout_d = nc.dram_tensor("bout", [1, D], BF16, kind="ExternalInput")
    wg_d = nc.dram_tensor("wg", [D, INTER], I8, kind="ExternalInput")
    wu_d = nc.dram_tensor("wu", [D, INTER], I8, kind="ExternalInput")
    wd_d = nc.dram_tensor("wd", [INTER, D], I8, kind="ExternalInput")
    sg_d = nc.dram_tensor("sg", [JC, 128], F32, kind="ExternalInput")
    su_d = nc.dram_tensor("su", [JC, 128], F32, kind="ExternalInput")
    sd_d = nc.dram_tensor("sd", [DC, 128], F32, kind="ExternalInput")
    eye_d = nc.dram_tensor("eye", [128, 128], BF16, kind="ExternalInput")
    eyef_d = nc.dram_tensor("eyef", [128, 128], F32, kind="ExternalInput")
    mask_d = nc.dram_tensor("mask", [128, 2048], BF16, kind="ExternalInput")
    ones_d = nc.dram_tensor("ones", [128, 512], BF16, kind="ExternalInput")
    onesf_d = nc.dram_tensor("onesf", [128, 128], F32, kind="ExternalInput")

    out_d = nc.dram_tensor("out", [T, D], F32, kind="ExternalOutput")
    scratch_kind = "ExternalOutput" if debug else "Internal"
    qts_d = nc.dram_tensor("qts", [DC, 128, T], BF16, kind=scratch_kind)
    kts_d = nc.dram_tensor("kts", [DC, 128, T], BF16, kind=scratch_kind)
    vs_d = nc.dram_tensor("vs", [T, D], BF16, kind=scratch_kind)
    atts_d = nc.dram_tensor("atts", [DC, 128, T], BF16, kind=scratch_kind)
    x1s_d = nc.dram_tensor("x1s", [DC, 128, T], BF16, kind=scratch_kind)
    r2s_d = nc.dram_tensor("r2s", [T, D], F32, kind=scratch_kind)
    wgb_d = nc.dram_tensor("wgb", [D, INTER], BF16, kind="Internal")
    wub_d = nc.dram_tensor("wub", [D, INTER], BF16, kind="Internal")
    wdb_d = nc.dram_tensor("wdb", [INTER, D], BF16, kind="Internal")
    prm_d = nc.dram_tensor("prmdbg", [128, 64], F32, kind=scratch_kind) if debug else None
    s1b_d = nc.dram_tensor("s1bdbg", [8, 128, 512], F32, kind=scratch_kind) if debug else None
    s2b_d = nc.dram_tensor("s2bdbg", [8, 128, 512], F32, kind=scratch_kind) if debug else None

    ENGS = ("sync", "gpsimd", "tensor", "vector", "scalar")
    Q = {k: [] for k in ENGS}
    C = {"LD": 0, "LG": 0, "ST": 0, "TM": 0, "VM": 0, "AM": 0, "GM": 0}
    sems = {}
    GETSEM = [lambda name: None]

    def em(qname, fn):
        Q[qname].append(fn)

    def wait(qname, sem, val):
        if val <= 0:
            return
        GETSEM[0](sem)
        em(qname, lambda e, s=sem, v=val: e.wait_ge(sems[s], v))

    def dma(qname, dst, src, sem):
        GETSEM[0](sem)
        C[sem] += 16
        em(qname, lambda e, d=dst, s=src, sm=sem: e.dma_start(d, s).then_inc(sems[sm], 16))

    def dma_t(qname, dst, src, sem):
        GETSEM[0](sem)
        C[sem] += 16
        em(qname, lambda e, d=dst, s=src, sm=sem: e.dma_start_transpose(d, s).then_inc(sems[sm], 16))

    def op(qname, f, sem=None):
        if sem is None:
            em(qname, lambda e, ff=f: ff(e))
        elif qname in ("scalar", "vector", "gpsimd"):
            C[sem] += 1
            em(qname, lambda e, ff=f: ff(e))
            em(qname, lambda e, sm=sem: e.drain().then_inc(sems[sm], 1))
        else:
            C[sem] += 1
            em(qname, lambda e, ff=f, sm=sem: ff(e).then_inc(sems[sm], 1))

    NBF = 78500
    NF32 = 12500

    with ExitStack() as ctx:
        e = ctx.enter_context
        BP = e(nc.sbuf_tensor("bp", [128, NBF], BF16))
        FP = e(nc.sbuf_tensor("fp", [128, NF32], F32))
        psA = e(nc.psum_tensor("psA", [128, 2048], F32))
        psB = e(nc.psum_tensor("psB", [128, 512], F32))
        psC = e(nc.psum_tensor("psC", [128, 512], F32))
        psT0 = e(nc.psum_tensor("psT0", [128, 1024], BF16))
        psT1 = e(nc.psum_tensor("psT1", [128, 1024], BF16))
        psT = [psT0, psT1]
        for s in ("LD", "LG", "ST", "TM", "VM", "AM", "GM"):
            sems[s] = e(nc.semaphore(s))
        def getsem(name):
            if name not in sems:
                sems[name] = e(nc.semaphore(name))
                C[name] = 0
            return sems[name]
        GETSEM[0] = getsem
        block = e(nc.Block())

        pbf = 0
        def abf(n):
            nonlocal pbf
            o = pbf; pbf += n
            assert pbf <= NBF, f"bf16 pool overflow {pbf}"
            return o
        pf = 0
        def af32(n):
            nonlocal pf
            o = pf; pf += n
            assert pf <= NF32, f"f32 pool overflow {pf}"
            return o

        o_eye = abf(128)
        o_mask = abf(2048)
        o_ones = abf(512)
        o_bqkv = abf(TD)
        o_bout = abf(D)
        DYN0 = pbf

        o_eyef = af32(128)
        o_onesf = af32(128)
        o_sgT = af32(JC)
        o_suT = af32(JC)
        o_sdT = af32(DC)
        o_sstage = af32(128)
        o_cols = af32(32)
        o_sq = af32(2048)
        o_nm = af32(8)
        o_nm4 = af32(16)
        o_prmc = af32(64)
        o_srow = af32(1024)
        o_rsb = af32(512)
        FDYN0 = pf

        EYE = BP[:, o_eye:o_eye + 128]
        MASK = BP[:, o_mask:o_mask + 2048]
        ONES = BP[:, o_ones:o_ones + 512]
        BQKV = BP[:, o_bqkv:o_bqkv + TD]
        BOUT = BP[:, o_bout:o_bout + D]
        EYEF = FP[:, o_eyef:o_eyef + 128]
        ONESF = FP[:, o_onesf:o_onesf + 128]
        SGT = FP[:, o_sgT:o_sgT + JC]
        SUT = FP[:, o_suT:o_suT + JC]
        SDT = FP[:, o_sdT:o_sdT + DC]
        SQ = FP[:, o_sq:o_sq + 2048]

        if 0 in phases:
            dma("sync", EYE, eye_d[:, :], "LD")
            dma("sync", MASK, mask_d[:, :], "LD")
            dma("sync", ONES[:, :], ones_d[:, :], "LD")
            dma("sync", BQKV[0:1, :], bqkv_d[:, :], "LD")
            dma("sync", BOUT[0:1, :], bout_d[:, :], "LD")
            dma("sync", EYEF, eyef_d[:, :], "LD")
            dma("sync", ONESF, onesf_d[:, :], "LD")
            sst = FP[:, o_sstage:o_sstage + 128]
            for (src_d, rows, dstap) in ((sg_d, JC, SGT), (su_d, JC, SUT), (sd_d, DC, SDT)):
                dma("sync", sst[:rows, :], src_d[:, :], "LD")
                wait("tensor", "LD", C["LD"])
                op("tensor", lambda t, r=rows: t.transpose(psB[:, 0:r], sst[:r, :], EYEF[:r, :r]), "TM")
                wait("scalar", "TM", C["TM"])
                op("scalar", lambda s, r=rows, dd=dstap: s.copy(dd[:, 0:r], psB[:, 0:r]), "AM")
                wait("tensor", "AM", C["AM"])
                wait("sync", "AM", C["AM"])

        if 2 in phases:
            pbf2 = DYN0
            def abf2(n):
                nonlocal pbf2
                o = pbf2; pbf2 += n
                assert pbf2 <= NBF, f"ph2 bf16 overflow {pbf2}"
                return o
            o_xt = abf2(DC * THALF)
            o_wqk = abf2(2 * DC * 128)
            o_wv = abf2(2 * DC * 512)
            o_qkst = abf2(2 * THALF)
            o_vst = abf2(2 * 512)
            o_wcvt = abf2(2 * 2048)
            XT = [BP[:, o_xt + dc * THALF: o_xt + (dc + 1) * THALF] for dc in range(DC)]
            WQK = [BP[:, o_wqk + i * DC * 128: o_wqk + (i + 1) * DC * 128] for i in range(2)]
            WVb = [BP[:, o_wv + i * DC * 512:o_wv + (i + 1) * DC * 512] for i in range(2)]
            QKST = [BP[:, o_qkst + i * THALF: o_qkst + (i + 1) * THALF] for i in range(2)]
            VST = [BP[:, o_vst + i * 512: o_vst + (i + 1) * 512] for i in range(2)]

            WCVT = [BP[:, o_wcvt + i * 2048:o_wcvt + (i + 1) * 2048] for i in range(2)]
            cvt_rel = [0, 0]
            cvt_jobs = []
            for (srcw, dstw) in ((wg_d, wgb_d), (wu_d, wub_d), (wd_d, wdb_d)):
                R = srcw.shape[0]
                for r0 in range(0, R, 2048):
                    for j in range(srcw.shape[1] // 128):
                        cvt_jobs.append((srcw, dstw, r0, j))
            wcix = 0
            for (srcw, dstw, r0, j) in cvt_jobs:
                wc = wcix % 2
                wcix += 1
                wait("gpsimd", "LC", C.get("LC", 0))
                dma("gpsimd", WCVT[wc].rearrange("p (r i) -> p r i", r=16),
                    srcw[r0:r0 + 2048, j * 128:(j + 1) * 128]
                    .rearrange("(r p) i -> p r i", p=128), "LC")
                wait("gpsimd", "LC", C["LC"])
                dma("gpsimd", dstw[r0:r0 + 2048, j * 128:(j + 1) * 128]
                    .rearrange("(r p) i -> p r i", p=128),
                    WCVT[wc].rearrange("p (r i) -> p r i", r=16), "LC")
            cvt_done = C["LC"]
            bank_rel = [{"AM": C["AM"], "VM": C["VM"]} for _ in range(4)]
            ech = 0
            qk_store_rel = [0, 0]
            v_store_rel = [0, 0]
            wqk_rel = [0, 0]
            wv_rel = [0, 0]
            xt_rel = 0

            for hf in range(NHALF):
                wait("sync", "TM", xt_rel)
                for dc in range(DC):
                    dma_t("sync", XT[dc], x_d[hf * THALF:(hf + 1) * THALF, dc * 128:(dc + 1) * 128], "LXT")
                ld_xt = C["LXT"]

                def load_wqk(f2):
                    wb2 = f2 % 2
                    wait("sync", "TM", wqk_rel[wb2])
                    dma("sync", WQK[wb2].rearrange("p (dc j) -> p dc j", dc=DC),
                        wqkv_d[:, f2 * 128:(f2 + 1) * 128].rearrange("(dc p) j -> p dc j", p=128),
                        f"LW{wb2}")
                    return C[f"LW{wb2}"]
                ld_wqk = [0] * (2 * DC)
                ld_wqk[0] = load_wqk(0)
                for f in range(2 * DC):
                    wb = f % 2
                    if f + 1 < 2 * DC:
                        ld_wqk[f + 1] = load_wqk(f + 1)
                    ld_w = ld_wqk[f]
                    qb = f % 2
                    wait("scalar", f"SQ{qb}", qk_store_rel[qb])
                    wait("vector", f"SQ{qb}", qk_store_rel[qb])
                    wait("tensor", "LXT", ld_xt)
                    wait("tensor", f"LW{wb}", ld_w)
                    for tb in range(THALF // 512):
                        b = (hf * 144 + f * 4 + tb) % 4
                        wait("tensor", "AM", bank_rel[b]["AM"])
                        wait("tensor", "VM", bank_rel[b]["VM"])
                        for k in range(DC):
                            op("tensor", lambda t, b=b, k=k, wb=wb, tb=tb:
                               t.matmul(psA[:, b * 512:(b + 1) * 512],
                                        WQK[wb][:, k * 128:(k + 1) * 128],
                                        XT[k][:, tb * 512:(tb + 1) * 512],
                                        start=(k == 0), stop=False))
                        op("tensor", lambda t, b=b, f=f:
                           t.matmul(psA[:, b * 512:(b + 1) * 512],
                                    BQKV[0:1, f * 128:(f + 1) * 128],
                                    ONES[0:1, 0:512], start=False, stop=True), "TM")
                        sc = INV_SQRT_HD if f < DC else 1.0
                        ech += 1
                        if ech % 2 == 0:
                            wait("scalar", "TM", C["TM"])
                            op("scalar", lambda s, b=b, qb=qb, tb=tb, sc=sc:
                               s.activation(QKST[qb][:, tb * 512:(tb + 1) * 512],
                                            psA[:, b * 512:(b + 1) * 512], AF.Copy, scale=sc), "AM")
                            bank_rel[b] = {"AM": C["AM"], "VM": 0}
                        else:
                            wait("vector", "TM", C["TM"])
                            op("vector", lambda v, b=b, qb=qb, tb=tb, sc=sc:
                               v.tensor_scalar(QKST[qb][:, tb * 512:(tb + 1) * 512],
                                               psA[:, b * 512:(b + 1) * 512], sc, None, ALU.mult), "VM")
                            bank_rel[b] = {"AM": 0, "VM": C["VM"]}
                    wqk_rel[wb] = C["TM"]
                    wait("sync", "AM", C["AM"])
                    wait("sync", "VM", C["VM"])
                    dst = (qts_d if f < DC else kts_d)[f % DC][:, hf * THALF:(hf + 1) * THALF]
                    dma("sync", dst, QKST[qb][:, :], f"SQ{qb}")
                    qk_store_rel[qb] = C[f"SQ{qb}"]

                def load_wv(vb2):
                    wait("sync", "TM", wv_rel[vb2 % 2])
                    dma("sync", WVb[vb2 % 2].rearrange("p (dc j) -> p dc j", dc=DC),
                        wqkv_d[:, 2 * D + vb2 * 512: 2 * D + (vb2 + 1) * 512]
                        .rearrange("(dc p) j -> p dc j", p=128), f"LV{vb2 % 2}")
                    return C[f"LV{vb2 % 2}"]
                ld_wvs = [0] * 4
                ld_wvs[0] = load_wv(0)
                for vb in range(4):
                    if vb + 1 < 4:
                        ld_wvs[vb + 1] = load_wv(vb + 1)
                    wait("tensor", f"LV{vb % 2}", ld_wvs[vb])
                    for tt in range(THALF // 128):
                        b = (vb * 16 + tt) % 4
                        vsb = tt % 2
                        wait("tensor", "AM", bank_rel[b]["AM"])
                        wait("tensor", "VM", bank_rel[b]["VM"])
                        for k in range(DC):
                            op("tensor", lambda t, b=b, k=k, tt=tt, WV=WVb[vb % 2]:
                               t.matmul(psA[:, b * 512:(b + 1) * 512],
                                        XT[k][:, tt * 128:(tt + 1) * 128],
                                        WV[:, k * 512:(k + 1) * 512],
                                        start=(k == 0), stop=False))
                        op("tensor", lambda t, b=b, vb=vb:
                           t.matmul(psA[:, b * 512:(b + 1) * 512],
                                    ONES[0:1, 0:128],
                                    BQKV[0:1, 2 * D + vb * 512: 2 * D + vb * 512 + 512],
                                    start=False, stop=True), "TM")
                        ech += 1
                        if ech % 2 == 0:
                            wait("scalar", "TM", C["TM"])
                            wait("scalar", f"SV{vsb}", v_store_rel[vsb])
                            op("scalar", lambda s, b=b, vsb=vsb:
                               s.copy(VST[vsb][:, :], psA[:, b * 512:(b + 1) * 512]), "AM")
                            bank_rel[b] = {"AM": C["AM"], "VM": 0}
                            wait("sync", "AM", C["AM"])
                        else:
                            wait("vector", "TM", C["TM"])
                            wait("vector", f"SV{vsb}", v_store_rel[vsb])
                            op("vector", lambda v, b=b, vsb=vsb:
                               v.tensor_copy(VST[vsb][:, :], psA[:, b * 512:(b + 1) * 512]), "VM")
                            bank_rel[b] = {"AM": 0, "VM": C["VM"]}
                            wait("sync", "VM", C["VM"])
                        r0 = hf * THALF + tt * 128
                        dma("sync", vs_d[r0:r0 + 128, vb * 512:(vb + 1) * 512], VST[vsb][:, :], f"SV{vsb}")
                        v_store_rel[vsb] = C[f"SV{vsb}"]
                    wv_rel[vb % 2] = C["TM"]
                xt_rel = C["TM"]
            ph2_TM, ph2_AM, ph2_VM, ph2_ST = C["TM"], C["AM"], C["VM"], C["ST"]
        else:
            ph2_TM = ph2_AM = ph2_VM = ph2_ST = 0

        if 3 in phases:
            pbf3 = DYN0
            def abf3(n):
                nonlocal pbf3
                o = pbf3; pbf3 += n
                assert pbf3 <= NBF
                return o
            o_qh = abf3(2 * SEQ)
            o_kh = abf3(2 * SEQ)
            o_vh = abf3(2 * SEQ)
            o_p = abf3(4 * 2048)
            o_pt = abf3(16 * 512)
            o_ast = abf3(2 * 512)
            QH = [BP[:, o_qh + i * SEQ:o_qh + (i + 1) * SEQ] for i in range(2)]
            KH = [BP[:, o_kh + i * SEQ:o_kh + (i + 1) * SEQ] for i in range(2)]
            VH = [BP[:, o_vh + i * SEQ:o_vh + (i + 1) * SEQ] for i in range(2)]
            PB = [BP[:, o_p + i * 2048:o_p + (i + 1) * 2048] for i in range(4)]
            PT = BP[:, o_pt:o_pt + 16 * 512]
            PT3 = PT.rearrange("p (kt j) -> p kt j", kt=16)
            AST = [BP[:, o_ast + i * 512:o_ast + (i + 1) * 512] for i in range(2)]
            NM = [FP[:, o_nm + i:o_nm + i + 1] for i in range(4)]
            NM4 = FP[:, o_nm4:o_nm4 + 16]
            SR1 = FP[0:1, o_srow:o_srow + 512]
            SR2 = FP[0:1, o_srow + 512:o_srow + 1024]
            RSB = FP[:, o_rsb:o_rsb + 512]

            wait("sync", "TM", ph2_TM)
            wait("sync", "AM", ph2_AM)
            if 2 in phases:
                for sn in ("SQ0", "SQ1", "SV0", "SV1"):
                    wait("sync", sn, C.get(sn, 0))

            qkv_rel = [0, 0]
            bank_rel = [{"AM": C["AM"], "VM": C["VM"]} for _ in range(4)]
            exp_done = [C["AM"]] * 4
            pt_am = C["AM"]
            pt_vm = C["VM"]
            av_done = C["TM"]
            o_evac = C["VM"]
            ast_rel = [0, 0]
            psT_rel = [{"AM": C["AM"], "VM": C["VM"]} for _ in range(2)]
            pb_rel = [C["TM"]] * 4
            r_evac = C["VM"]
            gchunk = 0

            pairs = [(s, h) for s in range(NSEQ) for h in range(H)]
            def emit_loads(idx):
                s, h = pairs[idx]
                hb = idx % 2
                wait("sync", "TM", qkv_rel[hb])
                dma("sync", QH[hb], qts_d[h][:, s * SEQ:(s + 1) * SEQ], f"LH{hb}")
                dma("sync", KH[hb], kts_d[h][:, s * SEQ:(s + 1) * SEQ], f"LH{hb}")
                dma("sync", VH[hb].rearrange("p (kt j) -> p kt j", kt=16),
                    vs_d[s * SEQ:(s + 1) * SEQ, h * 128:(h + 1) * 128]
                    .rearrange("(kt p) j -> p kt j", p=128), f"LH{hb}")
                return C[f"LH{hb}"]

            ld_done = [0] * len(pairs)
            ld_done[0] = emit_loads(0)
            pend_tr = []
            for idx, (s, h) in enumerate(pairs):
                hb = idx % 2
                if idx + 1 < len(pairs):
                    ld_done[idx + 1] = emit_loads(idx + 1)
                wait("tensor", f"LH{hb}", ld_done[idx])
                for qb in range(4):
                    nkt = 4 * (qb + 1)
                    N = 512 * (qb + 1)
                    wait("gpsimd", "TM", av_done)
                    wait("gpsimd", "AM", pt_am)
                    wait("gpsimd", "VM", pt_vm)
                    for qtl in range(3):
                        nz = 3 - qtl
                        qt = qb * 4 + qtl
                        op("gpsimd", lambda g, qt=qt, qtl=qtl, nz=nz:
                           g.memset(PT3[:, qt + 1:qt + 1 + nz, qtl * 128:(qtl + 1) * 128], 0.0), "GM")
                    gm_after = C["GM"]
                    wait("scalar", "TM", av_done)
                    wait("vector", "TM", av_done)
                    def emit_transposes(qt2, pi2, qtl2, exp_am):
                        nonlocal pt_am, pt_vm
                        wait("tensor", "AM", exp_am)
                        wait("tensor", "GM", gm_after)
                        ntr = qt2 + 1
                        for kt in range(ntr):
                            bkk = (kt // 8) % 2
                            if kt % 8 == 0:
                                wait("tensor", "AM", psT_rel[bkk]["AM"])
                                wait("tensor", "VM", psT_rel[bkk]["VM"])
                            lastt = (kt % 8 == 7 or kt == ntr - 1)
                            op("tensor", lambda t, kt=kt, pi2=pi2, bkk=bkk:
                               t.transpose(psT[bkk][:, (kt % 8) * 128:(kt % 8 + 1) * 128],
                                           PB[pi2][:, kt * 128:(kt + 1) * 128], EYE),
                               "TM" if lastt else None)
                            if lastt:
                                ntiles = kt % 8 + 1
                                k0 = kt - ntiles + 1
                                eng3 = "scalar" if bkk == 0 else "vector"
                                wait(eng3, "TM", C["TM"])
                                if eng3 == "scalar":
                                    op(eng3, lambda sC, k0=k0, ntiles=ntiles, qtl2=qtl2, bkk=bkk:
                                       sC.copy(PT3[:, k0:k0 + ntiles, qtl2 * 128:(qtl2 + 1) * 128],
                                               psT[bkk].rearrange("p (kt j) -> p kt j", kt=8)
                                               [:, 0:ntiles, :]), "AM")
                                    psT_rel[bkk] = {"AM": C["AM"], "VM": 0}
                                    pt_am = C["AM"]
                                else:
                                    op(eng3, lambda v, k0=k0, ntiles=ntiles, qtl2=qtl2, bkk=bkk:
                                       v.tensor_copy(PT3[:, k0:k0 + ntiles, qtl2 * 128:(qtl2 + 1) * 128],
                                                     psT[bkk].rearrange("p (kt j) -> p kt j", kt=8)
                                                     [:, 0:ntiles, :]), "VM")
                                    psT_rel[bkk] = {"AM": 0, "VM": C["VM"]}
                                    pt_vm = C["VM"]
                        pb_rel[pi2] = C["TM"]

                    for qtl in range(4):
                        qt = qb * 4 + qtl
                        pi = qt % 4
                        tile_banks = []
                        for c in range(qb + 1):
                            b = gchunk % 4
                            gchunk += 1
                            tile_banks.append((c, b))
                            diag = (c == qb)
                            wait("tensor", "AM", bank_rel[b]["AM"])
                            op("tensor", lambda t, c=c, qt=qt, hb=hb, b=b, diag=diag:
                               t.matmul(psA[:, b * 512:(b + 1) * 512],
                                        QH[hb][:, qt * 128:(qt + 1) * 128],
                                        KH[hb][:, c * 512:(c + 1) * 512],
                                        start=True, stop=not diag),
                               None if diag else "TM")
                            if diag:
                                op("tensor", lambda t, b=b, qtl=qtl:
                                   t.matmul(psA[:, b * 512:(b + 1) * 512],
                                            EYE, MASK[:, qtl * 512:(qtl + 1) * 512],
                                            start=False, stop=True), "TM")
                            mm_tm = C["TM"]
                            wait("vector", "TM", mm_tm)
                            if c == 0:
                                wait("vector", "AM", exp_done[pi])
                            op("vector", lambda v, pi=pi, c=c, b=b:
                               v.reduce_max(NM4[:, pi * 4 + c:pi * 4 + c + 1],
                                            psA[:, b * 512:(b + 1) * 512],
                                            axis=AX.X, negate=True),
                               "VM" if diag else None)
                        op("vector", lambda v, pi=pi, qb=qb:
                           v.tensor_reduce(NM[pi][:, :], NM4[:, pi * 4:pi * 4 + qb + 1],
                                           axis=AX.X, op=ALU.min), "VM")
                        wait("scalar", "VM", C["VM"])
                        wait("scalar", "TM", pb_rel[pi])
                        for c, b in tile_banks:
                            lastc = (c == qb)
                            op("scalar", lambda sC, pi=pi, c=c, b=b:
                               sC.activation(PB[pi][:, c * 512:(c + 1) * 512],
                                             psA[:, b * 512:(b + 1) * 512], AF.Exp,
                                             bias=NM[pi][:, :1], scale=1.0),
                               "AM" if lastc else None)
                        exp_done[pi] = C["AM"]
                        for c, b in tile_banks:
                            bank_rel[b] = {"AM": exp_done[pi], "VM": 0}
                        pend_tr.append((qt, pi, qtl, exp_done[pi]))
                        if len(pend_tr) > 3:
                            emit_transposes(*pend_tr.pop(0))
                    while pend_tr:
                        emit_transposes(*pend_tr.pop(0))
                    wait("tensor", "AM", pt_am)
                    wait("tensor", "VM", pt_vm)
                    wait("tensor", "GM", gm_after)
                    wait("tensor", "VM", r_evac)
                    for kt in range(nkt):
                        op("tensor", lambda t, kt=kt, nkt=nkt:
                           t.matmul(psC[0:1, :], ONES[:, 0:1],
                                    PT[:, kt * 512:(kt + 1) * 512],
                                    start=(kt == 0), stop=(kt == nkt - 1)),
                           "TM" if kt == nkt - 1 else None)
                    sums_tm = C["TM"]
                    wait("scalar", "TM", sums_tm)
                    op("scalar", lambda sC: sC.activation(SR2[:, :], psC[0:1, :], AF.Ln), "AM")
                    op("scalar", lambda sC: sC.activation(SR1[:, :], SR2[:, :], AF.Exp, scale=-1.0), "AM")
                    srow_am = C["AM"]
                    wait("tensor", "AM", srow_am)
                    op("tensor", lambda t:
                       t.matmul(psC[:, :], ONESF[0:1, 0:128], SR1[:, :],
                                start=True, stop=True), "TM")
                    bcast_tm = C["TM"]
                    wait("vector", "TM", bcast_tm)
                    op("vector", lambda v: v.tensor_copy(RSB[:, :], psC[:, :]), "VM")
                    r_evac = C["VM"]
                    wait("tensor", "VM", o_evac)
                    for kt in range(nkt):
                        op("tensor", lambda t, kt=kt, hb=hb, nkt=nkt:
                           t.matmul(psB[:, :], VH[hb][:, kt * 128:(kt + 1) * 128],
                                    PT[:, kt * 512:(kt + 1) * 512],
                                    start=(kt == 0), stop=(kt == nkt - 1)),
                           "TM" if kt == nkt - 1 else None)
                    av_done = C["TM"]
                    ab = qb % 2
                    wait("vector", "TM", av_done)
                    wait("vector", "VM", r_evac)
                    wait("vector", f"SA{ab}", ast_rel[ab])
                    op("vector", lambda v, ab=ab:
                       v.tensor_mul(AST[ab][:, :], psB[:, :], RSB[:, :]), "VM")
                    o_evac = C["VM"]
                    wait("sync", "VM", o_evac)
                    dma("sync", atts_d[h][:, s * SEQ + qb * 512: s * SEQ + (qb + 1) * 512],
                        AST[ab][:, :], f"SA{ab}")
                    ast_rel[ab] = C[f"SA{ab}"]
                qkv_rel[hb] = C["TM"]
            ph3_TM, ph3_AM, ph3_VM, ph3_ST = C["TM"], C["AM"], C["VM"], C["ST"]
        else:
            ph3_TM, ph3_AM, ph3_VM, ph3_ST = ph2_TM, ph2_AM, ph2_VM, ph2_ST


        if 4 in phases:
            pbf4 = DYN0
            def abf4(n):
                nonlocal pbf4
                o = pbf4; pbf4 += n
                assert pbf4 <= NBF
                return o
            o_attb = abf4(2 * DC * 512)
            o_wo = abf4(2 * DC * 512)
            o_xb = abf4(2 * 4 * D)
            o_r1 = abf4(4 * D)
            o_x1t = abf4(D)
            o_x1f = abf4(2 * D)
            ATTB = [BP[:, o_attb + i * DC * 512:o_attb + (i + 1) * DC * 512] for i in range(2)]
            WO = [BP[:, o_wo + i * DC * 512:o_wo + (i + 1) * DC * 512] for i in range(2)]
            XB = [BP[:, o_xb + i * 4 * D:o_xb + (i + 1) * 4 * D] for i in range(2)]
            R1 = [BP[:, o_r1 + tt * D:o_r1 + (tt + 1) * D] for tt in range(4)]
            X1T = BP[:, o_x1t:o_x1t + D]
            X1F = [BP[:, o_x1f + i * D:o_x1f + (i + 1) * D] for i in range(2)]
            COLS = FP[:, o_cols:o_cols + 32]

            wait("sync", "TM", ph3_TM)
            wait("sync", "AM", ph3_AM)
            wait("sync", "VM", ph3_VM)
            if 3 in phases:
                for sn in ("SA0", "SA1"):
                    wait("sync", sn, C.get(sn, 0))
            wait("scalar", "LC", cvt_done if 2 in phases else 0)
            wait("vector", "LC", cvt_done if 2 in phases else 0)

            attb_rel = [0, 0]
            wo_rel = [0, 0]
            xb_rel = [0, 0]
            bank_rel4 = [{"AM": C["AM"], "VM": C["VM"]} for _ in range(4)]
            x1f_rel = [0, 0]
            psT_rel4 = [C["AM"], C["AM"]]
            x1t_rel = C["TM"]
            r1_rel = C["AM"]
            cols_rel = [C["VM"], C["VM"]]

            for tb in range(8):
                bb = tb % 2
                wait("sync", "TM", attb_rel[bb])
                dma("sync", ATTB[bb].rearrange("p (dc j) -> p dc j", dc=DC),
                    atts_d[:, :, tb * 512:(tb + 1) * 512].rearrange("dc p j -> p dc j"), f"LT{bb}")
                wait("sync", "VM", xb_rel[bb])
                dma("sync", XB[bb].rearrange("p (tt j) -> p tt j", tt=4),
                    x_d[tb * 512:(tb + 1) * 512, :].rearrange("(tt p) j -> p tt j", p=128), f"LT{bb}")
                ld_tb = C[f"LT{bb}"]
                def load_wo(db2):
                    wb2 = db2 % 2
                    wait("sync", "TM", wo_rel[wb2])
                    dma("sync", WO[wb2].rearrange("p (dc j) -> p dc j", dc=DC),
                        wout_d[:, db2 * 512:(db2 + 1) * 512].rearrange("(dc p) j -> p dc j", p=128),
                        f"LO{wb2}")
                    return C[f"LO{wb2}"]
                if tb == 0:
                    ld_wo = [0] * 4
                    ld_wo[0] = load_wo(0)
                for db in range(4):
                    wb = db % 2
                    if db + 1 < 4:
                        ld_wo[db + 1] = load_wo(db + 1)
                    elif tb < 7:
                        ld_wo[0] = load_wo(0)
                    ld_w = ld_wo[db]
                    wait("tensor", f"LT{bb}", ld_tb)
                    wait("tensor", f"LO{db % 2}", ld_w)
                    for tt in range(4):
                        b = (tb * 16 + db * 4 + tt) % 4
                        wait("tensor", "AM", bank_rel4[b]["AM"])
                        wait("tensor", "VM", bank_rel4[b]["VM"])
                        for k in range(DC):
                            op("tensor", lambda t, b=b, k=k, bb=bb, wb=wb, tt=tt:
                               t.matmul(psA[:, b * 512:(b + 1) * 512],
                                        ATTB[bb][:, k * 512 + tt * 128: k * 512 + (tt + 1) * 128],
                                        WO[wb][:, k * 512:(k + 1) * 512],
                                        start=(k == 0), stop=False))
                        op("tensor", lambda t, b=b, db=db:
                           t.matmul(psA[:, b * 512:(b + 1) * 512],
                                    ONES[0:1, 0:128], BOUT[0:1, db * 512:(db + 1) * 512],
                                    start=False, stop=True), "TM")
                        wait("vector", "TM", C["TM"])
                        wait("vector", "AM", r1_rel)
                        op("vector", lambda v, b=b, bb=bb, tt=tt, db=db:
                           v.tensor_add(R1[tt][:, db * 512:(db + 1) * 512],
                                        psA[:, b * 512:(b + 1) * 512],
                                        XB[bb][:, tt * D + db * 512: tt * D + (db + 1) * 512]), "VM")
                        bank_rel4[b] = {"AM": C["AM"], "VM": C["VM"]}
                    wo_rel[wb] = C["TM"]
                attb_rel[bb] = C["TM"]
                xb_rel[bb] = C["VM"]
                vm_r1 = C["VM"]
                for tt in range(4):
                    par = tt % 2
                    cs = par * 16
                    S1 = COLS[:, cs + 0:cs + 1]; S2 = COLS[:, cs + 1:cs + 2]
                    M = COLS[:, cs + 2:cs + 3]; MS = COLS[:, cs + 3:cs + 4]
                    M2 = COLS[:, cs + 4:cs + 5]; VAR = COLS[:, cs + 5:cs + 6]
                    SD = COLS[:, cs + 6:cs + 7]; RSTD = COLS[:, cs + 7:cs + 8]
                    NMR = COLS[:, cs + 8:cs + 9]
                    wait("vector", "VM", cols_rel[par])
                    op("vector", lambda v, tt=tt, S1=S1: v.reduce_sum(S1, R1[tt][:, :], axis=AX.X), "VM")
                    wait("scalar", "VM", C["VM"])
                    op("scalar", lambda sC, tt=tt:
                       sC.activation(SQ[:, :], R1[tt][:, :], AF.Square), "AM")
                    op("vector", lambda v, S1=S1, M=M: v.tensor_scalar(M, S1, 1.0 / D, None, ALU.mult), "VM")
                    wait("vector", "AM", C["AM"])
                    op("vector", lambda v, S2=S2: v.reduce_sum(S2, SQ[:, :], axis=AX.X), "VM")
                    op("vector", lambda v, S2=S2, MS=MS: v.tensor_scalar(MS, S2, 1.0 / D, None, ALU.mult), "VM")
                    op("vector", lambda v, M=M, M2=M2: v.tensor_mul(M2, M, M), "VM")
                    op("vector", lambda v, MS=MS, M2=M2, VAR=VAR: v.tensor_sub(VAR, MS, M2), "VM")
                    op("vector", lambda v, VAR=VAR: v.tensor_scalar(VAR, VAR, EPS, None, ALU.add), "VM")
                    wait("scalar", "VM", C["VM"])
                    op("scalar", lambda sC, VAR=VAR, SD=SD:
                       sC.activation(SD, VAR, AF.Sqrt), "AM")
                    wait("vector", "AM", C["AM"])
                    op("vector", lambda v, SD=SD, RSTD=RSTD: v.reciprocal(RSTD, SD), "VM")
                    op("vector", lambda v, M=M, RSTD=RSTD, NMR=NMR:
                       v.scalar_tensor_tensor(NMR, M, -1.0, RSTD, ALU.mult, ALU.mult), "VM")
                    wait("vector", "TM", x1t_rel)
                    op("vector", lambda v, tt=tt, RSTD=RSTD, NMR=NMR:
                       v.tensor_scalar(X1T[:, :], R1[tt][:, :], RSTD[:, :1], NMR[:, :1],
                                       ALU.mult, ALU.add), "VM")
                    cols_rel[par] = C["VM"]
                    vm_x1t = C["VM"]
                    xf = tt % 2
                    wait("tensor", "VM", vm_x1t)
                    wait("tensor", f"SX{xf}", x1f_rel[xf])
                    for dc in range(DC):
                        bkk = (dc // 8) % 2
                        if dc % 8 == 0:
                            wait("tensor", "AM", psT_rel4[bkk])
                        op("tensor", lambda t, dc=dc, bkk=bkk:
                           t.transpose(psT[bkk][:, (dc % 8) * 128:(dc % 8 + 1) * 128],
                                       X1T[:, dc * 128:(dc + 1) * 128], EYE),
                           "TM" if dc % 8 == 7 else None)
                        if dc % 8 == 7:
                            wait("scalar", "TM", C["TM"])
                            op("scalar", lambda sC, dc=dc, bkk=bkk, xf=xf:
                               sC.copy(X1F[xf][:, (dc - 7) * 128:(dc + 1) * 128],
                                       psT[bkk][:, :]), "AM")
                            psT_rel4[bkk] = C["AM"]
                    x1t_rel = C["TM"]
                    wait("sync", "AM", C["AM"])
                    dma("sync", x1s_d[:, :, tb * 512 + tt * 128: tb * 512 + (tt + 1) * 128]
                        .rearrange("dc p j -> p dc j"),
                        X1F[xf].rearrange("p (dc j) -> p dc j", dc=DC), f"SX{xf}")
                    x1f_rel[xf] = C[f"SX{xf}"]
                r1_rel = C["AM"]
            ph4_TM, ph4_AM, ph4_VM, ph4_ST = C["TM"], C["AM"], C["VM"], C["ST"]
        else:
            ph4_TM, ph4_AM, ph4_VM, ph4_ST = ph3_TM, ph3_AM, ph3_VM, ph3_ST

        if 5 in phases:
            pbf5 = DYN0
            def abf5(n):
                nonlocal pbf5
                o = pbf5; pbf5 += n
                assert pbf5 <= NBF, f"ph5 bf16 overflow {pbf5}"
                return o
            o_fuse = abf5(JC * 512)
            o_x1 = abf5(DC * 512)
            o_wg = abf5(2 * DC * 128)
            o_wu = abf5(2 * DC * 128)
            o_wd = abf5(2 * JC * 128)
            o_gt = abf5(2 * 512)
            o_ut = abf5(2 * 512)
            FUSE = BP[:, o_fuse:o_fuse + JC * 512]
            X1 = BP[:, o_x1:o_x1 + DC * 512]
            WG = [BP[:, o_wg + i * DC * 128:o_wg + (i + 1) * DC * 128] for i in range(2)]
            WU = [BP[:, o_wu + i * DC * 128:o_wu + (i + 1) * DC * 128] for i in range(2)]
            WD = [BP[:, o_wd + i * JC * 128:o_wd + (i + 1) * JC * 128] for i in range(2)]
            GT = [BP[:, o_gt + i * 512:o_gt + (i + 1) * 512] for i in range(2)]
            UT = [BP[:, o_ut + i * 512:o_ut + (i + 1) * 512] for i in range(2)]
            pf5 = FDYN0
            def af5(n):
                nonlocal pf5
                o = pf5; pf5 += n
                assert pf5 <= NF32, f"ph5 f32 overflow {pf5}"
                return o
            o_r2 = af5(2 * 512)
            o_dt = af5(2 * 512)
            o_sqf = af5(2 * 512)
            o_s1b = af5(512)
            o_s2b = af5(512)
            o_tmp = af5(512)
            o_tmp2 = af5(512)
            o_ostg = af5(2 * 512)
            R2 = [FP[:, o_r2 + i * 512:o_r2 + (i + 1) * 512] for i in range(2)]
            DT = [FP[:, o_dt + i * 512:o_dt + (i + 1) * 512] for i in range(2)]
            SQF = [FP[:, o_sqf + i * 512:o_sqf + (i + 1) * 512] for i in range(2)]
            S1B = FP[:, o_s1b:o_s1b + 512]
            S2B = FP[:, o_s2b:o_s2b + 512]
            TMP = FP[:, o_tmp:o_tmp + 512]
            TMP2 = FP[:, o_tmp2:o_tmp2 + 512]
            OSTG = [FP[:, o_ostg + i * 512:o_ostg + (i + 1) * 512] for i in range(2)]
            PRMC = FP[:, o_prmc:o_prmc + 64]

            wait("sync", "TM", ph4_TM)
            wait("sync", "AM", ph4_AM)
            wait("sync", "VM", ph4_VM)
            wait("sync", "LC", cvt_done if 2 in phases else 0)
            wait("gpsimd", "LC", cvt_done if 2 in phases else 0)
            wait("gpsimd", "TM", ph4_TM)
            wait("gpsimd", "AM", ph4_AM)
            wait("gpsimd", "VM", ph4_VM)

            wg_rel = [0, 0]
            wd_rel = [0, 0]
            gtu_rel = [0, 0]
            gbank_rel = [C["AM"]] * 4
            fuse_rel = 0
            x1_rel_tm = 0
            x1_rel_vm = 0
            x1_rel_am = 0
            dn_evac = [C["AM"], C["AM"]]
            r2_rel_tm = [C["TM"], C["TM"]]
            r2_rel_am = [C["AM"], C["AM"]]
            dt_rel = [C["VM"], C["VM"]]
            sqf_rel = [C["TM"], C["TM"]]
            trb_rel = [C["AM"], C["AM"]]
            s12_rel = [C["VM"], C["VM"]]
            ostg_rel = [0, 0]
            prm_rel = 0

            for sl in range(8):
                if 4 in phases:
                    for sn in ("SX0", "SX1"):
                        wait("sync", sn, C.get(sn, 0))
                wait("sync", "TM", x1_rel_tm)
                wait("sync", "VM", x1_rel_vm)
                dma("sync", X1.rearrange("p (dc j) -> p dc j", dc=DC),
                    x1s_d[:, :, sl * 512:(sl + 1) * 512].rearrange("dc p j -> p dc j"), "LX")
                ld_x1 = C["LX"]
                for j in range(JC):
                    wb = j % 2
                    wait("sync", "TM", wg_rel[wb])
                    dma("sync", WG[wb].rearrange("p (dc i) -> p dc i", dc=DC),
                        wgb_d[:, j * 128:(j + 1) * 128].rearrange("(dc p) i -> p dc i", p=128), f"LWG{wb}")
                    dma("sync", WU[wb].rearrange("p (dc i) -> p dc i", dc=DC),
                        wub_d[:, j * 128:(j + 1) * 128].rearrange("(dc p) i -> p dc i", p=128), f"LWG{wb}")
                    ld_w = C[f"LWG{wb}"]
                    bg = (2 * j) % 4
                    bu = (2 * j + 1) % 4
                    wait("tensor", "LX", ld_x1)
                    wait("tensor", f"LWG{wb}", ld_w)
                    wait("tensor", "AM", max(gbank_rel[bg], gbank_rel[bu]))
                    for k in range(DC):
                        op("tensor", lambda t, bg=bg, k=k, wb=wb:
                           t.matmul(psA[:, bg * 512:(bg + 1) * 512],
                                    WG[wb][:, k * 128:(k + 1) * 128],
                                    X1[:, k * 512:(k + 1) * 512],
                                    start=(k == 0), stop=(k == DC - 1)),
                           "TM" if k == DC - 1 else None)
                    for k in range(DC):
                        op("tensor", lambda t, bu=bu, k=k, wb=wb:
                           t.matmul(psA[:, bu * 512:(bu + 1) * 512],
                                    WU[wb][:, k * 128:(k + 1) * 128],
                                    X1[:, k * 512:(k + 1) * 512],
                                    start=(k == 0), stop=(k == DC - 1)),
                           "TM" if k == DC - 1 else None)
                    wg_rel[wb] = C["TM"]
                    gp = j % 2
                    wait("scalar", "TM", C["TM"])
                    wait("scalar", "VM", gtu_rel[gp])
                    op("scalar", lambda sC, bg=bg, j=j, gp=gp:
                       sC.activation(GT[gp][:, :], psA[:, bg * 512:(bg + 1) * 512], AF.Silu,
                                     scale=SGT[:, j:j + 1]), None)
                    op("scalar", lambda sC, bu=bu, j=j, gp=gp:
                       sC.activation(UT[gp][:, :], psA[:, bu * 512:(bu + 1) * 512], AF.Copy,
                                     scale=SUT[:, j:j + 1]), "AM")
                    gbank_rel[bg] = C["AM"]
                    gbank_rel[bu] = C["AM"]
                    wait("vector", "AM", C["AM"])
                    wait("vector", "TM", fuse_rel)
                    op("vector", lambda v, j=j, gp=gp:
                       v.tensor_mul(FUSE[:, j * 512:(j + 1) * 512], GT[gp][:, :], UT[gp][:, :]), "VM")
                    gtu_rel[gp] = C["VM"]
                vm_fuse = C["VM"]
                def emit_tail(dout, dnb, vm_r2, am_sq):
                    nonlocal s12_rel, sqf_rel, r2_rel_tm, r2_rel_am, trb_rel, ostg_rel
                    wait("tensor", "VM", vm_r2)
                    wait("tensor", "VM", s12_rel[0])
                    op("tensor", lambda t, dnb=dnb:
                       t.matmul(psA[:, 2 * 512:3 * 512], ONESF, R2[dnb][:, :],
                                start=True, stop=True), "TM")
                    tm_s1 = C["TM"]
                    wait("tensor", "AM", am_sq)
                    wait("tensor", "VM", s12_rel[1])
                    op("tensor", lambda t, dnb=dnb:
                       t.matmul(psA[:, 3 * 512:4 * 512], ONESF, SQF[dnb][:, :],
                                start=True, stop=True), "TM")
                    tm_s2 = C["TM"]
                    sqf_rel[dnb] = tm_s2
                    wait("vector", "TM", tm_s1)
                    if dout == 0:
                        op("vector", lambda v: v.tensor_copy(S1B[:, :], psA[:, 2 * 512:3 * 512]), "VM")
                    else:
                        op("vector", lambda v: v.tensor_add(S1B[:, :], S1B[:, :], psA[:, 2 * 512:3 * 512]), "VM")
                    s12_rel[0] = C["VM"]
                    wait("vector", "TM", tm_s2)
                    if dout == 0:
                        op("vector", lambda v: v.tensor_copy(S2B[:, :], psA[:, 3 * 512:4 * 512]), "VM")
                    else:
                        op("vector", lambda v: v.tensor_add(S2B[:, :], S2B[:, :], psA[:, 3 * 512:4 * 512]), "VM")
                    s12_rel[1] = C["VM"]
                    bt = dout % 2
                    wait("tensor", "AM", trb_rel[bt])
                    for tt in range(4):
                        op("tensor", lambda t, tt=tt, dnb=dnb, bt=bt:
                           t.transpose(psA[:, bt * 512 + tt * 128: bt * 512 + (tt + 1) * 128],
                                       R2[dnb][:, tt * 128:(tt + 1) * 128], EYEF),
                           "TM" if tt == 3 else None)
                    r2_rel_tm[dnb] = C["TM"]
                    r2_rel_am[dnb] = am_sq
                    ob = dout % 2
                    wait("scalar", "TM", C["TM"])
                    wait("scalar", f"SR{ob}", ostg_rel[ob])
                    op("scalar", lambda sC, bt=bt, ob=ob:
                       sC.copy(OSTG[ob][:, :], psA[:, bt * 512:(bt + 1) * 512]), "AM")
                    trb_rel[bt] = C["AM"]
                    wait("sync", "AM", C["AM"])
                    dma("sync",
                        r2s_d[sl * 512:(sl + 1) * 512, dout * 128:(dout + 1) * 128]
                        .rearrange("(tt p) j -> p tt j", p=128),
                        OSTG[ob].rearrange("p (tt j) -> p tt j", tt=4), f"SR{ob}")
                    ostg_rel[ob] = C[f"SR{ob}"]

                pend_tail = None
                for dout in range(DC):
                    wb = dout % 2
                    wait("gpsimd", "TM", wd_rel[wb])
                    dma("gpsimd", WD[wb].rearrange("p (jc i) -> p jc i", jc=JC),
                        wdb_d[:, dout * 128:(dout + 1) * 128].rearrange("(jc p) i -> p jc i", p=128),
                        f"LWD{wb}")
                    ld_w = C[f"LWD{wb}"]
                    dnb = dout % 2
                    ps_dn = psB if dnb == 0 else psC
                    wait("tensor", f"LWD{dout % 2}", ld_w)
                    wait("tensor", "VM", vm_fuse)
                    wait("tensor", "AM", dn_evac[dnb])
                    for jc in range(JC):
                        op("tensor", lambda t, jc=jc, wb=wb, ps_dn=ps_dn:
                           t.matmul(ps_dn[:, :], WD[wb][:, jc * 128:(jc + 1) * 128],
                                    FUSE[:, jc * 512:(jc + 1) * 512],
                                    start=(jc == 0), stop=(jc == JC - 1)),
                           "TM" if jc == JC - 1 else None)
                    wd_rel[wb] = C["TM"]
                    dn_done = C["TM"]
                    wait("scalar", "TM", dn_done)
                    wait("scalar", "VM", dt_rel[dnb])
                    op("scalar", lambda sC, dnb=dnb, dout=dout, ps_dn=ps_dn:
                       sC.activation(DT[dnb][:, :], ps_dn[:, :], AF.Copy,
                                     scale=SDT[:, dout:dout + 1]), "AM")
                    dn_evac[dnb] = C["AM"]
                    wait("vector", "AM", dn_evac[dnb])
                    wait("vector", "TM", r2_rel_tm[dnb])
                    wait("vector", "AM", r2_rel_am[dnb])
                    op("vector", lambda v, dnb=dnb, dout=dout:
                       v.tensor_add(R2[dnb][:, :], DT[dnb][:, :], X1[:, dout * 512:(dout + 1) * 512]), "VM")
                    dt_rel[dnb] = C["VM"]
                    vm_r2 = C["VM"]
                    wait("scalar", "VM", vm_r2)
                    wait("scalar", "TM", sqf_rel[dnb])
                    op("scalar", lambda sC, dnb=dnb:
                       sC.activation(SQF[dnb][:, :], R2[dnb][:, :], AF.Square), "AM")
                    am_sq = C["AM"]
                    if pend_tail is not None:
                        emit_tail(*pend_tail)
                    pend_tail = (dout, dnb, vm_r2, am_sq)
                if pend_tail is not None:
                    emit_tail(*pend_tail)
                    pend_tail = None
                if debug:
                    wait("sync", "VM", C["VM"])
                    dma("sync", s1b_d[sl], S1B[:, :], "ST")
                    dma("sync", s2b_d[sl], S2B[:, :], "ST")
                    wait("vector", "ST", C["ST"])
                wait("vector", "AM", prm_rel)
                op("vector", lambda v: v.tensor_scalar(S1B[:, :], S1B[:, :], 1.0 / D, None, ALU.mult), "VM")
                op("vector", lambda v: v.tensor_scalar(S2B[:, :], S2B[:, :], 1.0 / D, None, ALU.mult), "VM")
                op("vector", lambda v: v.tensor_mul(TMP[:, :], S1B[:, :], S1B[:, :]), "VM")
                op("vector", lambda v: v.tensor_sub(S2B[:, :], S2B[:, :], TMP[:, :]), "VM")
                op("vector", lambda v: v.tensor_scalar(S2B[:, :], S2B[:, :], EPS, None, ALU.add), "VM")
                wait("scalar", "VM", C["VM"])
                op("scalar", lambda sC: sC.activation(TMP[:, :], S2B[:, :], AF.Sqrt), "AM")
                wait("vector", "AM", C["AM"])
                op("vector", lambda v: v.reciprocal(TMP2[:, :], TMP[:, :]), "VM")
                op("vector", lambda v:
                   v.scalar_tensor_tensor(S1B[:, :], S1B[:, :], -1.0, TMP2[:, :],
                                          ALU.mult, ALU.mult), "VM")
                vm_prm = C["VM"]
                wait("tensor", "VM", vm_prm)
                wait("tensor", "AM", trb_rel[0])
                for tt in range(4):
                    gt_ = sl * 4 + tt
                    op("tensor", lambda t, tt=tt:
                       t.transpose(psA[:, tt * 2:tt * 2 + 1], TMP2[0:1, tt * 128:(tt + 1) * 128],
                                   EYEF[:1, :1]), None)
                    op("tensor", lambda t, tt=tt:
                       t.transpose(psA[:, tt * 2 + 1:tt * 2 + 2], S1B[0:1, tt * 128:(tt + 1) * 128],
                                   EYEF[:1, :1]),
                       "TM" if tt == 3 else None)
                tm_prm = C["TM"]
                wait("scalar", "TM", tm_prm)
                for tt in range(4):
                    gt_ = sl * 4 + tt
                    op("scalar", lambda sC, tt=tt, gt_=gt_:
                       sC.copy(PRMC[:, gt_:gt_ + 1], psA[:, tt * 2:tt * 2 + 1]), None)
                    op("scalar", lambda sC, tt=tt, gt_=gt_:
                       sC.copy(PRMC[:, 32 + gt_:32 + gt_ + 1], psA[:, tt * 2 + 1:tt * 2 + 2]),
                       "AM" if tt == 3 else None)
                prm_rel = C["AM"]
                trb_rel[0] = C["AM"]
                gbank_rel = [C["AM"]] * 4
                fuse_rel = C["TM"]
                x1_rel_tm = C["TM"]
                x1_rel_vm = C["VM"]
            ph5_TM, ph5_AM, ph5_VM, ph5_ST = C["TM"], C["AM"], C["VM"], C["ST"]

            pf6 = FDYN0
            def af6(n):
                nonlocal pf6
                o = pf6; pf6 += n
                assert pf6 <= NF32
                return o
            o_rb = af6(2 * 2048)
            o_ob = af6(2 * 2048)
            RB = [FP[:, o_rb + i * 2048:o_rb + (i + 1) * 2048] for i in range(2)]
            OB = [FP[:, o_ob + i * 2048:o_ob + (i + 1) * 2048] for i in range(2)]
            wait("sync", "TM", ph5_TM)
            wait("sync", "AM", ph5_AM)
            wait("sync", "VM", ph5_VM)
            for sn in ("SR0", "SR1"):
                wait("sync", sn, C.get(sn, 0))
            rb_rel = [0, 0]
            ob_rel = [0, 0]
            for rb_i in range(T // 128):
                bi = rb_i % 2
                wait("sync", "VM", rb_rel[bi])
                dma("sync", RB[bi][:, :], r2s_d[rb_i * 128:(rb_i + 1) * 128, :], f"LR{bi}")
                ld_rb = C[f"LR{bi}"]
                wait("vector", f"LR{bi}", ld_rb)
                wait("vector", f"SO{bi}", ob_rel[bi])
                op("vector", lambda v, bi=bi, rb_i=rb_i:
                   v.tensor_scalar(OB[bi][:, :], RB[bi][:, :], PRMC[:, rb_i:rb_i + 1],
                                   PRMC[:, 32 + rb_i:32 + rb_i + 1],
                                   ALU.mult, ALU.add), "VM")
                rb_rel[bi] = C["VM"]
                wait("sync", "VM", C["VM"])
                dma("sync", out_d[rb_i * 128:(rb_i + 1) * 128, :], OB[bi][:, :], f"SO{bi}")
                ob_rel[bi] = C[f"SO{bi}"]

        if debug and 5 in phases:
            wait("sync", "AM", C["AM"])
            dma("sync", prm_d[:, :], FP[:, o_prmc:o_prmc + 64], "ST")
        for sname in sorted(list(C.keys())):
            if sname in ("TM", "VM", "AM", "GM"):
                continue
            wait("sync", sname, C[sname])

        @block.sync
        def _(eng):
            for f in Q["sync"]:
                f(eng)

        @block.gpsimd
        def _(eng):
            for f in Q["gpsimd"]:
                f(eng)

        @block.tensor
        def _(eng):
            for f in Q["tensor"]:
                f(eng)

        @block.vector
        def _(eng):
            for f in Q["vector"]:
                f(eng)

        @block.scalar
        def _(eng):
            for f in Q["scalar"]:
                f(eng)

    return nc


def _causal_mask():
    m = np.zeros((128, 2048), dtype=np.float32)
    p = np.arange(128)[:, None]
    for r in range(4):
        j = np.arange(512)[None, :]
        m[:, r * 512:(r + 1) * 512] = np.where(j > r * 128 + p, -1e30, 0.0)
    return m.astype(ml_dtypes.bfloat16)


def make_core_inputs(core, inputs):
    X = np.asarray(inputs["L0_attn_x"])
    x_c = np.ascontiguousarray(X[2 * core:2 * core + 2].reshape(T, D))
    return {
        "x": x_c,
        "wqkv": np.ascontiguousarray(inputs["L0_attn_W_qkv"]),
        "bqkv": np.ascontiguousarray(np.asarray(inputs["L0_attn_b_qkv"]).reshape(1, TD)),
        "wout": np.ascontiguousarray(inputs["L0_attn_W_out"]),
        "bout": np.ascontiguousarray(np.asarray(inputs["L0_attn_b_out"]).reshape(1, D)),
        "wg": np.ascontiguousarray(inputs["L0_ffn_w_gate"]),
        "wu": np.ascontiguousarray(inputs["L0_ffn_w_up"]),
        "wd": np.ascontiguousarray(inputs["L0_ffn_w_down"]),
        "sg": np.ascontiguousarray(np.asarray(inputs["L0_ffn_s_gate"], dtype=np.float32).reshape(JC, 128)),
        "su": np.ascontiguousarray(np.asarray(inputs["L0_ffn_s_up"], dtype=np.float32).reshape(JC, 128)),
        "sd": np.ascontiguousarray(np.asarray(inputs["L0_ffn_s_down"], dtype=np.float32).reshape(DC, 128)),
        "eye": np.eye(128, dtype=ml_dtypes.bfloat16),
        "eyef": np.eye(128, dtype=np.float32),
        "onesf": np.ones((128, 128), dtype=np.float32),
        "mask": _causal_mask(),
        "ones": np.ones((128, 512), dtype=ml_dtypes.bfloat16),
    }


def kernel(**inputs):
    nc = build_core_program()
    in_maps = [make_core_inputs(c, inputs) for c in range(NCORES)]
    res = run_bass_kernel_spmd(nc, in_maps, core_ids=list(range(NCORES)))
    out = np.concatenate([r["out"].reshape(NSEQ, SEQ, D) for r in res.results], axis=0)
    return out.astype(np.float32)


# version 07
# speedup vs baseline: 1.1375x; 1.0125x over previous
import numpy as np
import ml_dtypes
import concourse.bass as bass
import concourse.mybir as mybir
from concourse.bass_utils import run_bass_kernel_spmd
from contextlib import ExitStack

BF16 = mybir.dt.bfloat16
F32 = mybir.dt.float32
I8 = mybir.dt.int8
AF = mybir.ActivationFunctionType
ALU = mybir.AluOpType
AX = mybir.AxisListType

NCORES = 8
SEQ = 2048
NSEQ = 2
T = SEQ * NSEQ
D = 2048
DC = D // 128
H = 16
TD = 3 * D
INTER = 8192
JC = INTER // 128
NHALF = 2
THALF = T // NHALF
INV_SQRT_HD = 1.0 / float(np.float32(ml_dtypes.bfloat16(np.sqrt(128.0))))
EPS = 1e-5


def build_core_program(phases=(0, 2, 3, 4, 5), debug=False):
    nc = bass.Bass("TRN2", target_bir_lowering=False)

    x_d = nc.dram_tensor("x", [T, D], BF16, kind="ExternalInput")
    wqkv_d = nc.dram_tensor("wqkv", [D, TD], BF16, kind="ExternalInput")
    bqkv_d = nc.dram_tensor("bqkv", [1, TD], BF16, kind="ExternalInput")
    wout_d = nc.dram_tensor("wout", [D, D], BF16, kind="ExternalInput")
    bout_d = nc.dram_tensor("bout", [1, D], BF16, kind="ExternalInput")
    wg_d = nc.dram_tensor("wg", [D, INTER], I8, kind="ExternalInput")
    wu_d = nc.dram_tensor("wu", [D, INTER], I8, kind="ExternalInput")
    wd_d = nc.dram_tensor("wd", [INTER, D], I8, kind="ExternalInput")
    sg_d = nc.dram_tensor("sg", [JC, 128], F32, kind="ExternalInput")
    su_d = nc.dram_tensor("su", [JC, 128], F32, kind="ExternalInput")
    sd_d = nc.dram_tensor("sd", [DC, 128], F32, kind="ExternalInput")
    eye_d = nc.dram_tensor("eye", [128, 128], BF16, kind="ExternalInput")
    eyef_d = nc.dram_tensor("eyef", [128, 128], F32, kind="ExternalInput")
    mask_d = nc.dram_tensor("mask", [128, 2048], BF16, kind="ExternalInput")
    ones_d = nc.dram_tensor("ones", [128, 512], BF16, kind="ExternalInput")
    onesf_d = nc.dram_tensor("onesf", [128, 128], F32, kind="ExternalInput")

    out_d = nc.dram_tensor("out", [T, D], F32, kind="ExternalOutput")
    scratch_kind = "ExternalOutput" if debug else "Internal"
    qts_d = nc.dram_tensor("qts", [DC, 128, T], BF16, kind=scratch_kind)
    kts_d = nc.dram_tensor("kts", [DC, 128, T], BF16, kind=scratch_kind)
    vs_d = nc.dram_tensor("vs", [T, D], BF16, kind=scratch_kind)
    atts_d = nc.dram_tensor("atts", [DC, 128, T], BF16, kind=scratch_kind)
    x1s_d = nc.dram_tensor("x1s", [DC, 128, T], BF16, kind=scratch_kind)
    r2s_d = nc.dram_tensor("r2s", [T, D], F32, kind=scratch_kind)
    wgb_d = nc.dram_tensor("wgb", [D, INTER], BF16, kind="Internal")
    wub_d = nc.dram_tensor("wub", [D, INTER], BF16, kind="Internal")
    wdb_d = nc.dram_tensor("wdb", [INTER, D], BF16, kind="Internal")
    prm_d = nc.dram_tensor("prmdbg", [128, 64], F32, kind=scratch_kind) if debug else None
    s1b_d = nc.dram_tensor("s1bdbg", [8, 128, 512], F32, kind=scratch_kind) if debug else None
    s2b_d = nc.dram_tensor("s2bdbg", [8, 128, 512], F32, kind=scratch_kind) if debug else None

    ENGS = ("sync", "gpsimd", "tensor", "vector", "scalar")
    Q = {k: [] for k in ENGS}
    C = {"LD": 0, "LG": 0, "ST": 0, "TM": 0, "VM": 0, "AM": 0, "GM": 0}
    sems = {}
    GETSEM = [lambda name: None]

    def em(qname, fn):
        Q[qname].append(fn)

    def wait(qname, sem, val):
        if val <= 0:
            return
        GETSEM[0](sem)
        em(qname, lambda e, s=sem, v=val: e.wait_ge(sems[s], v))

    def dma(qname, dst, src, sem):
        GETSEM[0](sem)
        C[sem] += 16
        em(qname, lambda e, d=dst, s=src, sm=sem: e.dma_start(d, s).then_inc(sems[sm], 16))

    def dma_t(qname, dst, src, sem):
        GETSEM[0](sem)
        C[sem] += 16
        em(qname, lambda e, d=dst, s=src, sm=sem: e.dma_start_transpose(d, s).then_inc(sems[sm], 16))

    def op(qname, f, sem=None):
        if sem is None:
            em(qname, lambda e, ff=f: ff(e))
        elif qname in ("scalar", "vector", "gpsimd"):
            C[sem] += 1
            em(qname, lambda e, ff=f: ff(e))
            em(qname, lambda e, sm=sem: e.drain().then_inc(sems[sm], 1))
        else:
            C[sem] += 1
            em(qname, lambda e, ff=f, sm=sem: ff(e).then_inc(sems[sm], 1))

    NBF = 78500
    NF32 = 12500

    with ExitStack() as ctx:
        e = ctx.enter_context
        BP = e(nc.sbuf_tensor("bp", [128, NBF], BF16))
        FP = e(nc.sbuf_tensor("fp", [128, NF32], F32))
        psA = e(nc.psum_tensor("psA", [128, 2048], F32))
        psB = e(nc.psum_tensor("psB", [128, 512], F32))
        psC = e(nc.psum_tensor("psC", [128, 512], F32))
        psT0 = e(nc.psum_tensor("psT0", [128, 1024], BF16))
        psT1 = e(nc.psum_tensor("psT1", [128, 1024], BF16))
        psT = [psT0, psT1]
        for s in ("LD", "LG", "ST", "TM", "VM", "AM", "GM"):
            sems[s] = e(nc.semaphore(s))
        def getsem(name):
            if name not in sems:
                sems[name] = e(nc.semaphore(name))
                C[name] = 0
            return sems[name]
        GETSEM[0] = getsem
        block = e(nc.Block())

        pbf = 0
        def abf(n):
            nonlocal pbf
            o = pbf; pbf += n
            assert pbf <= NBF, f"bf16 pool overflow {pbf}"
            return o
        pf = 0
        def af32(n):
            nonlocal pf
            o = pf; pf += n
            assert pf <= NF32, f"f32 pool overflow {pf}"
            return o

        o_eye = abf(128)
        o_mask = abf(2048)
        o_ones = abf(512)
        o_bqkv = abf(TD)
        o_bout = abf(D)
        DYN0 = pbf

        o_eyef = af32(128)
        o_onesf = af32(128)
        o_sgT = af32(JC)
        o_suT = af32(JC)
        o_sdT = af32(DC)
        o_sstage = af32(128)
        o_cols = af32(32)
        o_sq = af32(2048)
        o_nm = af32(8)
        o_nm4 = af32(16)
        o_prmc = af32(64)
        o_srow = af32(1024)
        o_rsb = af32(512)
        FDYN0 = pf

        EYE = BP[:, o_eye:o_eye + 128]
        MASK = BP[:, o_mask:o_mask + 2048]
        ONES = BP[:, o_ones:o_ones + 512]
        BQKV = BP[:, o_bqkv:o_bqkv + TD]
        BOUT = BP[:, o_bout:o_bout + D]
        EYEF = FP[:, o_eyef:o_eyef + 128]
        ONESF = FP[:, o_onesf:o_onesf + 128]
        SGT = FP[:, o_sgT:o_sgT + JC]
        SUT = FP[:, o_suT:o_suT + JC]
        SDT = FP[:, o_sdT:o_sdT + DC]
        SQ = FP[:, o_sq:o_sq + 2048]

        if 0 in phases:
            dma("sync", EYE, eye_d[:, :], "LD")
            dma("sync", MASK, mask_d[:, :], "LD")
            dma("sync", ONES[:, :], ones_d[:, :], "LD")
            dma("sync", BQKV[0:1, :], bqkv_d[:, :], "LD")
            dma("sync", BOUT[0:1, :], bout_d[:, :], "LD")
            dma("sync", EYEF, eyef_d[:, :], "LD")
            dma("sync", ONESF, onesf_d[:, :], "LD")
            sst = FP[:, o_sstage:o_sstage + 128]
            for (src_d, rows, dstap) in ((sg_d, JC, SGT), (su_d, JC, SUT), (sd_d, DC, SDT)):
                dma("sync", sst[:rows, :], src_d[:, :], "LD")
                wait("tensor", "LD", C["LD"])
                op("tensor", lambda t, r=rows: t.transpose(psB[:, 0:r], sst[:r, :], EYEF[:r, :r]), "TM")
                wait("scalar", "TM", C["TM"])
                op("scalar", lambda s, r=rows, dd=dstap: s.copy(dd[:, 0:r], psB[:, 0:r]), "AM")
                wait("tensor", "AM", C["AM"])
                wait("sync", "AM", C["AM"])

        if 2 in phases:
            pbf2 = DYN0
            def abf2(n):
                nonlocal pbf2
                o = pbf2; pbf2 += n
                assert pbf2 <= NBF, f"ph2 bf16 overflow {pbf2}"
                return o
            o_xt = abf2(DC * THALF)
            o_wqk = abf2(2 * DC * 128)
            o_wv = abf2(2 * DC * 512)
            o_qkst = abf2(2 * THALF)
            o_vst = abf2(2 * 512)
            o_wcvt = abf2(2 * 2048)
            XT = [BP[:, o_xt + dc * THALF: o_xt + (dc + 1) * THALF] for dc in range(DC)]
            WQK = [BP[:, o_wqk + i * DC * 128: o_wqk + (i + 1) * DC * 128] for i in range(2)]
            WVb = [BP[:, o_wv + i * DC * 512:o_wv + (i + 1) * DC * 512] for i in range(2)]
            QKST = [BP[:, o_qkst + i * THALF: o_qkst + (i + 1) * THALF] for i in range(2)]
            VST = [BP[:, o_vst + i * 512: o_vst + (i + 1) * 512] for i in range(2)]

            WCVT = [BP[:, o_wcvt + i * 2048:o_wcvt + (i + 1) * 2048] for i in range(2)]
            cvt_rel = [0, 0]
            cvt_jobs = []
            for (srcw, dstw) in ((wg_d, wgb_d), (wu_d, wub_d), (wd_d, wdb_d)):
                R = srcw.shape[0]
                for r0 in range(0, R, 2048):
                    for j in range(srcw.shape[1] // 128):
                        cvt_jobs.append((srcw, dstw, r0, j))
            wcix = 0
            cvt_prev = [0, 0]
            for (srcw, dstw, r0, j) in cvt_jobs:
                wc = wcix % 2
                wcix += 1
                sn = f"LC{wc}"
                wait("gpsimd", sn, cvt_prev[wc])
                dma("gpsimd", WCVT[wc].rearrange("p (r i) -> p r i", r=16),
                    srcw[r0:r0 + 2048, j * 128:(j + 1) * 128]
                    .rearrange("(r p) i -> p r i", p=128), sn)
                wait("gpsimd", sn, C[sn])
                dma("gpsimd", dstw[r0:r0 + 2048, j * 128:(j + 1) * 128]
                    .rearrange("(r p) i -> p r i", p=128),
                    WCVT[wc].rearrange("p (r i) -> p r i", r=16), sn)
                cvt_prev[wc] = C[sn]
            cvt_done = None
            bank_rel = [{"AM": C["AM"], "VM": C["VM"]} for _ in range(4)]
            ech = 0
            qk_store_rel = [0, 0]
            v_store_rel = [0, 0]
            wqk_rel = [0, 0]
            wv_rel = [0, 0]
            xt_rel = 0

            for hf in range(NHALF):
                wait("sync", "TM", xt_rel)
                for dc in range(DC):
                    dma_t("sync", XT[dc], x_d[hf * THALF:(hf + 1) * THALF, dc * 128:(dc + 1) * 128], "LXT")
                ld_xt = C["LXT"]

                def load_wqk(f2):
                    wb2 = f2 % 2
                    wait("sync", "TM", wqk_rel[wb2])
                    dma("sync", WQK[wb2].rearrange("p (dc j) -> p dc j", dc=DC),
                        wqkv_d[:, f2 * 128:(f2 + 1) * 128].rearrange("(dc p) j -> p dc j", p=128),
                        f"LW{wb2}")
                    return C[f"LW{wb2}"]
                ld_wqk = [0] * (2 * DC)
                ld_wqk[0] = load_wqk(0)
                for f in range(2 * DC):
                    wb = f % 2
                    if f + 1 < 2 * DC:
                        ld_wqk[f + 1] = load_wqk(f + 1)
                    ld_w = ld_wqk[f]
                    qb = f % 2
                    wait("scalar", f"SQ{qb}", qk_store_rel[qb])
                    wait("vector", f"SQ{qb}", qk_store_rel[qb])
                    wait("tensor", "LXT", ld_xt)
                    wait("tensor", f"LW{wb}", ld_w)
                    for tb in range(THALF // 512):
                        b = (hf * 144 + f * 4 + tb) % 4
                        wait("tensor", "AM", bank_rel[b]["AM"])
                        wait("tensor", "VM", bank_rel[b]["VM"])
                        for k in range(DC):
                            op("tensor", lambda t, b=b, k=k, wb=wb, tb=tb:
                               t.matmul(psA[:, b * 512:(b + 1) * 512],
                                        WQK[wb][:, k * 128:(k + 1) * 128],
                                        XT[k][:, tb * 512:(tb + 1) * 512],
                                        start=(k == 0), stop=False))
                        op("tensor", lambda t, b=b, f=f:
                           t.matmul(psA[:, b * 512:(b + 1) * 512],
                                    BQKV[0:1, f * 128:(f + 1) * 128],
                                    ONES[0:1, 0:512], start=False, stop=True), "TM")
                        sc = INV_SQRT_HD if f < DC else 1.0
                        ech += 1
                        if ech % 2 == 0:
                            wait("scalar", "TM", C["TM"])
                            op("scalar", lambda s, b=b, qb=qb, tb=tb, sc=sc:
                               s.activation(QKST[qb][:, tb * 512:(tb + 1) * 512],
                                            psA[:, b * 512:(b + 1) * 512], AF.Copy, scale=sc), "AM")
                            bank_rel[b] = {"AM": C["AM"], "VM": 0}
                        else:
                            wait("vector", "TM", C["TM"])
                            op("vector", lambda v, b=b, qb=qb, tb=tb, sc=sc:
                               v.tensor_scalar(QKST[qb][:, tb * 512:(tb + 1) * 512],
                                               psA[:, b * 512:(b + 1) * 512], sc, None, ALU.mult), "VM")
                            bank_rel[b] = {"AM": 0, "VM": C["VM"]}
                    wqk_rel[wb] = C["TM"]
                    wait("sync", "AM", C["AM"])
                    wait("sync", "VM", C["VM"])
                    dst = (qts_d if f < DC else kts_d)[f % DC][:, hf * THALF:(hf + 1) * THALF]
                    dma("sync", dst, QKST[qb][:, :], f"SQ{qb}")
                    qk_store_rel[qb] = C[f"SQ{qb}"]

                def load_wv(vb2):
                    wait("sync", "TM", wv_rel[vb2 % 2])
                    dma("sync", WVb[vb2 % 2].rearrange("p (dc j) -> p dc j", dc=DC),
                        wqkv_d[:, 2 * D + vb2 * 512: 2 * D + (vb2 + 1) * 512]
                        .rearrange("(dc p) j -> p dc j", p=128), f"LV{vb2 % 2}")
                    return C[f"LV{vb2 % 2}"]
                ld_wvs = [0] * 4
                ld_wvs[0] = load_wv(0)
                for vb in range(4):
                    if vb + 1 < 4:
                        ld_wvs[vb + 1] = load_wv(vb + 1)
                    wait("tensor", f"LV{vb % 2}", ld_wvs[vb])
                    for tt in range(THALF // 128):
                        b = (vb * 16 + tt) % 4
                        vsb = tt % 2
                        wait("tensor", "AM", bank_rel[b]["AM"])
                        wait("tensor", "VM", bank_rel[b]["VM"])
                        for k in range(DC):
                            op("tensor", lambda t, b=b, k=k, tt=tt, WV=WVb[vb % 2]:
                               t.matmul(psA[:, b * 512:(b + 1) * 512],
                                        XT[k][:, tt * 128:(tt + 1) * 128],
                                        WV[:, k * 512:(k + 1) * 512],
                                        start=(k == 0), stop=False))
                        op("tensor", lambda t, b=b, vb=vb:
                           t.matmul(psA[:, b * 512:(b + 1) * 512],
                                    ONES[0:1, 0:128],
                                    BQKV[0:1, 2 * D + vb * 512: 2 * D + vb * 512 + 512],
                                    start=False, stop=True), "TM")
                        ech += 1
                        if ech % 2 == 0:
                            wait("scalar", "TM", C["TM"])
                            wait("scalar", f"SV{vsb}", v_store_rel[vsb])
                            op("scalar", lambda s, b=b, vsb=vsb:
                               s.copy(VST[vsb][:, :], psA[:, b * 512:(b + 1) * 512]), "AM")
                            bank_rel[b] = {"AM": C["AM"], "VM": 0}
                            wait("sync", "AM", C["AM"])
                        else:
                            wait("vector", "TM", C["TM"])
                            wait("vector", f"SV{vsb}", v_store_rel[vsb])
                            op("vector", lambda v, b=b, vsb=vsb:
                               v.tensor_copy(VST[vsb][:, :], psA[:, b * 512:(b + 1) * 512]), "VM")
                            bank_rel[b] = {"AM": 0, "VM": C["VM"]}
                            wait("sync", "VM", C["VM"])
                        r0 = hf * THALF + tt * 128
                        dma("sync", vs_d[r0:r0 + 128, vb * 512:(vb + 1) * 512], VST[vsb][:, :], f"SV{vsb}")
                        v_store_rel[vsb] = C[f"SV{vsb}"]
                    wv_rel[vb % 2] = C["TM"]
                xt_rel = C["TM"]
            ph2_TM, ph2_AM, ph2_VM, ph2_ST = C["TM"], C["AM"], C["VM"], C["ST"]
        else:
            ph2_TM = ph2_AM = ph2_VM = ph2_ST = 0

        if 3 in phases:
            pbf3 = DYN0
            def abf3(n):
                nonlocal pbf3
                o = pbf3; pbf3 += n
                assert pbf3 <= NBF
                return o
            o_qh = abf3(2 * SEQ)
            o_kh = abf3(2 * SEQ)
            o_vh = abf3(2 * SEQ)
            o_p = abf3(4 * 2048)
            o_pt = abf3(16 * 512)
            o_ast = abf3(2 * 512)
            QH = [BP[:, o_qh + i * SEQ:o_qh + (i + 1) * SEQ] for i in range(2)]
            KH = [BP[:, o_kh + i * SEQ:o_kh + (i + 1) * SEQ] for i in range(2)]
            VH = [BP[:, o_vh + i * SEQ:o_vh + (i + 1) * SEQ] for i in range(2)]
            PB = [BP[:, o_p + i * 2048:o_p + (i + 1) * 2048] for i in range(4)]
            PT = BP[:, o_pt:o_pt + 16 * 512]
            PT3 = PT.rearrange("p (kt j) -> p kt j", kt=16)
            AST = [BP[:, o_ast + i * 512:o_ast + (i + 1) * 512] for i in range(2)]
            NM = [FP[:, o_nm + i:o_nm + i + 1] for i in range(4)]
            NM4 = FP[:, o_nm4:o_nm4 + 16]
            SR1 = FP[0:1, o_srow:o_srow + 512]
            SR2 = FP[0:1, o_srow + 512:o_srow + 1024]
            RSB = FP[:, o_rsb:o_rsb + 512]

            wait("sync", "TM", ph2_TM)
            wait("sync", "AM", ph2_AM)
            if 2 in phases:
                for sn in ("SQ0", "SQ1", "SV0", "SV1"):
                    wait("sync", sn, C.get(sn, 0))

            qkv_rel = [0, 0]
            bank_rel = [{"AM": C["AM"], "VM": C["VM"]} for _ in range(4)]
            exp_done = [C["AM"]] * 4
            pt_am = C["AM"]
            pt_vm = C["VM"]
            av_done = C["TM"]
            o_evac = C["VM"]
            ast_rel = [0, 0]
            psT_rel = [{"AM": C["AM"], "VM": C["VM"]} for _ in range(2)]
            pb_rel = [C["TM"]] * 4
            r_evac = C["VM"]
            gchunk = 0

            pairs = [(s, h) for s in range(NSEQ) for h in range(H)]
            def emit_loads(idx):
                s, h = pairs[idx]
                hb = idx % 2
                wait("sync", "TM", qkv_rel[hb])
                dma("sync", QH[hb], qts_d[h][:, s * SEQ:(s + 1) * SEQ], f"LH{hb}")
                dma("sync", KH[hb], kts_d[h][:, s * SEQ:(s + 1) * SEQ], f"LH{hb}")
                dma("sync", VH[hb].rearrange("p (kt j) -> p kt j", kt=16),
                    vs_d[s * SEQ:(s + 1) * SEQ, h * 128:(h + 1) * 128]
                    .rearrange("(kt p) j -> p kt j", p=128), f"LH{hb}")
                return C[f"LH{hb}"]

            ld_done = [0] * len(pairs)
            ld_done[0] = emit_loads(0)
            pend_tr = []
            for idx, (s, h) in enumerate(pairs):
                hb = idx % 2
                if idx + 1 < len(pairs):
                    ld_done[idx + 1] = emit_loads(idx + 1)
                wait("tensor", f"LH{hb}", ld_done[idx])
                for qb in range(4):
                    nkt = 4 * (qb + 1)
                    N = 512 * (qb + 1)
                    wait("gpsimd", "TM", av_done)
                    wait("gpsimd", "AM", pt_am)
                    wait("gpsimd", "VM", pt_vm)
                    for qtl in range(3):
                        nz = 3 - qtl
                        qt = qb * 4 + qtl
                        op("gpsimd", lambda g, qt=qt, qtl=qtl, nz=nz:
                           g.memset(PT3[:, qt + 1:qt + 1 + nz, qtl * 128:(qtl + 1) * 128], 0.0), "GM")
                    gm_after = C["GM"]
                    wait("scalar", "TM", av_done)
                    wait("vector", "TM", av_done)
                    def emit_transposes(qt2, pi2, qtl2, exp_am):
                        nonlocal pt_am, pt_vm
                        wait("tensor", "AM", exp_am)
                        wait("tensor", "GM", gm_after)
                        ntr = qt2 + 1
                        for kt in range(ntr):
                            bkk = (kt // 8) % 2
                            if kt % 8 == 0:
                                wait("tensor", "AM", psT_rel[bkk]["AM"])
                                wait("tensor", "VM", psT_rel[bkk]["VM"])
                            lastt = (kt % 8 == 7 or kt == ntr - 1)
                            op("tensor", lambda t, kt=kt, pi2=pi2, bkk=bkk:
                               t.transpose(psT[bkk][:, (kt % 8) * 128:(kt % 8 + 1) * 128],
                                           PB[pi2][:, kt * 128:(kt + 1) * 128], EYE),
                               "TM" if lastt else None)
                            if lastt:
                                ntiles = kt % 8 + 1
                                k0 = kt - ntiles + 1
                                eng3 = "scalar" if bkk == 0 else "vector"
                                wait(eng3, "TM", C["TM"])
                                if eng3 == "scalar":
                                    op(eng3, lambda sC, k0=k0, ntiles=ntiles, qtl2=qtl2, bkk=bkk:
                                       sC.copy(PT3[:, k0:k0 + ntiles, qtl2 * 128:(qtl2 + 1) * 128],
                                               psT[bkk].rearrange("p (kt j) -> p kt j", kt=8)
                                               [:, 0:ntiles, :]), "AM")
                                    psT_rel[bkk] = {"AM": C["AM"], "VM": 0}
                                    pt_am = C["AM"]
                                else:
                                    op(eng3, lambda v, k0=k0, ntiles=ntiles, qtl2=qtl2, bkk=bkk:
                                       v.tensor_copy(PT3[:, k0:k0 + ntiles, qtl2 * 128:(qtl2 + 1) * 128],
                                                     psT[bkk].rearrange("p (kt j) -> p kt j", kt=8)
                                                     [:, 0:ntiles, :]), "VM")
                                    psT_rel[bkk] = {"AM": 0, "VM": C["VM"]}
                                    pt_vm = C["VM"]
                        pb_rel[pi2] = C["TM"]

                    for qtl in range(4):
                        qt = qb * 4 + qtl
                        pi = qt % 4
                        tile_banks = []
                        for c in range(qb + 1):
                            b = gchunk % 4
                            gchunk += 1
                            tile_banks.append((c, b))
                            diag = (c == qb)
                            wait("tensor", "AM", bank_rel[b]["AM"])
                            op("tensor", lambda t, c=c, qt=qt, hb=hb, b=b, diag=diag:
                               t.matmul(psA[:, b * 512:(b + 1) * 512],
                                        QH[hb][:, qt * 128:(qt + 1) * 128],
                                        KH[hb][:, c * 512:(c + 1) * 512],
                                        start=True, stop=not diag),
                               None if diag else "TM")
                            if diag:
                                op("tensor", lambda t, b=b, qtl=qtl:
                                   t.matmul(psA[:, b * 512:(b + 1) * 512],
                                            EYE, MASK[:, qtl * 512:(qtl + 1) * 512],
                                            start=False, stop=True), "TM")
                            mm_tm = C["TM"]
                            wait("vector", "TM", mm_tm)
                            if c == 0:
                                wait("vector", "AM", exp_done[pi])
                            op("vector", lambda v, pi=pi, c=c, b=b:
                               v.reduce_max(NM4[:, pi * 4 + c:pi * 4 + c + 1],
                                            psA[:, b * 512:(b + 1) * 512],
                                            axis=AX.X, negate=True),
                               "VM" if diag else None)
                        op("vector", lambda v, pi=pi, qb=qb:
                           v.tensor_reduce(NM[pi][:, :], NM4[:, pi * 4:pi * 4 + qb + 1],
                                           axis=AX.X, op=ALU.min), "VM")
                        wait("scalar", "VM", C["VM"])
                        wait("scalar", "TM", pb_rel[pi])
                        for c, b in tile_banks:
                            lastc = (c == qb)
                            op("scalar", lambda sC, pi=pi, c=c, b=b:
                               sC.activation(PB[pi][:, c * 512:(c + 1) * 512],
                                             psA[:, b * 512:(b + 1) * 512], AF.Exp,
                                             bias=NM[pi][:, :1], scale=1.0),
                               "AM" if lastc else None)
                        exp_done[pi] = C["AM"]
                        for c, b in tile_banks:
                            bank_rel[b] = {"AM": exp_done[pi], "VM": 0}
                        pend_tr.append((qt, pi, qtl, exp_done[pi]))
                        if len(pend_tr) > 3:
                            emit_transposes(*pend_tr.pop(0))
                    while pend_tr:
                        emit_transposes(*pend_tr.pop(0))
                    wait("tensor", "AM", pt_am)
                    wait("tensor", "VM", pt_vm)
                    wait("tensor", "GM", gm_after)
                    wait("tensor", "VM", r_evac)
                    for kt in range(nkt):
                        op("tensor", lambda t, kt=kt, nkt=nkt:
                           t.matmul(psC[0:1, :], ONES[:, 0:1],
                                    PT[:, kt * 512:(kt + 1) * 512],
                                    start=(kt == 0), stop=(kt == nkt - 1)),
                           "TM" if kt == nkt - 1 else None)
                    sums_tm = C["TM"]
                    wait("scalar", "TM", sums_tm)
                    op("scalar", lambda sC: sC.activation(SR2[:, :], psC[0:1, :], AF.Ln), "AM")
                    op("scalar", lambda sC: sC.activation(SR1[:, :], SR2[:, :], AF.Exp, scale=-1.0), "AM")
                    srow_am = C["AM"]
                    wait("tensor", "AM", srow_am)
                    op("tensor", lambda t:
                       t.matmul(psC[:, :], ONESF[0:1, 0:128], SR1[:, :],
                                start=True, stop=True), "TM")
                    bcast_tm = C["TM"]
                    wait("vector", "TM", bcast_tm)
                    op("vector", lambda v: v.tensor_copy(RSB[:, :], psC[:, :]), "VM")
                    r_evac = C["VM"]
                    wait("tensor", "VM", o_evac)
                    for kt in range(nkt):
                        op("tensor", lambda t, kt=kt, hb=hb, nkt=nkt:
                           t.matmul(psB[:, :], VH[hb][:, kt * 128:(kt + 1) * 128],
                                    PT[:, kt * 512:(kt + 1) * 512],
                                    start=(kt == 0), stop=(kt == nkt - 1)),
                           "TM" if kt == nkt - 1 else None)
                    av_done = C["TM"]
                    ab = qb % 2
                    wait("vector", "TM", av_done)
                    wait("vector", "VM", r_evac)
                    wait("vector", f"SA{ab}", ast_rel[ab])
                    op("vector", lambda v, ab=ab:
                       v.tensor_mul(AST[ab][:, :], psB[:, :], RSB[:, :]), "VM")
                    o_evac = C["VM"]
                    wait("sync", "VM", o_evac)
                    dma("sync", atts_d[h][:, s * SEQ + qb * 512: s * SEQ + (qb + 1) * 512],
                        AST[ab][:, :], f"SA{ab}")
                    ast_rel[ab] = C[f"SA{ab}"]
                qkv_rel[hb] = C["TM"]
            ph3_TM, ph3_AM, ph3_VM, ph3_ST = C["TM"], C["AM"], C["VM"], C["ST"]
        else:
            ph3_TM, ph3_AM, ph3_VM, ph3_ST = ph2_TM, ph2_AM, ph2_VM, ph2_ST


        if 4 in phases:
            pbf4 = DYN0
            def abf4(n):
                nonlocal pbf4
                o = pbf4; pbf4 += n
                assert pbf4 <= NBF
                return o
            o_attb = abf4(2 * DC * 512)
            o_wo = abf4(2 * DC * 512)
            o_xb = abf4(2 * 4 * D)
            o_r1 = abf4(4 * D)
            o_x1t = abf4(D)
            o_x1f = abf4(2 * D)
            ATTB = [BP[:, o_attb + i * DC * 512:o_attb + (i + 1) * DC * 512] for i in range(2)]
            WO = [BP[:, o_wo + i * DC * 512:o_wo + (i + 1) * DC * 512] for i in range(2)]
            XB = [BP[:, o_xb + i * 4 * D:o_xb + (i + 1) * 4 * D] for i in range(2)]
            R1 = [BP[:, o_r1 + tt * D:o_r1 + (tt + 1) * D] for tt in range(4)]
            X1T = BP[:, o_x1t:o_x1t + D]
            X1F = [BP[:, o_x1f + i * D:o_x1f + (i + 1) * D] for i in range(2)]
            COLS = FP[:, o_cols:o_cols + 32]

            wait("sync", "TM", ph3_TM)
            wait("sync", "AM", ph3_AM)
            wait("sync", "VM", ph3_VM)
            if 3 in phases:
                for sn in ("SA0", "SA1"):
                    wait("sync", sn, C.get(sn, 0))
            if 2 in phases:
                wait("scalar", "LC0", C.get("LC0", 0))
                wait("scalar", "LC1", C.get("LC1", 0))
                wait("vector", "LC0", C.get("LC0", 0))
                wait("vector", "LC1", C.get("LC1", 0))

            attb_rel = [0, 0]
            wo_rel = [0, 0]
            xb_rel = [0, 0]
            bank_rel4 = [{"AM": C["AM"], "VM": C["VM"]} for _ in range(4)]
            x1f_rel = [0, 0]
            psT_rel4 = [C["AM"], C["AM"]]
            x1t_rel = C["TM"]
            r1_rel = C["AM"]
            cols_rel = [C["VM"], C["VM"]]

            for tb in range(8):
                bb = tb % 2
                wait("sync", "TM", attb_rel[bb])
                dma("sync", ATTB[bb].rearrange("p (dc j) -> p dc j", dc=DC),
                    atts_d[:, :, tb * 512:(tb + 1) * 512].rearrange("dc p j -> p dc j"), f"LT{bb}")
                wait("sync", "VM", xb_rel[bb])
                dma("sync", XB[bb].rearrange("p (tt j) -> p tt j", tt=4),
                    x_d[tb * 512:(tb + 1) * 512, :].rearrange("(tt p) j -> p tt j", p=128), f"LT{bb}")
                ld_tb = C[f"LT{bb}"]
                def load_wo(db2):
                    wb2 = db2 % 2
                    wait("sync", "TM", wo_rel[wb2])
                    dma("sync", WO[wb2].rearrange("p (dc j) -> p dc j", dc=DC),
                        wout_d[:, db2 * 512:(db2 + 1) * 512].rearrange("(dc p) j -> p dc j", p=128),
                        f"LO{wb2}")
                    return C[f"LO{wb2}"]
                if tb == 0:
                    ld_wo = [0] * 4
                    ld_wo[0] = load_wo(0)
                for db in range(4):
                    wb = db % 2
                    if db + 1 < 4:
                        ld_wo[db + 1] = load_wo(db + 1)
                    elif tb < 7:
                        ld_wo[0] = load_wo(0)
                    ld_w = ld_wo[db]
                    wait("tensor", f"LT{bb}", ld_tb)
                    wait("tensor", f"LO{db % 2}", ld_w)
                    for tt in range(4):
                        b = (tb * 16 + db * 4 + tt) % 4
                        wait("tensor", "AM", bank_rel4[b]["AM"])
                        wait("tensor", "VM", bank_rel4[b]["VM"])
                        for k in range(DC):
                            op("tensor", lambda t, b=b, k=k, bb=bb, wb=wb, tt=tt:
                               t.matmul(psA[:, b * 512:(b + 1) * 512],
                                        ATTB[bb][:, k * 512 + tt * 128: k * 512 + (tt + 1) * 128],
                                        WO[wb][:, k * 512:(k + 1) * 512],
                                        start=(k == 0), stop=False))
                        op("tensor", lambda t, b=b, db=db:
                           t.matmul(psA[:, b * 512:(b + 1) * 512],
                                    ONES[0:1, 0:128], BOUT[0:1, db * 512:(db + 1) * 512],
                                    start=False, stop=True), "TM")
                        wait("vector", "TM", C["TM"])
                        wait("vector", "AM", r1_rel)
                        op("vector", lambda v, b=b, bb=bb, tt=tt, db=db:
                           v.tensor_add(R1[tt][:, db * 512:(db + 1) * 512],
                                        psA[:, b * 512:(b + 1) * 512],
                                        XB[bb][:, tt * D + db * 512: tt * D + (db + 1) * 512]), "VM")
                        bank_rel4[b] = {"AM": C["AM"], "VM": C["VM"]}
                    wo_rel[wb] = C["TM"]
                attb_rel[bb] = C["TM"]
                xb_rel[bb] = C["VM"]
                vm_r1 = C["VM"]
                for tt in range(4):
                    par = tt % 2
                    cs = par * 16
                    S1 = COLS[:, cs + 0:cs + 1]; S2 = COLS[:, cs + 1:cs + 2]
                    M = COLS[:, cs + 2:cs + 3]; MS = COLS[:, cs + 3:cs + 4]
                    M2 = COLS[:, cs + 4:cs + 5]; VAR = COLS[:, cs + 5:cs + 6]
                    SD = COLS[:, cs + 6:cs + 7]; RSTD = COLS[:, cs + 7:cs + 8]
                    NMR = COLS[:, cs + 8:cs + 9]
                    wait("vector", "VM", cols_rel[par])
                    op("vector", lambda v, tt=tt, S1=S1: v.reduce_sum(S1, R1[tt][:, :], axis=AX.X), "VM")
                    wait("scalar", "VM", C["VM"])
                    op("scalar", lambda sC, tt=tt:
                       sC.activation(SQ[:, :], R1[tt][:, :], AF.Square), "AM")
                    op("vector", lambda v, S1=S1, M=M: v.tensor_scalar(M, S1, 1.0 / D, None, ALU.mult), "VM")
                    wait("vector", "AM", C["AM"])
                    op("vector", lambda v, S2=S2: v.reduce_sum(S2, SQ[:, :], axis=AX.X), "VM")
                    op("vector", lambda v, S2=S2, MS=MS: v.tensor_scalar(MS, S2, 1.0 / D, None, ALU.mult), "VM")
                    op("vector", lambda v, M=M, M2=M2: v.tensor_mul(M2, M, M), "VM")
                    op("vector", lambda v, MS=MS, M2=M2, VAR=VAR: v.tensor_sub(VAR, MS, M2), "VM")
                    op("vector", lambda v, VAR=VAR: v.tensor_scalar(VAR, VAR, EPS, None, ALU.add), "VM")
                    wait("scalar", "VM", C["VM"])
                    op("scalar", lambda sC, VAR=VAR, SD=SD:
                       sC.activation(SD, VAR, AF.Sqrt), "AM")
                    wait("vector", "AM", C["AM"])
                    op("vector", lambda v, SD=SD, RSTD=RSTD: v.reciprocal(RSTD, SD), "VM")
                    op("vector", lambda v, M=M, RSTD=RSTD, NMR=NMR:
                       v.scalar_tensor_tensor(NMR, M, -1.0, RSTD, ALU.mult, ALU.mult), "VM")
                    wait("vector", "TM", x1t_rel)
                    op("vector", lambda v, tt=tt, RSTD=RSTD, NMR=NMR:
                       v.tensor_scalar(X1T[:, :], R1[tt][:, :], RSTD[:, :1], NMR[:, :1],
                                       ALU.mult, ALU.add), "VM")
                    cols_rel[par] = C["VM"]
                    vm_x1t = C["VM"]
                    xf = tt % 2
                    wait("tensor", "VM", vm_x1t)
                    wait("tensor", f"SX{xf}", x1f_rel[xf])
                    for dc in range(DC):
                        bkk = (dc // 8) % 2
                        if dc % 8 == 0:
                            wait("tensor", "AM", psT_rel4[bkk])
                        op("tensor", lambda t, dc=dc, bkk=bkk:
                           t.transpose(psT[bkk][:, (dc % 8) * 128:(dc % 8 + 1) * 128],
                                       X1T[:, dc * 128:(dc + 1) * 128], EYE),
                           "TM" if dc % 8 == 7 else None)
                        if dc % 8 == 7:
                            wait("scalar", "TM", C["TM"])
                            op("scalar", lambda sC, dc=dc, bkk=bkk, xf=xf:
                               sC.copy(X1F[xf][:, (dc - 7) * 128:(dc + 1) * 128],
                                       psT[bkk][:, :]), "AM")
                            psT_rel4[bkk] = C["AM"]
                    x1t_rel = C["TM"]
                    wait("sync", "AM", C["AM"])
                    dma("sync", x1s_d[:, :, tb * 512 + tt * 128: tb * 512 + (tt + 1) * 128]
                        .rearrange("dc p j -> p dc j"),
                        X1F[xf].rearrange("p (dc j) -> p dc j", dc=DC), f"SX{xf}")
                    x1f_rel[xf] = C[f"SX{xf}"]
                r1_rel = C["AM"]
            ph4_TM, ph4_AM, ph4_VM, ph4_ST = C["TM"], C["AM"], C["VM"], C["ST"]
        else:
            ph4_TM, ph4_AM, ph4_VM, ph4_ST = ph3_TM, ph3_AM, ph3_VM, ph3_ST

        if 5 in phases:
            pbf5 = DYN0
            def abf5(n):
                nonlocal pbf5
                o = pbf5; pbf5 += n
                assert pbf5 <= NBF, f"ph5 bf16 overflow {pbf5}"
                return o
            o_fuse = abf5(JC * 512)
            o_x1 = abf5(DC * 512)
            o_wg = abf5(2 * DC * 128)
            o_wu = abf5(2 * DC * 128)
            o_wd = abf5(2 * JC * 128)
            o_gt = abf5(2 * 512)
            o_ut = abf5(2 * 512)
            FUSE = BP[:, o_fuse:o_fuse + JC * 512]
            X1 = BP[:, o_x1:o_x1 + DC * 512]
            WG = [BP[:, o_wg + i * DC * 128:o_wg + (i + 1) * DC * 128] for i in range(2)]
            WU = [BP[:, o_wu + i * DC * 128:o_wu + (i + 1) * DC * 128] for i in range(2)]
            WD = [BP[:, o_wd + i * JC * 128:o_wd + (i + 1) * JC * 128] for i in range(2)]
            GT = [BP[:, o_gt + i * 512:o_gt + (i + 1) * 512] for i in range(2)]
            UT = [BP[:, o_ut + i * 512:o_ut + (i + 1) * 512] for i in range(2)]
            pf5 = FDYN0
            def af5(n):
                nonlocal pf5
                o = pf5; pf5 += n
                assert pf5 <= NF32, f"ph5 f32 overflow {pf5}"
                return o
            o_r2 = af5(2 * 512)
            o_dt = af5(2 * 512)
            o_sqf = af5(2 * 512)
            o_s1b = af5(512)
            o_s2b = af5(512)
            o_tmp = af5(512)
            o_tmp2 = af5(512)
            o_ostg = af5(2 * 512)
            R2 = [FP[:, o_r2 + i * 512:o_r2 + (i + 1) * 512] for i in range(2)]
            DT = [FP[:, o_dt + i * 512:o_dt + (i + 1) * 512] for i in range(2)]
            SQF = [FP[:, o_sqf + i * 512:o_sqf + (i + 1) * 512] for i in range(2)]
            S1B = FP[:, o_s1b:o_s1b + 512]
            S2B = FP[:, o_s2b:o_s2b + 512]
            TMP = FP[:, o_tmp:o_tmp + 512]
            TMP2 = FP[:, o_tmp2:o_tmp2 + 512]
            OSTG = [FP[:, o_ostg + i * 512:o_ostg + (i + 1) * 512] for i in range(2)]
            PRMC = FP[:, o_prmc:o_prmc + 64]

            wait("sync", "TM", ph4_TM)
            wait("sync", "AM", ph4_AM)
            wait("sync", "VM", ph4_VM)
            if 2 in phases:
                wait("sync", "LC0", C.get("LC0", 0))
                wait("sync", "LC1", C.get("LC1", 0))
                wait("gpsimd", "LC0", C.get("LC0", 0))
                wait("gpsimd", "LC1", C.get("LC1", 0))
            wait("gpsimd", "TM", ph4_TM)
            wait("gpsimd", "AM", ph4_AM)
            wait("gpsimd", "VM", ph4_VM)

            wg_rel = [0, 0]
            wd_rel = [0, 0]
            gtu_rel = [0, 0]
            gbank_rel = [C["AM"]] * 4
            fuse_rel = 0
            x1_rel_tm = 0
            x1_rel_vm = 0
            x1_rel_am = 0
            dn_evac = [C["AM"], C["AM"]]
            r2_rel_tm = [C["TM"], C["TM"]]
            r2_rel_am = [C["AM"], C["AM"]]
            dt_rel = [C["VM"], C["VM"]]
            sqf_rel = [C["TM"], C["TM"]]
            trb_rel = [C["AM"], C["AM"]]
            s12_rel = [C["VM"], C["VM"]]
            ostg_rel = [0, 0]
            prm_rel = 0

            for sl in range(8):
                if 4 in phases:
                    for sn in ("SX0", "SX1"):
                        wait("sync", sn, C.get(sn, 0))
                wait("sync", "TM", x1_rel_tm)
                wait("sync", "VM", x1_rel_vm)
                dma("sync", X1.rearrange("p (dc j) -> p dc j", dc=DC),
                    x1s_d[:, :, sl * 512:(sl + 1) * 512].rearrange("dc p j -> p dc j"), "LX")
                ld_x1 = C["LX"]
                for j in range(JC):
                    wb = j % 2
                    wait("sync", "TM", wg_rel[wb])
                    dma("sync", WG[wb].rearrange("p (dc i) -> p dc i", dc=DC),
                        wgb_d[:, j * 128:(j + 1) * 128].rearrange("(dc p) i -> p dc i", p=128), f"LWG{wb}")
                    dma("sync", WU[wb].rearrange("p (dc i) -> p dc i", dc=DC),
                        wub_d[:, j * 128:(j + 1) * 128].rearrange("(dc p) i -> p dc i", p=128), f"LWG{wb}")
                    ld_w = C[f"LWG{wb}"]
                    bg = (2 * j) % 4
                    bu = (2 * j + 1) % 4
                    wait("tensor", "LX", ld_x1)
                    wait("tensor", f"LWG{wb}", ld_w)
                    wait("tensor", "AM", max(gbank_rel[bg], gbank_rel[bu]))
                    for k in range(DC):
                        op("tensor", lambda t, bg=bg, k=k, wb=wb:
                           t.matmul(psA[:, bg * 512:(bg + 1) * 512],
                                    WG[wb][:, k * 128:(k + 1) * 128],
                                    X1[:, k * 512:(k + 1) * 512],
                                    start=(k == 0), stop=(k == DC - 1)),
                           "TM" if k == DC - 1 else None)
                    for k in range(DC):
                        op("tensor", lambda t, bu=bu, k=k, wb=wb:
                           t.matmul(psA[:, bu * 512:(bu + 1) * 512],
                                    WU[wb][:, k * 128:(k + 1) * 128],
                                    X1[:, k * 512:(k + 1) * 512],
                                    start=(k == 0), stop=(k == DC - 1)),
                           "TM" if k == DC - 1 else None)
                    wg_rel[wb] = C["TM"]
                    gp = j % 2
                    wait("scalar", "TM", C["TM"])
                    wait("scalar", "VM", gtu_rel[gp])
                    op("scalar", lambda sC, bg=bg, j=j, gp=gp:
                       sC.activation(GT[gp][:, :], psA[:, bg * 512:(bg + 1) * 512], AF.Silu,
                                     scale=SGT[:, j:j + 1]), None)
                    op("scalar", lambda sC, bu=bu, j=j, gp=gp:
                       sC.activation(UT[gp][:, :], psA[:, bu * 512:(bu + 1) * 512], AF.Copy,
                                     scale=SUT[:, j:j + 1]), "AM")
                    gbank_rel[bg] = C["AM"]
                    gbank_rel[bu] = C["AM"]
                    wait("vector", "AM", C["AM"])
                    wait("vector", "TM", fuse_rel)
                    op("vector", lambda v, j=j, gp=gp:
                       v.tensor_mul(FUSE[:, j * 512:(j + 1) * 512], GT[gp][:, :], UT[gp][:, :]), "VM")
                    gtu_rel[gp] = C["VM"]
                vm_fuse = C["VM"]
                def emit_tail(dout, dnb, vm_r2, am_sq):
                    nonlocal s12_rel, sqf_rel, r2_rel_tm, r2_rel_am, trb_rel, ostg_rel
                    wait("tensor", "VM", vm_r2)
                    wait("tensor", "VM", s12_rel[0])
                    op("tensor", lambda t, dnb=dnb:
                       t.matmul(psA[:, 2 * 512:3 * 512], ONESF, R2[dnb][:, :],
                                start=True, stop=True), "TM")
                    tm_s1 = C["TM"]
                    wait("tensor", "AM", am_sq)
                    wait("tensor", "VM", s12_rel[1])
                    op("tensor", lambda t, dnb=dnb:
                       t.matmul(psA[:, 3 * 512:4 * 512], ONESF, SQF[dnb][:, :],
                                start=True, stop=True), "TM")
                    tm_s2 = C["TM"]
                    sqf_rel[dnb] = tm_s2
                    wait("vector", "TM", tm_s1)
                    if dout == 0:
                        op("vector", lambda v: v.tensor_copy(S1B[:, :], psA[:, 2 * 512:3 * 512]), "VM")
                    else:
                        op("vector", lambda v: v.tensor_add(S1B[:, :], S1B[:, :], psA[:, 2 * 512:3 * 512]), "VM")
                    s12_rel[0] = C["VM"]
                    wait("vector", "TM", tm_s2)
                    if dout == 0:
                        op("vector", lambda v: v.tensor_copy(S2B[:, :], psA[:, 3 * 512:4 * 512]), "VM")
                    else:
                        op("vector", lambda v: v.tensor_add(S2B[:, :], S2B[:, :], psA[:, 3 * 512:4 * 512]), "VM")
                    s12_rel[1] = C["VM"]
                    bt = dout % 2
                    wait("tensor", "AM", trb_rel[bt])
                    for tt in range(4):
                        op("tensor", lambda t, tt=tt, dnb=dnb, bt=bt:
                           t.transpose(psA[:, bt * 512 + tt * 128: bt * 512 + (tt + 1) * 128],
                                       R2[dnb][:, tt * 128:(tt + 1) * 128], EYEF),
                           "TM" if tt == 3 else None)
                    r2_rel_tm[dnb] = C["TM"]
                    r2_rel_am[dnb] = am_sq
                    ob = dout % 2
                    wait("scalar", "TM", C["TM"])
                    wait("scalar", f"SR{ob}", ostg_rel[ob])
                    op("scalar", lambda sC, bt=bt, ob=ob:
                       sC.copy(OSTG[ob][:, :], psA[:, bt * 512:(bt + 1) * 512]), "AM")
                    trb_rel[bt] = C["AM"]
                    wait("sync", "AM", C["AM"])
                    dma("sync",
                        r2s_d[sl * 512:(sl + 1) * 512, dout * 128:(dout + 1) * 128]
                        .rearrange("(tt p) j -> p tt j", p=128),
                        OSTG[ob].rearrange("p (tt j) -> p tt j", tt=4), f"SR{ob}")
                    ostg_rel[ob] = C[f"SR{ob}"]

                pend_tail = None
                for dout in range(DC):
                    wb = dout % 2
                    wait("gpsimd", "TM", wd_rel[wb])
                    dma("gpsimd", WD[wb].rearrange("p (jc i) -> p jc i", jc=JC),
                        wdb_d[:, dout * 128:(dout + 1) * 128].rearrange("(jc p) i -> p jc i", p=128),
                        f"LWD{wb}")
                    ld_w = C[f"LWD{wb}"]
                    dnb = dout % 2
                    ps_dn = psB if dnb == 0 else psC
                    wait("tensor", f"LWD{dout % 2}", ld_w)
                    wait("tensor", "VM", vm_fuse)
                    wait("tensor", "AM", dn_evac[dnb])
                    for jc in range(JC):
                        op("tensor", lambda t, jc=jc, wb=wb, ps_dn=ps_dn:
                           t.matmul(ps_dn[:, :], WD[wb][:, jc * 128:(jc + 1) * 128],
                                    FUSE[:, jc * 512:(jc + 1) * 512],
                                    start=(jc == 0), stop=(jc == JC - 1)),
                           "TM" if jc == JC - 1 else None)
                    wd_rel[wb] = C["TM"]
                    dn_done = C["TM"]
                    wait("scalar", "TM", dn_done)
                    wait("scalar", "VM", dt_rel[dnb])
                    op("scalar", lambda sC, dnb=dnb, dout=dout, ps_dn=ps_dn:
                       sC.activation(DT[dnb][:, :], ps_dn[:, :], AF.Copy,
                                     scale=SDT[:, dout:dout + 1]), "AM")
                    dn_evac[dnb] = C["AM"]
                    wait("vector", "AM", dn_evac[dnb])
                    wait("vector", "TM", r2_rel_tm[dnb])
                    wait("vector", "AM", r2_rel_am[dnb])
                    op("vector", lambda v, dnb=dnb, dout=dout:
                       v.tensor_add(R2[dnb][:, :], DT[dnb][:, :], X1[:, dout * 512:(dout + 1) * 512]), "VM")
                    dt_rel[dnb] = C["VM"]
                    vm_r2 = C["VM"]
                    wait("scalar", "VM", vm_r2)
                    wait("scalar", "TM", sqf_rel[dnb])
                    op("scalar", lambda sC, dnb=dnb:
                       sC.activation(SQF[dnb][:, :], R2[dnb][:, :], AF.Square), "AM")
                    am_sq = C["AM"]
                    if pend_tail is not None:
                        emit_tail(*pend_tail)
                    pend_tail = (dout, dnb, vm_r2, am_sq)
                if pend_tail is not None:
                    emit_tail(*pend_tail)
                    pend_tail = None
                if debug:
                    wait("sync", "VM", C["VM"])
                    dma("sync", s1b_d[sl], S1B[:, :], "ST")
                    dma("sync", s2b_d[sl], S2B[:, :], "ST")
                    wait("vector", "ST", C["ST"])
                wait("vector", "AM", prm_rel)
                op("vector", lambda v: v.tensor_scalar(S1B[:, :], S1B[:, :], 1.0 / D, None, ALU.mult), "VM")
                op("vector", lambda v: v.tensor_scalar(S2B[:, :], S2B[:, :], 1.0 / D, None, ALU.mult), "VM")
                op("vector", lambda v: v.tensor_mul(TMP[:, :], S1B[:, :], S1B[:, :]), "VM")
                op("vector", lambda v: v.tensor_sub(S2B[:, :], S2B[:, :], TMP[:, :]), "VM")
                op("vector", lambda v: v.tensor_scalar(S2B[:, :], S2B[:, :], EPS, None, ALU.add), "VM")
                wait("scalar", "VM", C["VM"])
                op("scalar", lambda sC: sC.activation(TMP[:, :], S2B[:, :], AF.Sqrt), "AM")
                wait("vector", "AM", C["AM"])
                op("vector", lambda v: v.reciprocal(TMP2[:, :], TMP[:, :]), "VM")
                op("vector", lambda v:
                   v.scalar_tensor_tensor(S1B[:, :], S1B[:, :], -1.0, TMP2[:, :],
                                          ALU.mult, ALU.mult), "VM")
                vm_prm = C["VM"]
                wait("tensor", "VM", vm_prm)
                wait("tensor", "AM", trb_rel[0])
                for tt in range(4):
                    gt_ = sl * 4 + tt
                    op("tensor", lambda t, tt=tt:
                       t.transpose(psA[:, tt * 2:tt * 2 + 1], TMP2[0:1, tt * 128:(tt + 1) * 128],
                                   EYEF[:1, :1]), None)
                    op("tensor", lambda t, tt=tt:
                       t.transpose(psA[:, tt * 2 + 1:tt * 2 + 2], S1B[0:1, tt * 128:(tt + 1) * 128],
                                   EYEF[:1, :1]),
                       "TM" if tt == 3 else None)
                tm_prm = C["TM"]
                wait("scalar", "TM", tm_prm)
                for tt in range(4):
                    gt_ = sl * 4 + tt
                    op("scalar", lambda sC, tt=tt, gt_=gt_:
                       sC.copy(PRMC[:, gt_:gt_ + 1], psA[:, tt * 2:tt * 2 + 1]), None)
                    op("scalar", lambda sC, tt=tt, gt_=gt_:
                       sC.copy(PRMC[:, 32 + gt_:32 + gt_ + 1], psA[:, tt * 2 + 1:tt * 2 + 2]),
                       "AM" if tt == 3 else None)
                prm_rel = C["AM"]
                trb_rel[0] = C["AM"]
                gbank_rel = [C["AM"]] * 4
                fuse_rel = C["TM"]
                x1_rel_tm = C["TM"]
                x1_rel_vm = C["VM"]
            ph5_TM, ph5_AM, ph5_VM, ph5_ST = C["TM"], C["AM"], C["VM"], C["ST"]

            pf6 = FDYN0
            def af6(n):
                nonlocal pf6
                o = pf6; pf6 += n
                assert pf6 <= NF32
                return o
            o_rb = af6(2 * 2048)
            o_ob = af6(2 * 2048)
            RB = [FP[:, o_rb + i * 2048:o_rb + (i + 1) * 2048] for i in range(2)]
            OB = [FP[:, o_ob + i * 2048:o_ob + (i + 1) * 2048] for i in range(2)]
            wait("sync", "TM", ph5_TM)
            wait("sync", "AM", ph5_AM)
            wait("sync", "VM", ph5_VM)
            for sn in ("SR0", "SR1"):
                wait("sync", sn, C.get(sn, 0))
            rb_rel = [0, 0]
            ob_rel = [0, 0]
            for rb_i in range(T // 128):
                bi = rb_i % 2
                wait("sync", "VM", rb_rel[bi])
                dma("sync", RB[bi][:, :], r2s_d[rb_i * 128:(rb_i + 1) * 128, :], f"LR{bi}")
                ld_rb = C[f"LR{bi}"]
                wait("vector", f"LR{bi}", ld_rb)
                wait("vector", f"SO{bi}", ob_rel[bi])
                op("vector", lambda v, bi=bi, rb_i=rb_i:
                   v.tensor_scalar(OB[bi][:, :], RB[bi][:, :], PRMC[:, rb_i:rb_i + 1],
                                   PRMC[:, 32 + rb_i:32 + rb_i + 1],
                                   ALU.mult, ALU.add), "VM")
                rb_rel[bi] = C["VM"]
                wait("sync", "VM", C["VM"])
                dma("sync", out_d[rb_i * 128:(rb_i + 1) * 128, :], OB[bi][:, :], f"SO{bi}")
                ob_rel[bi] = C[f"SO{bi}"]

        if debug and 5 in phases:
            wait("sync", "AM", C["AM"])
            dma("sync", prm_d[:, :], FP[:, o_prmc:o_prmc + 64], "ST")
        for sname in sorted(list(C.keys())):
            if sname in ("TM", "VM", "AM", "GM"):
                continue
            wait("sync", sname, C[sname])

        @block.sync
        def _(eng):
            for f in Q["sync"]:
                f(eng)

        @block.gpsimd
        def _(eng):
            for f in Q["gpsimd"]:
                f(eng)

        @block.tensor
        def _(eng):
            for f in Q["tensor"]:
                f(eng)

        @block.vector
        def _(eng):
            for f in Q["vector"]:
                f(eng)

        @block.scalar
        def _(eng):
            for f in Q["scalar"]:
                f(eng)

    return nc


def _causal_mask():
    m = np.zeros((128, 2048), dtype=np.float32)
    p = np.arange(128)[:, None]
    for r in range(4):
        j = np.arange(512)[None, :]
        m[:, r * 512:(r + 1) * 512] = np.where(j > r * 128 + p, -1e30, 0.0)
    return m.astype(ml_dtypes.bfloat16)


def make_core_inputs(core, inputs):
    X = np.asarray(inputs["L0_attn_x"])
    x_c = np.ascontiguousarray(X[2 * core:2 * core + 2].reshape(T, D))
    return {
        "x": x_c,
        "wqkv": np.ascontiguousarray(inputs["L0_attn_W_qkv"]),
        "bqkv": np.ascontiguousarray(np.asarray(inputs["L0_attn_b_qkv"]).reshape(1, TD)),
        "wout": np.ascontiguousarray(inputs["L0_attn_W_out"]),
        "bout": np.ascontiguousarray(np.asarray(inputs["L0_attn_b_out"]).reshape(1, D)),
        "wg": np.ascontiguousarray(inputs["L0_ffn_w_gate"]),
        "wu": np.ascontiguousarray(inputs["L0_ffn_w_up"]),
        "wd": np.ascontiguousarray(inputs["L0_ffn_w_down"]),
        "sg": np.ascontiguousarray(np.asarray(inputs["L0_ffn_s_gate"], dtype=np.float32).reshape(JC, 128)),
        "su": np.ascontiguousarray(np.asarray(inputs["L0_ffn_s_up"], dtype=np.float32).reshape(JC, 128)),
        "sd": np.ascontiguousarray(np.asarray(inputs["L0_ffn_s_down"], dtype=np.float32).reshape(DC, 128)),
        "eye": np.eye(128, dtype=ml_dtypes.bfloat16),
        "eyef": np.eye(128, dtype=np.float32),
        "onesf": np.ones((128, 128), dtype=np.float32),
        "mask": _causal_mask(),
        "ones": np.ones((128, 512), dtype=ml_dtypes.bfloat16),
    }


def kernel(**inputs):
    nc = build_core_program()
    in_maps = [make_core_inputs(c, inputs) for c in range(NCORES)]
    res = run_bass_kernel_spmd(nc, in_maps, core_ids=list(range(NCORES)))
    out = np.concatenate([r["out"].reshape(NSEQ, SEQ, D) for r in res.results], axis=0)
    return out.astype(np.float32)


# version 08 (final)
# speedup vs baseline: 1.1612x; 1.0208x over previous
import numpy as np
import ml_dtypes
import concourse.bass as bass
import concourse.mybir as mybir
from concourse.bass_utils import run_bass_kernel_spmd
from contextlib import ExitStack

BF16 = mybir.dt.bfloat16
F32 = mybir.dt.float32
I8 = mybir.dt.int8
AF = mybir.ActivationFunctionType
ALU = mybir.AluOpType
AX = mybir.AxisListType

NCORES = 8
SEQ = 2048
NSEQ = 2
T = SEQ * NSEQ
D = 2048
DC = D // 128
H = 16
TD = 3 * D
INTER = 8192
JC = INTER // 128
NHALF = 2
THALF = T // NHALF
INV_SQRT_HD = 1.0 / float(np.float32(ml_dtypes.bfloat16(np.sqrt(128.0))))
EPS = 1e-5


def build_core_program(phases=(0, 2, 3, 4, 5), debug=False):
    nc = bass.Bass("TRN2", target_bir_lowering=False)

    x_d = nc.dram_tensor("x", [T, D], BF16, kind="ExternalInput")
    wqkv_d = nc.dram_tensor("wqkv", [D, TD], BF16, kind="ExternalInput")
    bqkv_d = nc.dram_tensor("bqkv", [1, TD], BF16, kind="ExternalInput")
    wout_d = nc.dram_tensor("wout", [D, D], BF16, kind="ExternalInput")
    bout_d = nc.dram_tensor("bout", [1, D], BF16, kind="ExternalInput")
    wg_d = nc.dram_tensor("wg", [D, INTER], I8, kind="ExternalInput")
    wu_d = nc.dram_tensor("wu", [D, INTER], I8, kind="ExternalInput")
    wd_d = nc.dram_tensor("wd", [INTER, D], I8, kind="ExternalInput")
    sg_d = nc.dram_tensor("sg", [JC, 128], F32, kind="ExternalInput")
    su_d = nc.dram_tensor("su", [JC, 128], F32, kind="ExternalInput")
    sd_d = nc.dram_tensor("sd", [DC, 128], F32, kind="ExternalInput")
    eye_d = nc.dram_tensor("eye", [128, 128], BF16, kind="ExternalInput")
    eyef_d = nc.dram_tensor("eyef", [128, 128], F32, kind="ExternalInput")
    mask_d = nc.dram_tensor("mask", [128, 2048], BF16, kind="ExternalInput")
    ones_d = nc.dram_tensor("ones", [128, 512], BF16, kind="ExternalInput")
    onesf_d = nc.dram_tensor("onesf", [128, 128], F32, kind="ExternalInput")

    out_d = nc.dram_tensor("out", [T, D], F32, kind="ExternalOutput")
    scratch_kind = "ExternalOutput" if debug else "Internal"
    qts_d = nc.dram_tensor("qts", [DC, 128, T], BF16, kind=scratch_kind)
    kts_d = nc.dram_tensor("kts", [DC, 128, T], BF16, kind=scratch_kind)
    vs_d = nc.dram_tensor("vs", [T, D], BF16, kind=scratch_kind)
    atts_d = nc.dram_tensor("atts", [DC, 128, T], BF16, kind=scratch_kind)
    x1s_d = nc.dram_tensor("x1s", [DC, 128, T], BF16, kind=scratch_kind)
    r2s_d = nc.dram_tensor("r2s", [T, D], F32, kind=scratch_kind)
    wgb_d = nc.dram_tensor("wgb", [D, INTER], BF16, kind="Internal")
    wub_d = nc.dram_tensor("wub", [D, INTER], BF16, kind="Internal")
    wdb_d = nc.dram_tensor("wdb", [INTER, D], BF16, kind="Internal")
    prm_d = nc.dram_tensor("prmdbg", [128, 64], F32, kind=scratch_kind) if debug else None
    s1b_d = nc.dram_tensor("s1bdbg", [8, 128, 512], F32, kind=scratch_kind) if debug else None
    s2b_d = nc.dram_tensor("s2bdbg", [8, 128, 512], F32, kind=scratch_kind) if debug else None

    ENGS = ("sync", "gpsimd", "tensor", "vector", "scalar")
    Q = {k: [] for k in ENGS}
    C = {"LD": 0, "LG": 0, "ST": 0, "TM": 0, "VM": 0, "AM": 0, "GM": 0}
    sems = {}
    GETSEM = [lambda name: None]

    def em(qname, fn):
        Q[qname].append(fn)

    def wait(qname, sem, val):
        if val <= 0:
            return
        GETSEM[0](sem)
        em(qname, lambda e, s=sem, v=val: e.wait_ge(sems[s], v))

    def dma(qname, dst, src, sem):
        GETSEM[0](sem)
        C[sem] += 16
        em(qname, lambda e, d=dst, s=src, sm=sem: e.dma_start(d, s).then_inc(sems[sm], 16))

    def dma_t(qname, dst, src, sem):
        GETSEM[0](sem)
        C[sem] += 16
        em(qname, lambda e, d=dst, s=src, sm=sem: e.dma_start_transpose(d, s).then_inc(sems[sm], 16))

    def op(qname, f, sem=None):
        if sem is None:
            em(qname, lambda e, ff=f: ff(e))
        elif qname in ("scalar", "vector", "gpsimd"):
            C[sem] += 1
            em(qname, lambda e, ff=f: ff(e))
            em(qname, lambda e, sm=sem: e.drain().then_inc(sems[sm], 1))
        else:
            C[sem] += 1
            em(qname, lambda e, ff=f, sm=sem: ff(e).then_inc(sems[sm], 1))

    NBF = 78500
    NF32 = 12500

    with ExitStack() as ctx:
        e = ctx.enter_context
        BP = e(nc.sbuf_tensor("bp", [128, NBF], BF16))
        FP = e(nc.sbuf_tensor("fp", [128, NF32], F32))
        psA = e(nc.psum_tensor("psA", [128, 2048], F32))
        psB = e(nc.psum_tensor("psB", [128, 512], F32))
        psC = e(nc.psum_tensor("psC", [128, 512], F32))
        psT0 = e(nc.psum_tensor("psT0", [128, 1024], BF16))
        psT1 = e(nc.psum_tensor("psT1", [128, 1024], BF16))
        psT = [psT0, psT1]
        for s in ("LD", "LG", "ST", "TM", "VM", "AM", "GM"):
            sems[s] = e(nc.semaphore(s))
        def getsem(name):
            if name not in sems:
                sems[name] = e(nc.semaphore(name))
                C[name] = 0
            return sems[name]
        GETSEM[0] = getsem
        block = e(nc.Block())

        pbf = 0
        def abf(n):
            nonlocal pbf
            o = pbf; pbf += n
            assert pbf <= NBF, f"bf16 pool overflow {pbf}"
            return o
        pf = 0
        def af32(n):
            nonlocal pf
            o = pf; pf += n
            assert pf <= NF32, f"f32 pool overflow {pf}"
            return o

        o_eye = abf(128)
        o_mask = abf(2048)
        o_ones = abf(512)
        o_bqkv = abf(TD)
        o_bout = abf(D)
        DYN0 = pbf

        o_eyef = af32(128)
        o_onesf = af32(128)
        o_sgT = af32(JC)
        o_suT = af32(JC)
        o_sdT = af32(DC)
        o_sstage = af32(128)
        o_cols = af32(32)
        o_sq = af32(2048)
        o_nm = af32(8)
        o_nm4 = af32(16)
        o_prmc = af32(64)
        o_srow = af32(1024)
        o_rsb = af32(512)
        FDYN0 = pf

        EYE = BP[:, o_eye:o_eye + 128]
        MASK = BP[:, o_mask:o_mask + 2048]
        ONES = BP[:, o_ones:o_ones + 512]
        BQKV = BP[:, o_bqkv:o_bqkv + TD]
        BOUT = BP[:, o_bout:o_bout + D]
        EYEF = FP[:, o_eyef:o_eyef + 128]
        ONESF = FP[:, o_onesf:o_onesf + 128]
        SGT = FP[:, o_sgT:o_sgT + JC]
        SUT = FP[:, o_suT:o_suT + JC]
        SDT = FP[:, o_sdT:o_sdT + DC]
        SQ = FP[:, o_sq:o_sq + 2048]

        if 0 in phases:
            dma("sync", EYE, eye_d[:, :], "LD")
            dma("sync", MASK, mask_d[:, :], "LD")
            dma("sync", ONES[:, :], ones_d[:, :], "LD")
            dma("sync", BQKV[0:1, :], bqkv_d[:, :], "LD")
            dma("sync", BOUT[0:1, :], bout_d[:, :], "LD")
            dma("sync", EYEF, eyef_d[:, :], "LD")
            dma("sync", ONESF, onesf_d[:, :], "LD")
            sst = FP[:, o_sstage:o_sstage + 128]
            for (src_d, rows, dstap) in ((sg_d, JC, SGT), (su_d, JC, SUT), (sd_d, DC, SDT)):
                dma("sync", sst[:rows, :], src_d[:, :], "LD")
                wait("tensor", "LD", C["LD"])
                op("tensor", lambda t, r=rows: t.transpose(psB[:, 0:r], sst[:r, :], EYEF[:r, :r]), "TM")
                wait("scalar", "TM", C["TM"])
                op("scalar", lambda s, r=rows, dd=dstap: s.copy(dd[:, 0:r], psB[:, 0:r]), "AM")
                wait("tensor", "AM", C["AM"])
                wait("sync", "AM", C["AM"])

        if 2 in phases:
            pbf2 = DYN0
            def abf2(n):
                nonlocal pbf2
                o = pbf2; pbf2 += n
                assert pbf2 <= NBF, f"ph2 bf16 overflow {pbf2}"
                return o
            o_xt = abf2(DC * THALF)
            o_wqk = abf2(2 * DC * 128)
            o_wv = abf2(2 * DC * 512)
            o_qkst = abf2(2 * THALF)
            o_vst = abf2(2 * 512)
            o_wcvt = abf2(2 * 2048)
            XT = [BP[:, o_xt + dc * THALF: o_xt + (dc + 1) * THALF] for dc in range(DC)]
            WQK = [BP[:, o_wqk + i * DC * 128: o_wqk + (i + 1) * DC * 128] for i in range(2)]
            WVb = [BP[:, o_wv + i * DC * 512:o_wv + (i + 1) * DC * 512] for i in range(2)]
            QKST = [BP[:, o_qkst + i * THALF: o_qkst + (i + 1) * THALF] for i in range(2)]
            VST = [BP[:, o_vst + i * 512: o_vst + (i + 1) * 512] for i in range(2)]

            WCVT = [BP[:, o_wcvt + i * 2048:o_wcvt + (i + 1) * 2048] for i in range(2)]
            cvt_rel = [0, 0]
            cvt_jobs = []
            for (srcw, dstw) in ((wg_d, wgb_d), (wu_d, wub_d), (wd_d, wdb_d)):
                R = srcw.shape[0]
                for r0 in range(0, R, 2048):
                    for j in range(srcw.shape[1] // 128):
                        cvt_jobs.append((srcw, dstw, r0, j))
            wcix = 0
            cvt_prev = [0, 0]
            for (srcw, dstw, r0, j) in cvt_jobs:
                wc = wcix % 2
                wcix += 1
                sn = f"LC{wc}"
                wait("gpsimd", sn, cvt_prev[wc])
                dma("gpsimd", WCVT[wc].rearrange("p (r i) -> p r i", r=16),
                    srcw[r0:r0 + 2048, j * 128:(j + 1) * 128]
                    .rearrange("(r p) i -> p r i", p=128), sn)
                wait("gpsimd", sn, C[sn])
                dma("gpsimd", dstw[r0:r0 + 2048, j * 128:(j + 1) * 128]
                    .rearrange("(r p) i -> p r i", p=128),
                    WCVT[wc].rearrange("p (r i) -> p r i", r=16), sn)
                cvt_prev[wc] = C[sn]
            cvt_done = None
            bank_rel = [{"AM": C["AM"], "VM": C["VM"]} for _ in range(4)]
            ech = 0
            qk_store_rel = [0, 0]
            v_store_rel = [0, 0]
            wqk_rel = [0, 0]
            wv_rel = [0, 0]
            xt_rel = 0

            for hf in range(NHALF):
                wait("sync", "TM", xt_rel)
                for dc in range(DC):
                    dma_t("sync", XT[dc], x_d[hf * THALF:(hf + 1) * THALF, dc * 128:(dc + 1) * 128], "LXT")
                ld_xt = C["LXT"]

                def load_wqk(f2):
                    wb2 = f2 % 2
                    wait("sync", "TM", wqk_rel[wb2])
                    dma("sync", WQK[wb2].rearrange("p (dc j) -> p dc j", dc=DC),
                        wqkv_d[:, f2 * 128:(f2 + 1) * 128].rearrange("(dc p) j -> p dc j", p=128),
                        f"LW{wb2}")
                    return C[f"LW{wb2}"]
                ld_wqk = [0] * (2 * DC)
                ld_wqk[0] = load_wqk(0)
                for f in range(2 * DC):
                    wb = f % 2
                    if f + 1 < 2 * DC:
                        ld_wqk[f + 1] = load_wqk(f + 1)
                    ld_w = ld_wqk[f]
                    qb = f % 2
                    wait("scalar", f"SQ{qb}", qk_store_rel[qb])
                    wait("vector", f"SQ{qb}", qk_store_rel[qb])
                    wait("tensor", "LXT", ld_xt)
                    wait("tensor", f"LW{wb}", ld_w)
                    for tb in range(THALF // 512):
                        b = (hf * 144 + f * 4 + tb) % 4
                        wait("tensor", "AM", bank_rel[b]["AM"])
                        wait("tensor", "VM", bank_rel[b]["VM"])
                        for k in range(DC):
                            op("tensor", lambda t, b=b, k=k, wb=wb, tb=tb:
                               t.matmul(psA[:, b * 512:(b + 1) * 512],
                                        WQK[wb][:, k * 128:(k + 1) * 128],
                                        XT[k][:, tb * 512:(tb + 1) * 512],
                                        start=(k == 0), stop=False))
                        op("tensor", lambda t, b=b, f=f:
                           t.matmul(psA[:, b * 512:(b + 1) * 512],
                                    BQKV[0:1, f * 128:(f + 1) * 128],
                                    ONES[0:1, 0:512], start=False, stop=True), "TM")
                        sc = INV_SQRT_HD if f < DC else 1.0
                        ech += 1
                        if ech % 2 == 0:
                            wait("scalar", "TM", C["TM"])
                            op("scalar", lambda s, b=b, qb=qb, tb=tb, sc=sc:
                               s.activation(QKST[qb][:, tb * 512:(tb + 1) * 512],
                                            psA[:, b * 512:(b + 1) * 512], AF.Copy, scale=sc), "AM")
                            bank_rel[b] = {"AM": C["AM"], "VM": 0}
                        else:
                            wait("vector", "TM", C["TM"])
                            op("vector", lambda v, b=b, qb=qb, tb=tb, sc=sc:
                               v.tensor_scalar(QKST[qb][:, tb * 512:(tb + 1) * 512],
                                               psA[:, b * 512:(b + 1) * 512], sc, None, ALU.mult), "VM")
                            bank_rel[b] = {"AM": 0, "VM": C["VM"]}
                    wqk_rel[wb] = C["TM"]
                    wait("sync", "AM", C["AM"])
                    wait("sync", "VM", C["VM"])
                    dst = (qts_d if f < DC else kts_d)[f % DC][:, hf * THALF:(hf + 1) * THALF]
                    dma("sync", dst, QKST[qb][:, :], f"SQ{qb}")
                    qk_store_rel[qb] = C[f"SQ{qb}"]

                def load_wv(vb2):
                    wait("sync", "TM", wv_rel[vb2 % 2])
                    dma("sync", WVb[vb2 % 2].rearrange("p (dc j) -> p dc j", dc=DC),
                        wqkv_d[:, 2 * D + vb2 * 512: 2 * D + (vb2 + 1) * 512]
                        .rearrange("(dc p) j -> p dc j", p=128), f"LV{vb2 % 2}")
                    return C[f"LV{vb2 % 2}"]
                ld_wvs = [0] * 4
                ld_wvs[0] = load_wv(0)
                for vb in range(4):
                    if vb + 1 < 4:
                        ld_wvs[vb + 1] = load_wv(vb + 1)
                    wait("tensor", f"LV{vb % 2}", ld_wvs[vb])
                    for tt in range(THALF // 128):
                        b = (vb * 16 + tt) % 4
                        vsb = tt % 2
                        wait("tensor", "AM", bank_rel[b]["AM"])
                        wait("tensor", "VM", bank_rel[b]["VM"])
                        for k in range(DC):
                            op("tensor", lambda t, b=b, k=k, tt=tt, WV=WVb[vb % 2]:
                               t.matmul(psA[:, b * 512:(b + 1) * 512],
                                        XT[k][:, tt * 128:(tt + 1) * 128],
                                        WV[:, k * 512:(k + 1) * 512],
                                        start=(k == 0), stop=False))
                        op("tensor", lambda t, b=b, vb=vb:
                           t.matmul(psA[:, b * 512:(b + 1) * 512],
                                    ONES[0:1, 0:128],
                                    BQKV[0:1, 2 * D + vb * 512: 2 * D + vb * 512 + 512],
                                    start=False, stop=True), "TM")
                        ech += 1
                        if ech % 2 == 0:
                            wait("scalar", "TM", C["TM"])
                            wait("scalar", f"SV{vsb}", v_store_rel[vsb])
                            op("scalar", lambda s, b=b, vsb=vsb:
                               s.copy(VST[vsb][:, :], psA[:, b * 512:(b + 1) * 512]), "AM")
                            bank_rel[b] = {"AM": C["AM"], "VM": 0}
                            wait("sync", "AM", C["AM"])
                        else:
                            wait("vector", "TM", C["TM"])
                            wait("vector", f"SV{vsb}", v_store_rel[vsb])
                            op("vector", lambda v, b=b, vsb=vsb:
                               v.tensor_copy(VST[vsb][:, :], psA[:, b * 512:(b + 1) * 512]), "VM")
                            bank_rel[b] = {"AM": 0, "VM": C["VM"]}
                            wait("sync", "VM", C["VM"])
                        r0 = hf * THALF + tt * 128
                        dma("sync", vs_d[r0:r0 + 128, vb * 512:(vb + 1) * 512], VST[vsb][:, :], f"SV{vsb}")
                        v_store_rel[vsb] = C[f"SV{vsb}"]
                    wv_rel[vb % 2] = C["TM"]
                xt_rel = C["TM"]
            ph2_TM, ph2_AM, ph2_VM, ph2_ST = C["TM"], C["AM"], C["VM"], C["ST"]
        else:
            ph2_TM = ph2_AM = ph2_VM = ph2_ST = 0

        if 3 in phases:
            pbf3 = DYN0
            def abf3(n):
                nonlocal pbf3
                o = pbf3; pbf3 += n
                assert pbf3 <= NBF
                return o
            o_qh = abf3(2 * SEQ)
            o_kh = abf3(2 * SEQ)
            o_vh = abf3(2 * SEQ)
            o_p = abf3(4 * 2048)
            o_pt = abf3(16 * 512)
            o_ast = abf3(2 * 512)
            QH = [BP[:, o_qh + i * SEQ:o_qh + (i + 1) * SEQ] for i in range(2)]
            KH = [BP[:, o_kh + i * SEQ:o_kh + (i + 1) * SEQ] for i in range(2)]
            VH = [BP[:, o_vh + i * SEQ:o_vh + (i + 1) * SEQ] for i in range(2)]
            PB = [BP[:, o_p + i * 2048:o_p + (i + 1) * 2048] for i in range(4)]
            PT = BP[:, o_pt:o_pt + 16 * 512]
            PT3 = PT.rearrange("p (kt j) -> p kt j", kt=16)
            AST = [BP[:, o_ast + i * 512:o_ast + (i + 1) * 512] for i in range(2)]
            NM = [FP[:, o_nm + i:o_nm + i + 1] for i in range(4)]
            NM4 = FP[:, o_nm4:o_nm4 + 16]
            SR1 = FP[0:1, o_srow:o_srow + 512]
            SR2 = FP[0:1, o_srow + 512:o_srow + 1024]
            RSB = FP[:, o_rsb:o_rsb + 512]

            wait("sync", "TM", ph2_TM)
            wait("sync", "AM", ph2_AM)
            if 2 in phases:
                for sn in ("SQ0", "SQ1", "SV0", "SV1"):
                    wait("sync", sn, C.get(sn, 0))

            qkv_rel = [0, 0]
            bank_rel = [{"AM": C["AM"], "VM": C["VM"]} for _ in range(4)]
            exp_done = [C["AM"]] * 4
            pt_am = C["AM"]
            pt_vm = C["VM"]
            av_done = C["TM"]
            o_evac = C["VM"]
            ast_rel = [0, 0]
            psT_rel = [{"AM": C["AM"], "VM": C["VM"]} for _ in range(2)]
            pb_rel = [C["TM"]] * 4
            r_evac = C["VM"]
            gchunk = 0

            pairs = [(s, h) for s in range(NSEQ) for h in range(H)]
            def emit_loads(idx):
                s, h = pairs[idx]
                hb = idx % 2
                wait("sync", "TM", qkv_rel[hb])
                dma("sync", QH[hb], qts_d[h][:, s * SEQ:(s + 1) * SEQ], f"LH{hb}")
                dma("sync", KH[hb], kts_d[h][:, s * SEQ:(s + 1) * SEQ], f"LH{hb}")
                dma("sync", VH[hb].rearrange("p (kt j) -> p kt j", kt=16),
                    vs_d[s * SEQ:(s + 1) * SEQ, h * 128:(h + 1) * 128]
                    .rearrange("(kt p) j -> p kt j", p=128), f"LH{hb}")
                return C[f"LH{hb}"]

            ld_done = [0] * len(pairs)
            ld_done[0] = emit_loads(0)
            pend_tr = []
            for idx, (s, h) in enumerate(pairs):
                hb = idx % 2
                if idx + 1 < len(pairs):
                    ld_done[idx + 1] = emit_loads(idx + 1)
                wait("tensor", f"LH{hb}", ld_done[idx])
                for qb in range(4):
                    nkt = 4 * (qb + 1)
                    N = 512 * (qb + 1)
                    wait("gpsimd", "TM", av_done)
                    wait("gpsimd", "AM", pt_am)
                    wait("gpsimd", "VM", pt_vm)
                    for qtl in range(3):
                        nz = 3 - qtl
                        qt = qb * 4 + qtl
                        op("gpsimd", lambda g, qt=qt, qtl=qtl, nz=nz:
                           g.memset(PT3[:, qt + 1:qt + 1 + nz, qtl * 128:(qtl + 1) * 128], 0.0), "GM")
                    gm_after = C["GM"]
                    wait("scalar", "TM", av_done)
                    wait("vector", "TM", av_done)
                    def emit_transposes(qt2, pi2, qtl2, exp_am):
                        nonlocal pt_am, pt_vm
                        wait("tensor", "AM", exp_am)
                        wait("tensor", "GM", gm_after)
                        ntr = qt2 + 1
                        for kt in range(ntr):
                            bkk = (kt // 8) % 2
                            if kt % 8 == 0:
                                wait("tensor", "AM", psT_rel[bkk]["AM"])
                                wait("tensor", "VM", psT_rel[bkk]["VM"])
                            lastt = (kt % 8 == 7 or kt == ntr - 1)
                            op("tensor", lambda t, kt=kt, pi2=pi2, bkk=bkk:
                               t.transpose(psT[bkk][:, (kt % 8) * 128:(kt % 8 + 1) * 128],
                                           PB[pi2][:, kt * 128:(kt + 1) * 128], EYE),
                               "TM" if lastt else None)
                            if lastt:
                                ntiles = kt % 8 + 1
                                k0 = kt - ntiles + 1
                                eng3 = "scalar" if bkk == 0 else "vector"
                                wait(eng3, "TM", C["TM"])
                                if eng3 == "scalar":
                                    op(eng3, lambda sC, k0=k0, ntiles=ntiles, qtl2=qtl2, bkk=bkk:
                                       sC.copy(PT3[:, k0:k0 + ntiles, qtl2 * 128:(qtl2 + 1) * 128],
                                               psT[bkk].rearrange("p (kt j) -> p kt j", kt=8)
                                               [:, 0:ntiles, :]), "AM")
                                    psT_rel[bkk] = {"AM": C["AM"], "VM": 0}
                                    pt_am = C["AM"]
                                else:
                                    op(eng3, lambda v, k0=k0, ntiles=ntiles, qtl2=qtl2, bkk=bkk:
                                       v.tensor_copy(PT3[:, k0:k0 + ntiles, qtl2 * 128:(qtl2 + 1) * 128],
                                                     psT[bkk].rearrange("p (kt j) -> p kt j", kt=8)
                                                     [:, 0:ntiles, :]), "VM")
                                    psT_rel[bkk] = {"AM": 0, "VM": C["VM"]}
                                    pt_vm = C["VM"]
                        pb_rel[pi2] = C["TM"]

                    for qtl in range(4):
                        qt = qb * 4 + qtl
                        pi = qt % 4
                        tile_banks = []
                        for c in range(qb + 1):
                            b = gchunk % 4
                            gchunk += 1
                            tile_banks.append((c, b))
                            diag = (c == qb)
                            wait("tensor", "AM", bank_rel[b]["AM"])
                            op("tensor", lambda t, c=c, qt=qt, hb=hb, b=b, diag=diag:
                               t.matmul(psA[:, b * 512:(b + 1) * 512],
                                        QH[hb][:, qt * 128:(qt + 1) * 128],
                                        KH[hb][:, c * 512:(c + 1) * 512],
                                        start=True, stop=not diag),
                               None if diag else "TM")
                            if diag:
                                op("tensor", lambda t, b=b, qtl=qtl:
                                   t.matmul(psA[:, b * 512:(b + 1) * 512],
                                            EYE, MASK[:, qtl * 512:(qtl + 1) * 512],
                                            start=False, stop=True), "TM")
                            mm_tm = C["TM"]
                            wait("vector", "TM", mm_tm)
                            if c == 0:
                                wait("vector", "AM", exp_done[pi])
                            op("vector", lambda v, pi=pi, c=c, b=b:
                               v.reduce_max(NM4[:, pi * 4 + c:pi * 4 + c + 1],
                                            psA[:, b * 512:(b + 1) * 512],
                                            axis=AX.X, negate=True),
                               "VM" if diag else None)
                        op("vector", lambda v, pi=pi, qb=qb:
                           v.tensor_reduce(NM[pi][:, :], NM4[:, pi * 4:pi * 4 + qb + 1],
                                           axis=AX.X, op=ALU.min), "VM")
                        wait("scalar", "VM", C["VM"])
                        wait("scalar", "TM", pb_rel[pi])
                        for c, b in tile_banks:
                            lastc = (c == qb)
                            op("scalar", lambda sC, pi=pi, c=c, b=b:
                               sC.activation(PB[pi][:, c * 512:(c + 1) * 512],
                                             psA[:, b * 512:(b + 1) * 512], AF.Exp,
                                             bias=NM[pi][:, :1], scale=1.0),
                               "AM" if lastc else None)
                        exp_done[pi] = C["AM"]
                        for c, b in tile_banks:
                            bank_rel[b] = {"AM": exp_done[pi], "VM": 0}
                        pend_tr.append((qt, pi, qtl, exp_done[pi]))
                        if len(pend_tr) > 3:
                            emit_transposes(*pend_tr.pop(0))
                    while pend_tr:
                        emit_transposes(*pend_tr.pop(0))
                    wait("tensor", "AM", pt_am)
                    wait("tensor", "VM", pt_vm)
                    wait("tensor", "GM", gm_after)
                    wait("tensor", "VM", r_evac)
                    for kt in range(nkt):
                        op("tensor", lambda t, kt=kt, nkt=nkt:
                           t.matmul(psC[0:1, :], ONES[:, 0:1],
                                    PT[:, kt * 512:(kt + 1) * 512],
                                    start=(kt == 0), stop=(kt == nkt - 1)),
                           "TM" if kt == nkt - 1 else None)
                    sums_tm = C["TM"]
                    wait("scalar", "TM", sums_tm)
                    op("scalar", lambda sC: sC.activation(SR2[:, :], psC[0:1, :], AF.Ln), "AM")
                    op("scalar", lambda sC: sC.activation(SR1[:, :], SR2[:, :], AF.Exp, scale=-1.0), "AM")
                    srow_am = C["AM"]
                    wait("tensor", "AM", srow_am)
                    op("tensor", lambda t:
                       t.matmul(psC[:, :], ONESF[0:1, 0:128], SR1[:, :],
                                start=True, stop=True), "TM")
                    bcast_tm = C["TM"]
                    wait("vector", "TM", bcast_tm)
                    op("vector", lambda v: v.tensor_copy(RSB[:, :], psC[:, :]), "VM")
                    r_evac = C["VM"]
                    wait("tensor", "VM", o_evac)
                    for kt in range(nkt):
                        op("tensor", lambda t, kt=kt, hb=hb, nkt=nkt:
                           t.matmul(psB[:, :], VH[hb][:, kt * 128:(kt + 1) * 128],
                                    PT[:, kt * 512:(kt + 1) * 512],
                                    start=(kt == 0), stop=(kt == nkt - 1)),
                           "TM" if kt == nkt - 1 else None)
                    av_done = C["TM"]
                    ab = qb % 2
                    wait("vector", "TM", av_done)
                    wait("vector", "VM", r_evac)
                    wait("vector", f"SA{ab}", ast_rel[ab])
                    op("vector", lambda v, ab=ab:
                       v.tensor_mul(AST[ab][:, :], psB[:, :], RSB[:, :]), "VM")
                    o_evac = C["VM"]
                    wait("sync", "VM", o_evac)
                    dma("sync", atts_d[h][:, s * SEQ + qb * 512: s * SEQ + (qb + 1) * 512],
                        AST[ab][:, :], f"SA{ab}")
                    ast_rel[ab] = C[f"SA{ab}"]
                qkv_rel[hb] = C["TM"]
            ph3_TM, ph3_AM, ph3_VM, ph3_ST = C["TM"], C["AM"], C["VM"], C["ST"]
        else:
            ph3_TM, ph3_AM, ph3_VM, ph3_ST = ph2_TM, ph2_AM, ph2_VM, ph2_ST


        if 4 in phases:
            pbf4 = DYN0
            def abf4(n):
                nonlocal pbf4
                o = pbf4; pbf4 += n
                assert pbf4 <= NBF
                return o
            o_attb = abf4(2 * DC * 512)
            o_wo = abf4(2 * DC * 512)
            o_xb = abf4(2 * 4 * D)
            o_r1 = abf4(4 * D)
            o_x1t = abf4(D)
            o_x1f = abf4(2 * D)
            ATTB = [BP[:, o_attb + i * DC * 512:o_attb + (i + 1) * DC * 512] for i in range(2)]
            WO = [BP[:, o_wo + i * DC * 512:o_wo + (i + 1) * DC * 512] for i in range(2)]
            XB = [BP[:, o_xb + i * 4 * D:o_xb + (i + 1) * 4 * D] for i in range(2)]
            R1 = [BP[:, o_r1 + tt * D:o_r1 + (tt + 1) * D] for tt in range(4)]
            X1T = BP[:, o_x1t:o_x1t + D]
            X1F = [BP[:, o_x1f + i * D:o_x1f + (i + 1) * D] for i in range(2)]
            COLS = FP[:, o_cols:o_cols + 32]

            wait("sync", "TM", ph3_TM)
            wait("sync", "AM", ph3_AM)
            wait("sync", "VM", ph3_VM)
            if 3 in phases:
                for sn in ("SA0", "SA1"):
                    wait("sync", sn, C.get(sn, 0))
            if 2 in phases:
                wait("scalar", "LC0", C.get("LC0", 0))
                wait("scalar", "LC1", C.get("LC1", 0))
                wait("vector", "LC0", C.get("LC0", 0))
                wait("vector", "LC1", C.get("LC1", 0))

            attb_rel = [0, 0]
            wo_rel = [0, 0]
            xb_rel = [0, 0]
            bank_rel4 = [{"AM": C["AM"], "VM": C["VM"]} for _ in range(4)]
            x1f_rel = [0, 0]
            psT_rel4 = [C["AM"], C["AM"]]
            x1t_rel = C["TM"]
            r1_rel = C["AM"]
            cols_rel = [C["VM"], C["VM"]]

            for tb in range(8):
                bb = tb % 2
                wait("sync", "TM", attb_rel[bb])
                dma("sync", ATTB[bb].rearrange("p (dc j) -> p dc j", dc=DC),
                    atts_d[:, :, tb * 512:(tb + 1) * 512].rearrange("dc p j -> p dc j"), f"LT{bb}")
                wait("sync", "VM", xb_rel[bb])
                dma("sync", XB[bb].rearrange("p (tt j) -> p tt j", tt=4),
                    x_d[tb * 512:(tb + 1) * 512, :].rearrange("(tt p) j -> p tt j", p=128), f"LT{bb}")
                ld_tb = C[f"LT{bb}"]
                def load_wo(db2):
                    wb2 = db2 % 2
                    wait("sync", "TM", wo_rel[wb2])
                    dma("sync", WO[wb2].rearrange("p (dc j) -> p dc j", dc=DC),
                        wout_d[:, db2 * 512:(db2 + 1) * 512].rearrange("(dc p) j -> p dc j", p=128),
                        f"LO{wb2}")
                    return C[f"LO{wb2}"]
                if tb == 0:
                    ld_wo = [0] * 4
                    ld_wo[0] = load_wo(0)
                for db in range(4):
                    wb = db % 2
                    if db + 1 < 4:
                        ld_wo[db + 1] = load_wo(db + 1)
                    elif tb < 7:
                        ld_wo[0] = load_wo(0)
                    ld_w = ld_wo[db]
                    wait("tensor", f"LT{bb}", ld_tb)
                    wait("tensor", f"LO{db % 2}", ld_w)
                    for tt in range(4):
                        b = (tb * 16 + db * 4 + tt) % 4
                        wait("tensor", "AM", bank_rel4[b]["AM"])
                        wait("tensor", "VM", bank_rel4[b]["VM"])
                        for k in range(DC):
                            op("tensor", lambda t, b=b, k=k, bb=bb, wb=wb, tt=tt:
                               t.matmul(psA[:, b * 512:(b + 1) * 512],
                                        ATTB[bb][:, k * 512 + tt * 128: k * 512 + (tt + 1) * 128],
                                        WO[wb][:, k * 512:(k + 1) * 512],
                                        start=(k == 0), stop=False))
                        op("tensor", lambda t, b=b, db=db:
                           t.matmul(psA[:, b * 512:(b + 1) * 512],
                                    ONES[0:1, 0:128], BOUT[0:1, db * 512:(db + 1) * 512],
                                    start=False, stop=True), "TM")
                        wait("vector", "TM", C["TM"])
                        wait("vector", "AM", r1_rel)
                        op("vector", lambda v, b=b, bb=bb, tt=tt, db=db:
                           v.tensor_add(R1[tt][:, db * 512:(db + 1) * 512],
                                        psA[:, b * 512:(b + 1) * 512],
                                        XB[bb][:, tt * D + db * 512: tt * D + (db + 1) * 512]), "VM")
                        bank_rel4[b] = {"AM": C["AM"], "VM": C["VM"]}
                    wo_rel[wb] = C["TM"]
                attb_rel[bb] = C["TM"]
                xb_rel[bb] = C["VM"]
                vm_r1 = C["VM"]
                for tt in range(4):
                    par = tt % 2
                    cs = par * 16
                    S1 = COLS[:, cs + 0:cs + 1]; S2 = COLS[:, cs + 1:cs + 2]
                    M = COLS[:, cs + 2:cs + 3]; MS = COLS[:, cs + 3:cs + 4]
                    M2 = COLS[:, cs + 4:cs + 5]; VAR = COLS[:, cs + 5:cs + 6]
                    SD = COLS[:, cs + 6:cs + 7]; RSTD = COLS[:, cs + 7:cs + 8]
                    NMR = COLS[:, cs + 8:cs + 9]
                    wait("vector", "VM", cols_rel[par])
                    op("vector", lambda v, tt=tt, S1=S1: v.reduce_sum(S1, R1[tt][:, :], axis=AX.X), "VM")
                    wait("scalar", "VM", C["VM"])
                    op("scalar", lambda sC, tt=tt:
                       sC.activation(SQ[:, :], R1[tt][:, :], AF.Square), "AM")
                    op("vector", lambda v, S1=S1, M=M: v.tensor_scalar(M, S1, 1.0 / D, None, ALU.mult), "VM")
                    wait("vector", "AM", C["AM"])
                    op("vector", lambda v, S2=S2: v.reduce_sum(S2, SQ[:, :], axis=AX.X), "VM")
                    op("vector", lambda v, S2=S2, MS=MS: v.tensor_scalar(MS, S2, 1.0 / D, None, ALU.mult), "VM")
                    op("vector", lambda v, M=M, M2=M2: v.tensor_mul(M2, M, M), "VM")
                    op("vector", lambda v, MS=MS, M2=M2, VAR=VAR: v.tensor_sub(VAR, MS, M2), "VM")
                    op("vector", lambda v, VAR=VAR: v.tensor_scalar(VAR, VAR, EPS, None, ALU.add), "VM")
                    wait("scalar", "VM", C["VM"])
                    op("scalar", lambda sC, VAR=VAR, SD=SD:
                       sC.activation(SD, VAR, AF.Sqrt), "AM")
                    wait("vector", "AM", C["AM"])
                    op("vector", lambda v, SD=SD, RSTD=RSTD: v.reciprocal(RSTD, SD), "VM")
                    op("vector", lambda v, M=M, RSTD=RSTD, NMR=NMR:
                       v.scalar_tensor_tensor(NMR, M, -1.0, RSTD, ALU.mult, ALU.mult), "VM")
                    wait("vector", "TM", x1t_rel)
                    op("vector", lambda v, tt=tt, RSTD=RSTD, NMR=NMR:
                       v.tensor_scalar(X1T[:, :], R1[tt][:, :], RSTD[:, :1], NMR[:, :1],
                                       ALU.mult, ALU.add), "VM")
                    cols_rel[par] = C["VM"]
                    vm_x1t = C["VM"]
                    xf = tt % 2
                    wait("tensor", "VM", vm_x1t)
                    wait("tensor", f"SX{xf}", x1f_rel[xf])
                    for dc in range(DC):
                        bkk = (dc // 8) % 2
                        if dc % 8 == 0:
                            wait("tensor", "AM", psT_rel4[bkk])
                        op("tensor", lambda t, dc=dc, bkk=bkk:
                           t.transpose(psT[bkk][:, (dc % 8) * 128:(dc % 8 + 1) * 128],
                                       X1T[:, dc * 128:(dc + 1) * 128], EYE),
                           "TM" if dc % 8 == 7 else None)
                        if dc % 8 == 7:
                            wait("scalar", "TM", C["TM"])
                            op("scalar", lambda sC, dc=dc, bkk=bkk, xf=xf:
                               sC.copy(X1F[xf][:, (dc - 7) * 128:(dc + 1) * 128],
                                       psT[bkk][:, :]), "AM")
                            psT_rel4[bkk] = C["AM"]
                    x1t_rel = C["TM"]
                    wait("sync", "AM", C["AM"])
                    dma("sync", x1s_d[:, :, tb * 512 + tt * 128: tb * 512 + (tt + 1) * 128]
                        .rearrange("dc p j -> p dc j"),
                        X1F[xf].rearrange("p (dc j) -> p dc j", dc=DC), f"SX{xf}")
                    x1f_rel[xf] = C[f"SX{xf}"]
                r1_rel = C["AM"]
            ph4_TM, ph4_AM, ph4_VM, ph4_ST = C["TM"], C["AM"], C["VM"], C["ST"]
        else:
            ph4_TM, ph4_AM, ph4_VM, ph4_ST = ph3_TM, ph3_AM, ph3_VM, ph3_ST

        if 5 in phases:
            pbf5 = DYN0
            def abf5(n):
                nonlocal pbf5
                o = pbf5; pbf5 += n
                assert pbf5 <= NBF, f"ph5 bf16 overflow {pbf5}"
                return o
            o_fuse = abf5(JC * 512)
            o_x1 = abf5(DC * 512)
            o_wg = abf5(2 * DC * 128)
            o_wu = abf5(2 * DC * 128)
            o_wd = abf5(2 * JC * 128)
            o_gt = abf5(2 * 512)
            o_ut = abf5(2 * 512)
            FUSE = BP[:, o_fuse:o_fuse + JC * 512]
            X1 = BP[:, o_x1:o_x1 + DC * 512]
            WG = [BP[:, o_wg + i * DC * 128:o_wg + (i + 1) * DC * 128] for i in range(2)]
            WU = [BP[:, o_wu + i * DC * 128:o_wu + (i + 1) * DC * 128] for i in range(2)]
            WD = [BP[:, o_wd + i * JC * 128:o_wd + (i + 1) * JC * 128] for i in range(2)]
            GT = [BP[:, o_gt + i * 512:o_gt + (i + 1) * 512] for i in range(2)]
            UT = [BP[:, o_ut + i * 512:o_ut + (i + 1) * 512] for i in range(2)]
            pf5 = FDYN0
            def af5(n):
                nonlocal pf5
                o = pf5; pf5 += n
                assert pf5 <= NF32, f"ph5 f32 overflow {pf5}"
                return o
            o_r2 = af5(2 * 512)
            o_dt = af5(2 * 512)
            o_sqf = af5(2 * 512)
            o_s1b = af5(512)
            o_s2b = af5(512)
            o_tmp = af5(512)
            o_tmp2 = af5(512)
            o_ostg = af5(2 * 512)
            R2 = [FP[:, o_r2 + i * 512:o_r2 + (i + 1) * 512] for i in range(2)]
            DT = [FP[:, o_dt + i * 512:o_dt + (i + 1) * 512] for i in range(2)]
            SQF = [FP[:, o_sqf + i * 512:o_sqf + (i + 1) * 512] for i in range(2)]
            S1B = FP[:, o_s1b:o_s1b + 512]
            S2B = FP[:, o_s2b:o_s2b + 512]
            TMP = FP[:, o_tmp:o_tmp + 512]
            TMP2 = FP[:, o_tmp2:o_tmp2 + 512]
            OSTG = [FP[:, o_ostg + i * 512:o_ostg + (i + 1) * 512] for i in range(2)]
            PRMC = FP[:, o_prmc:o_prmc + 64]

            wait("sync", "TM", ph4_TM)
            wait("sync", "AM", ph4_AM)
            wait("sync", "VM", ph4_VM)
            if 2 in phases:
                wait("sync", "LC0", C.get("LC0", 0))
                wait("sync", "LC1", C.get("LC1", 0))
                wait("gpsimd", "LC0", C.get("LC0", 0))
                wait("gpsimd", "LC1", C.get("LC1", 0))
            if 4 in phases:
                wait("gpsimd", "SX0", C.get("SX0", 0))
                wait("gpsimd", "SX1", C.get("SX1", 0))
            wait("gpsimd", "TM", ph4_TM)
            wait("gpsimd", "AM", ph4_AM)
            wait("gpsimd", "VM", ph4_VM)

            wg_rel = [0, 0]
            wd_rel = [0, 0]
            gtu_rel = [0, 0]
            gbank_rel = [C["AM"]] * 4
            fuse_rel = 0
            x1_rel_tm = 0
            x1_rel_vm = 0
            x1_rel_am = 0
            dn_evac = [C["AM"], C["AM"]]
            r2_rel_tm = [C["TM"], C["TM"]]
            r2_rel_am = [C["AM"], C["AM"]]
            dt_rel = [C["VM"], C["VM"]]
            sqf_rel = [C["TM"], C["TM"]]
            trb_rel = [C["AM"], C["AM"]]
            s12_rel = [C["VM"], C["VM"]]
            ostg_rel = [0, 0]
            prm_rel = 0

            for sl in range(8):
                if 4 in phases:
                    for sn in ("SX0", "SX1"):
                        wait("sync", sn, C.get(sn, 0))
                wait("sync", "TM", x1_rel_tm)
                wait("sync", "VM", x1_rel_vm)
                dma("sync", X1.rearrange("p (dc j) -> p dc j", dc=DC),
                    x1s_d[:, :, sl * 512:(sl + 1) * 512].rearrange("dc p j -> p dc j"), "LX")
                ld_x1 = C["LX"]
                for j in range(JC):
                    wb = j % 2
                    wait("sync", "TM", wg_rel[wb])
                    dma("sync", WG[wb].rearrange("p (dc i) -> p dc i", dc=DC),
                        wgb_d[:, j * 128:(j + 1) * 128].rearrange("(dc p) i -> p dc i", p=128), f"LWG{wb}")
                    dma("sync", WU[wb].rearrange("p (dc i) -> p dc i", dc=DC),
                        wub_d[:, j * 128:(j + 1) * 128].rearrange("(dc p) i -> p dc i", p=128), f"LWG{wb}")
                    ld_w = C[f"LWG{wb}"]
                    bg = (2 * j) % 4
                    bu = (2 * j + 1) % 4
                    wait("tensor", "LX", ld_x1)
                    wait("tensor", f"LWG{wb}", ld_w)
                    wait("tensor", "AM", max(gbank_rel[bg], gbank_rel[bu]))
                    for k in range(DC):
                        op("tensor", lambda t, bg=bg, k=k, wb=wb:
                           t.matmul(psA[:, bg * 512:(bg + 1) * 512],
                                    WG[wb][:, k * 128:(k + 1) * 128],
                                    X1[:, k * 512:(k + 1) * 512],
                                    start=(k == 0), stop=(k == DC - 1)),
                           "TM" if k == DC - 1 else None)
                    for k in range(DC):
                        op("tensor", lambda t, bu=bu, k=k, wb=wb:
                           t.matmul(psA[:, bu * 512:(bu + 1) * 512],
                                    WU[wb][:, k * 128:(k + 1) * 128],
                                    X1[:, k * 512:(k + 1) * 512],
                                    start=(k == 0), stop=(k == DC - 1)),
                           "TM" if k == DC - 1 else None)
                    wg_rel[wb] = C["TM"]
                    gp = j % 2
                    wait("scalar", "TM", C["TM"])
                    wait("scalar", "VM", gtu_rel[gp])
                    op("scalar", lambda sC, bg=bg, j=j, gp=gp:
                       sC.activation(GT[gp][:, :], psA[:, bg * 512:(bg + 1) * 512], AF.Silu,
                                     scale=SGT[:, j:j + 1]), None)
                    op("scalar", lambda sC, bu=bu, j=j, gp=gp:
                       sC.activation(UT[gp][:, :], psA[:, bu * 512:(bu + 1) * 512], AF.Copy,
                                     scale=SUT[:, j:j + 1]), "AM")
                    gbank_rel[bg] = C["AM"]
                    gbank_rel[bu] = C["AM"]
                    wait("vector", "AM", C["AM"])
                    wait("vector", "TM", fuse_rel)
                    op("vector", lambda v, j=j, gp=gp:
                       v.tensor_mul(FUSE[:, j * 512:(j + 1) * 512], GT[gp][:, :], UT[gp][:, :]), "VM")
                    gtu_rel[gp] = C["VM"]
                vm_fuse = C["VM"]
                def emit_tail(dout, dnb, vm_r2, am_sq):
                    nonlocal s12_rel, sqf_rel, r2_rel_tm, r2_rel_am, trb_rel, ostg_rel
                    wait("tensor", "VM", vm_r2)
                    wait("tensor", "VM", s12_rel[0])
                    op("tensor", lambda t, dnb=dnb:
                       t.matmul(psA[:, 2 * 512:3 * 512], ONESF, R2[dnb][:, :],
                                start=True, stop=True), "TM")
                    tm_s1 = C["TM"]
                    wait("tensor", "AM", am_sq)
                    wait("tensor", "VM", s12_rel[1])
                    op("tensor", lambda t, dnb=dnb:
                       t.matmul(psA[:, 3 * 512:4 * 512], ONESF, SQF[dnb][:, :],
                                start=True, stop=True), "TM")
                    tm_s2 = C["TM"]
                    sqf_rel[dnb] = tm_s2
                    wait("vector", "TM", tm_s1)
                    if dout == 0:
                        op("vector", lambda v: v.tensor_copy(S1B[:, :], psA[:, 2 * 512:3 * 512]), "VM")
                    else:
                        op("vector", lambda v: v.tensor_add(S1B[:, :], S1B[:, :], psA[:, 2 * 512:3 * 512]), "VM")
                    s12_rel[0] = C["VM"]
                    wait("vector", "TM", tm_s2)
                    if dout == 0:
                        op("vector", lambda v: v.tensor_copy(S2B[:, :], psA[:, 3 * 512:4 * 512]), "VM")
                    else:
                        op("vector", lambda v: v.tensor_add(S2B[:, :], S2B[:, :], psA[:, 3 * 512:4 * 512]), "VM")
                    s12_rel[1] = C["VM"]
                    bt = dout % 2
                    wait("tensor", "AM", trb_rel[bt])
                    for tt in range(4):
                        op("tensor", lambda t, tt=tt, dnb=dnb, bt=bt:
                           t.transpose(psA[:, bt * 512 + tt * 128: bt * 512 + (tt + 1) * 128],
                                       R2[dnb][:, tt * 128:(tt + 1) * 128], EYEF),
                           "TM" if tt == 3 else None)
                    r2_rel_tm[dnb] = C["TM"]
                    r2_rel_am[dnb] = am_sq
                    ob = dout % 2
                    wait("scalar", "TM", C["TM"])
                    wait("scalar", f"SR{ob}", ostg_rel[ob])
                    op("scalar", lambda sC, bt=bt, ob=ob:
                       sC.copy(OSTG[ob][:, :], psA[:, bt * 512:(bt + 1) * 512]), "AM")
                    trb_rel[bt] = C["AM"]
                    wait("sync", "AM", C["AM"])
                    dma("sync",
                        r2s_d[sl * 512:(sl + 1) * 512, dout * 128:(dout + 1) * 128]
                        .rearrange("(tt p) j -> p tt j", p=128),
                        OSTG[ob].rearrange("p (tt j) -> p tt j", tt=4), f"SR{ob}")
                    ostg_rel[ob] = C[f"SR{ob}"]

                pend_tail = None
                for dout in range(DC):
                    wb = dout % 2
                    wait("gpsimd", "TM", wd_rel[wb])
                    dma("gpsimd", WD[wb].rearrange("p (jc i) -> p jc i", jc=JC),
                        wdb_d[:, dout * 128:(dout + 1) * 128].rearrange("(jc p) i -> p jc i", p=128),
                        f"LWD{wb}")
                    ld_w = C[f"LWD{wb}"]
                    dnb = dout % 2
                    ps_dn = psB if dnb == 0 else psC
                    wait("tensor", f"LWD{dout % 2}", ld_w)
                    wait("tensor", "VM", vm_fuse)
                    wait("tensor", "AM", dn_evac[dnb])
                    for jc in range(JC):
                        op("tensor", lambda t, jc=jc, wb=wb, ps_dn=ps_dn:
                           t.matmul(ps_dn[:, :], WD[wb][:, jc * 128:(jc + 1) * 128],
                                    FUSE[:, jc * 512:(jc + 1) * 512],
                                    start=(jc == 0), stop=(jc == JC - 1)),
                           "TM" if jc == JC - 1 else None)
                    wd_rel[wb] = C["TM"]
                    dn_done = C["TM"]
                    wait("scalar", "TM", dn_done)
                    wait("scalar", "VM", dt_rel[dnb])
                    op("scalar", lambda sC, dnb=dnb, dout=dout, ps_dn=ps_dn:
                       sC.activation(DT[dnb][:, :], ps_dn[:, :], AF.Copy,
                                     scale=SDT[:, dout:dout + 1]), "AM")
                    dn_evac[dnb] = C["AM"]
                    wait("vector", "AM", dn_evac[dnb])
                    wait("vector", "TM", r2_rel_tm[dnb])
                    wait("vector", "AM", r2_rel_am[dnb])
                    op("vector", lambda v, dnb=dnb, dout=dout:
                       v.tensor_add(R2[dnb][:, :], DT[dnb][:, :], X1[:, dout * 512:(dout + 1) * 512]), "VM")
                    dt_rel[dnb] = C["VM"]
                    vm_r2 = C["VM"]
                    wait("scalar", "VM", vm_r2)
                    wait("scalar", "TM", sqf_rel[dnb])
                    op("scalar", lambda sC, dnb=dnb:
                       sC.activation(SQF[dnb][:, :], R2[dnb][:, :], AF.Square), "AM")
                    am_sq = C["AM"]
                    if pend_tail is not None:
                        emit_tail(*pend_tail)
                    pend_tail = (dout, dnb, vm_r2, am_sq)
                if pend_tail is not None:
                    emit_tail(*pend_tail)
                    pend_tail = None
                if debug:
                    wait("sync", "VM", C["VM"])
                    dma("sync", s1b_d[sl], S1B[:, :], "ST")
                    dma("sync", s2b_d[sl], S2B[:, :], "ST")
                    wait("vector", "ST", C["ST"])
                wait("vector", "AM", prm_rel)
                op("vector", lambda v: v.tensor_scalar(S1B[:, :], S1B[:, :], 1.0 / D, None, ALU.mult), "VM")
                op("vector", lambda v: v.tensor_scalar(S2B[:, :], S2B[:, :], 1.0 / D, None, ALU.mult), "VM")
                op("vector", lambda v: v.tensor_mul(TMP[:, :], S1B[:, :], S1B[:, :]), "VM")
                op("vector", lambda v: v.tensor_sub(S2B[:, :], S2B[:, :], TMP[:, :]), "VM")
                op("vector", lambda v: v.tensor_scalar(S2B[:, :], S2B[:, :], EPS, None, ALU.add), "VM")
                wait("scalar", "VM", C["VM"])
                op("scalar", lambda sC: sC.activation(TMP[:, :], S2B[:, :], AF.Sqrt), "AM")
                wait("vector", "AM", C["AM"])
                op("vector", lambda v: v.reciprocal(TMP2[:, :], TMP[:, :]), "VM")
                op("vector", lambda v:
                   v.scalar_tensor_tensor(S1B[:, :], S1B[:, :], -1.0, TMP2[:, :],
                                          ALU.mult, ALU.mult), "VM")
                vm_prm = C["VM"]
                wait("tensor", "VM", vm_prm)
                wait("tensor", "AM", trb_rel[0])
                for tt in range(4):
                    gt_ = sl * 4 + tt
                    op("tensor", lambda t, tt=tt:
                       t.transpose(psA[:, tt * 2:tt * 2 + 1], TMP2[0:1, tt * 128:(tt + 1) * 128],
                                   EYEF[:1, :1]), None)
                    op("tensor", lambda t, tt=tt:
                       t.transpose(psA[:, tt * 2 + 1:tt * 2 + 2], S1B[0:1, tt * 128:(tt + 1) * 128],
                                   EYEF[:1, :1]),
                       "TM" if tt == 3 else None)
                tm_prm = C["TM"]
                wait("scalar", "TM", tm_prm)
                for tt in range(4):
                    gt_ = sl * 4 + tt
                    op("scalar", lambda sC, tt=tt, gt_=gt_:
                       sC.copy(PRMC[:, gt_:gt_ + 1], psA[:, tt * 2:tt * 2 + 1]), None)
                    op("scalar", lambda sC, tt=tt, gt_=gt_:
                       sC.copy(PRMC[:, 32 + gt_:32 + gt_ + 1], psA[:, tt * 2 + 1:tt * 2 + 2]),
                       "AM" if tt == 3 else None)
                prm_rel = C["AM"]
                trb_rel[0] = C["AM"]
                gbank_rel = [C["AM"]] * 4
                fuse_rel = C["TM"]
                x1_rel_tm = C["TM"]
                x1_rel_vm = C["VM"]
            ph5_TM, ph5_AM, ph5_VM, ph5_ST = C["TM"], C["AM"], C["VM"], C["ST"]

            pf6 = FDYN0
            def af6(n):
                nonlocal pf6
                o = pf6; pf6 += n
                assert pf6 <= NF32
                return o
            o_rb = af6(2 * 2048)
            o_ob = af6(2 * 2048)
            RB = [FP[:, o_rb + i * 2048:o_rb + (i + 1) * 2048] for i in range(2)]
            OB = [FP[:, o_ob + i * 2048:o_ob + (i + 1) * 2048] for i in range(2)]
            wait("sync", "TM", ph5_TM)
            wait("sync", "AM", ph5_AM)
            wait("sync", "VM", ph5_VM)
            for sn in ("SR0", "SR1"):
                wait("sync", sn, C.get(sn, 0))
            rb_rel = [0, 0]
            ob_rel = [0, 0]
            for rb_i in range(T // 128):
                bi = rb_i % 2
                wait("sync", "VM", rb_rel[bi])
                dma("sync", RB[bi][:, :], r2s_d[rb_i * 128:(rb_i + 1) * 128, :], f"LR{bi}")
                ld_rb = C[f"LR{bi}"]
                wait("vector", f"LR{bi}", ld_rb)
                wait("vector", f"SO{bi}", ob_rel[bi])
                op("vector", lambda v, bi=bi, rb_i=rb_i:
                   v.tensor_scalar(OB[bi][:, :], RB[bi][:, :], PRMC[:, rb_i:rb_i + 1],
                                   PRMC[:, 32 + rb_i:32 + rb_i + 1],
                                   ALU.mult, ALU.add), "VM")
                rb_rel[bi] = C["VM"]
                wait("sync", "VM", C["VM"])
                dma("sync", out_d[rb_i * 128:(rb_i + 1) * 128, :], OB[bi][:, :], f"SO{bi}")
                ob_rel[bi] = C[f"SO{bi}"]

        if debug and 5 in phases:
            wait("sync", "AM", C["AM"])
            dma("sync", prm_d[:, :], FP[:, o_prmc:o_prmc + 64], "ST")
        for sname in sorted(list(C.keys())):
            if sname in ("TM", "VM", "AM", "GM"):
                continue
            wait("sync", sname, C[sname])

        @block.sync
        def _(eng):
            for f in Q["sync"]:
                f(eng)

        @block.gpsimd
        def _(eng):
            for f in Q["gpsimd"]:
                f(eng)

        @block.tensor
        def _(eng):
            for f in Q["tensor"]:
                f(eng)

        @block.vector
        def _(eng):
            for f in Q["vector"]:
                f(eng)

        @block.scalar
        def _(eng):
            for f in Q["scalar"]:
                f(eng)

    return nc


def _causal_mask():
    m = np.zeros((128, 2048), dtype=np.float32)
    p = np.arange(128)[:, None]
    for r in range(4):
        j = np.arange(512)[None, :]
        m[:, r * 512:(r + 1) * 512] = np.where(j > r * 128 + p, -1e30, 0.0)
    return m.astype(ml_dtypes.bfloat16)


def make_core_inputs(core, inputs):
    X = np.asarray(inputs["L0_attn_x"])
    x_c = np.ascontiguousarray(X[2 * core:2 * core + 2].reshape(T, D))
    return {
        "x": x_c,
        "wqkv": np.ascontiguousarray(inputs["L0_attn_W_qkv"]),
        "bqkv": np.ascontiguousarray(np.asarray(inputs["L0_attn_b_qkv"]).reshape(1, TD)),
        "wout": np.ascontiguousarray(inputs["L0_attn_W_out"]),
        "bout": np.ascontiguousarray(np.asarray(inputs["L0_attn_b_out"]).reshape(1, D)),
        "wg": np.ascontiguousarray(inputs["L0_ffn_w_gate"]),
        "wu": np.ascontiguousarray(inputs["L0_ffn_w_up"]),
        "wd": np.ascontiguousarray(inputs["L0_ffn_w_down"]),
        "sg": np.ascontiguousarray(np.asarray(inputs["L0_ffn_s_gate"], dtype=np.float32).reshape(JC, 128)),
        "su": np.ascontiguousarray(np.asarray(inputs["L0_ffn_s_up"], dtype=np.float32).reshape(JC, 128)),
        "sd": np.ascontiguousarray(np.asarray(inputs["L0_ffn_s_down"], dtype=np.float32).reshape(DC, 128)),
        "eye": np.eye(128, dtype=ml_dtypes.bfloat16),
        "eyef": np.eye(128, dtype=np.float32),
        "onesf": np.ones((128, 128), dtype=np.float32),
        "mask": _causal_mask(),
        "ones": np.ones((128, 512), dtype=ml_dtypes.bfloat16),
    }


def kernel(**inputs):
    nc = build_core_program()
    in_maps = [make_core_inputs(c, inputs) for c in range(NCORES)]
    res = run_bass_kernel_spmd(nc, in_maps, core_ids=list(range(NCORES)))
    out = np.concatenate([r["out"].reshape(NSEQ, SEQ, D) for r in res.results], axis=0)
    return out.astype(np.float32)
